# version 01
# baseline (speedup 1.0000x reference)
import numpy as np
import ml_dtypes
import concourse.bass as bass
import concourse.mybir as mybir
from concourse.bass_utils import run_bass_kernel_spmd
from concourse.alu_op_type import AluOpType as OP

F32 = mybir.dt.float32
BF16 = mybir.dt.bfloat16
U32 = mybir.dt.uint32
AF = mybir.ActivationFunctionType
AX = mybir.AxisListType

BF16NP = ml_dtypes.bfloat16

B = 2
T = 16384
D = 1024
CHUNK = 4096
HALO = 2048
SLABT = HALO + CHUNK
NEG = -1e30


ENGINES = ("pe", "dve", "act", "pool", "sp")
QUEUES = ("qsp", "qpool", "qact")
ENG_OF_Q = {"qsp": "sp", "qpool": "pool", "qact": "act"}


class Prog:
    def __init__(self, nc, debug_serial=False):
        self.nc = nc
        self.ops = {e: [] for e in ENGINES}
        self.count = dict.fromkeys(ENGINES, 0)
        self.qcount = dict.fromkeys(QUEUES, 0)
        self.track = {}
        self.waited = {e: {} for e in ENGINES}
        self.debug_serial = debug_serial
        self.act_last_func = None

    def _resolve(self, engine, reads, writes):
        waits = {}

        def add(tok):
            if tok is None:
                return
            sem, v = tok
            if waits.get(sem, 0) < v:
                waits[sem] = v

        for k in reads:
            ent = self.track.get(k)
            if ent:
                add(ent[0])
        for k in writes:
            ent = self.track.get(k)
            if ent:
                add(ent[0])
                for t in ent[1]:
                    add(t)
        out = []
        w = self.waited[engine]
        for sem, v in sorted(waits.items()):
            if w.get(sem, 0) < v:
                out.append((sem, v))
                w[sem] = v
        return out

    def _commit(self, tok, reads, writes):
        for k in writes:
            self.track[k] = [tok, []]
        for k in reads:
            ent = self.track.get(k)
            if ent is None:
                self.track[k] = [None, [tok]]
            else:
                ent[1].append(tok)

    def op(self, engine, emit, reads=(), writes=(), act_func=None):
        waits = self._resolve(engine, reads, writes)
        if engine == "act":
            if (self.act_last_func is not None and act_func != self.act_last_func
                    and self.count["act"] > 0):
                v = self.count["act"]
                if self.waited["act"].get("act", 0) < v:
                    waits.append(("act", v))
                    self.waited["act"]["act"] = v
            self.act_last_func = act_func
        if self.debug_serial and self.count[engine] > 0:
            v = self.count[engine]
            if self.waited[engine].get(engine, 0) < v:
                waits.append((engine, v))
                self.waited[engine][engine] = v
        self.count[engine] += 1
        tok = (engine, self.count[engine])
        self.ops[engine].append((waits, emit, tok, 1))
        self._commit(tok, reads, writes)
        return tok

    def dma(self, queue, emit, reads=(), writes=()):
        eng = ENG_OF_Q[queue]
        waits = self._resolve(eng, reads, writes)
        self.qcount[queue] += 16
        tok = (queue, self.qcount[queue])
        self.ops[eng].append((waits, emit, tok, 16))
        self._commit(tok, reads, writes)
        return tok

    def barrier_all(self, engine="sp"):
        waits = []
        for e in ENGINES:
            if e != engine and self.count[e] > 0:
                waits.append((e, self.count[e]))
        for q in QUEUES:
            if self.qcount[q] > 0:
                waits.append((q, self.qcount[q]))
        w = self.waited[engine]
        waits = [(s, v) for (s, v) in waits if w.get(s, 0) < v]
        for s, v in waits:
            w[s] = v
        self.ops[engine].append((waits, None, None, 0))

    def fence(self):
        snap_e = dict(self.count)
        snap_q = dict(self.qcount)
        for e in ENGINES:
            waits = [(o, snap_e[o]) for o in ENGINES if o != e and snap_e[o] > 0]
            waits += [(q, snap_q[q]) for q in QUEUES if snap_q[q] > 0]
            w = self.waited[e]
            waits = [(s_, v) for (s_, v) in waits if w.get(s_, 0) < v]
            for s_, v in waits:
                w[s_] = v
            if waits:
                self.ops[e].append((waits, None, None, 0))

    def materialize(self, block, sems):
        def run(ename, eng):
            for waits, emit, tok, inc in self.ops[ename]:
                for sem, v in waits:
                    eng.wait_ge(sems[sem], v)
                if emit is None:
                    continue
                inst = emit(eng)
                assert inst is not None, f"op on {ename} returned no instruction"
                inst.then_inc(sems[tok[0]], inc)
            self.ops[ename] = []

        for ename, starter in (("pe", block.tensor), ("dve", block.vector),
                               ("act", block.scalar), ("pool", block.gpsimd),
                               ("sp", block.sync)):
            if self.ops[ename]:
                starter(lambda eng, ename=ename: run(ename, eng))



def mm_group(p, ps_ap, pairs, reads, write_key, accumulate=False):
    def emit(eng):
        inst = None
        n = len(pairs)
        for i, (lhsT, rhs) in enumerate(pairs):
            inst = eng.matmul(ps_ap, lhsT, rhs,
                              start=(i == 0 and not accumulate),
                              stop=(i == n - 1))
        return inst
    return p.op("pe", emit, reads=reads, writes=(write_key,))


def transpose_group(p, items, reads, write_key):
    def emit(eng):
        inst = None
        for dst, src, ident in items:
            inst = eng.transpose(dst, src, ident)
        return inst
    return p.op("pe", emit, reads=reads, writes=(write_key,))


def dve(p, name, *a, reads=(), writes=(), **kw):
    def emit(eng):
        return getattr(eng, name)(*a, **kw)
    return p.op("dve", emit, reads=reads, writes=writes)


def pool(p, name, *a, reads=(), writes=(), **kw):
    def emit(eng):
        return getattr(eng, name)(*a, **kw)
    return p.op("pool", emit, reads=reads, writes=writes)


def act(p, out, in_, func, reads=(), writes=(), **kw):
    def emit(eng):
        return eng.activation(out, in_, func, **kw)
    return p.op("act", emit, reads=reads, writes=writes, act_func=func)


def dma(p, queue, out, in_, reads=(), writes=(), **kw):
    def emit(eng):
        return eng.dma_start(out=out, in_=in_, **kw)
    return p.dma(queue, emit, reads=reads, writes=writes)


def gather(p, out, src, idx_ap, reads=(), writes=()):
    def emit(eng):
        return eng.indirect_dma_start(
            out=out, out_offset=None, in_=src,
            in_offset=bass.IndirectOffsetOnAxis(ap=idx_ap, axis=0))
    return p.dma("qpool", emit, reads=reads, writes=writes)


def scatter(p, dst, dst_idx_ap, src, reads=(), writes=()):
    def emit(eng):
        return eng.indirect_dma_start(
            out=dst, out_offset=bass.IndirectOffsetOnAxis(ap=dst_idx_ap, axis=0),
            in_=src, in_offset=None)
    return p.dma("qpool", emit, reads=reads, writes=writes)



def emit_ln(p, xc_buf, kxc, sq_buf, ksq, stat, kst, x_ap, key_in, outs):
    nmean = stat[:, 0:1]
    ssq = stat[:, 1:2]
    var = stat[:, 2:3]
    rstd = stat[:, 3:4]
    sq = stat[:, 4:5]
    xc = xc_buf[:, 0:D]
    dve(p, "tensor_reduce", nmean, x_ap, axis=AX.X, op=OP.add, negate=True,
        reads=(key_in,), writes=((kst, "nmean"),))
    dve(p, "tensor_scalar", nmean, nmean, 1.0 / D, None, op0=OP.mult,
        reads=((kst, "nmean"),), writes=((kst, "nmean"),))
    dve(p, "tensor_scalar", xc, x_ap, nmean, None, op0=OP.add,
        reads=(key_in, (kst, "nmean")), writes=(kxc,))
    dve(p, "scalar_tensor_tensor", sq_buf[:, 0:D], xc, 1.0, xc,
        op0=OP.mult, op1=OP.mult, accum_out=ssq,
        reads=(kxc,), writes=(ksq, (kst, "ssq")))
    dve(p, "tensor_scalar", var, ssq, 1.0 / D, 1e-5, op0=OP.mult, op1=OP.add,
        reads=((kst, "ssq"),), writes=((kst, "var"),))
    act(p, sq, var, AF.Sqrt, reads=((kst, "var"),), writes=((kst, "sqv"),))
    dve(p, "reciprocal", rstd, sq, reads=((kst, "sqv"),), writes=((kst, "rstd"),))
    for out_ap, out_key in outs:
        dve(p, "tensor_scalar", out_ap, xc, rstd, None, op0=OP.mult,
            reads=(kxc, (kst, "rstd")), writes=(out_key,))


class S:
    pass


P1_NSLAB = None


def build_phase1(nc, p, s, g):
    SLAB = 512
    NT = SLAB // 128
    NSLAB = T // SLAB if P1_NSLAB is None else P1_NSLAB

    for wname, wap in (("wf", g["Wf_t"]), ("wi", g["Wi_t"]), ("wh", g["Wh_t"])):
        dma(p, "qpool", getattr(s, wname)[:],
            wap.rearrange("(k p) e -> p k e", p=128),
            writes=(wname,))
    dma(p, "qsp", s.wr32[:], g["w_router"].rearrange("(k p) e -> p k e", p=128),
        writes=("wr32",))
    for bname, bap in (("bfb", g["b_f"]), ("bib", g["b_i"]), ("bhb", g["b_h"])):
        dma(p, "qsp", getattr(s, bname)[:], bap, writes=(bname,))
    pool(p, "memset", s.zrow_bf[:], 0.0, writes=("zrowbf",))
    dma(p, "qpool", g["xn_dram"][T:T + 1, :], s.zrow_bf[0:1, 0:D],
        reads=("zrowbf",), writes=(("xn_dram", "z"),))
    pool(p, "memset", s.state[:], 0.0, writes=tuple(("state", e) for e in range(8)))

    pre = [s.preF, s.preI, s.preH]
    wsb = [s.wf, s.wi, s.wh]
    wk = ["wf", "wi", "wh"]
    bias = [s.bfb, s.bib, s.bhb]
    biask = ["bfb", "bib", "bhb"]
    kF, kI, kH = "preF", "preI", "preH"
    k1, k2 = "S1", "S2"
    kx, kxn, kxnT = "xsl", "xnsl", "xnT"

    xb_r = g["xb"].rearrange("(n tt p) d -> n p tt d", p=128, tt=NT)
    xn_r = g["xn_dram"][0:T, :].rearrange("(n tt p) d -> n p tt d", p=128, tt=NT)
    pa_r = g["partial_dram"].rearrange("(n tt p) d -> n p tt d", p=128, tt=NT)

    for sl in range(NSLAB):
        dma(p, "qsp", s.xsl[:], xb_r[sl], writes=(kx,))
        for tt in range(NT):
            emit_ln(p, s.lnxc, "lnxc", s.xn32sl[:, tt], "xn32sl",
                    s.stats, "lnst",
                    s.xsl[:, tt], kx, [(s.xn32sl[:, tt], "xn32sl")])
        dma(p, "qpool", xn_r[sl], s.xn32sl[:], reads=("xn32sl",),
            writes=(("xn_dram", sl),))
        for k in range(8):
            items = [(s.pt_f[:, tt * 128:(tt + 1) * 128],
                      s.xn32sl[:, tt, k * 128:(k + 1) * 128], s.ident_f32[:])
                     for tt in range(NT)]
            transpose_group(p, items, reads=("xn32sl", "identf32"),
                            write_key="ptf")
            dve(p, "tensor_copy", s.xnT[:, k], s.pt_f[:, 0:512],
                reads=("ptf",), writes=((kxnT, k),))
            dve(p, "tensor_copy", s.xnT32k[:], s.pt_f[:, 0:512],
                reads=("ptf",), writes=("xnT32k",))
            def emit_rmm(eng, k=k):
                inst = None
                for tt in range(NT):
                    inst = eng.matmul(s.ps_mm[tt][:, 0:8],
                                      s.xnT32k[:, tt * 128:(tt + 1) * 128],
                                      s.wr32[:, k],
                                      start=(k == 0), stop=(k == 7))
                return inst
            p.op("pe", emit_rmm, reads=("xnT32k", "wr32"),
                 writes=tuple(("psmm", tt) for tt in range(NT)))
        rxnT = tuple((kxnT, k) for k in range(8))
        for tt in range(NT):
            dve(p, "tensor_copy",
                s.logits[:, sl * NT + tt, :], s.ps_mm[tt][:, 0:8],
                reads=(("psmm", tt),),
                writes=(("logits", sl * NT + tt),))
        for w in range(3):
            for e in range(8):
                bk = (w * 8 + e) % 4
                bank = s.ps_mm[bk]
                mm_group(p, bank[:],
                         [(wsb[w][:, k, e * 128:(e + 1) * 128], s.xnT[:, k])
                          for k in range(8)],
                         reads=rxnT + (wk[w],), write_key=("psmm", bk))
                dve(p, "tensor_scalar", pre[w][:, e], bank[:],
                    bias[w][:, e:e + 1], None, op0=OP.add,
                    reads=(("psmm", bk), biask[w]),
                    writes=((wk[w] + "pre", e),))
        rF = tuple(("wfpre", e) for e in range(8))
        rI = tuple(("wipre", e) for e in range(8))
        rH = tuple(("whpre", e) for e in range(8))
        if sl == 0 and g.get("dbg_pre") is not None:
            dma(p, "qsp", g["dbg_pre"][0], s.preF[:], reads=rF, writes=(("dbg", 0),))
            dma(p, "qsp", g["dbg_pre"][1], s.preI[:], reads=rI, writes=(("dbg", 1),))
            dma(p, "qsp", g["dbg_pre"][2], s.preH[:], reads=rH, writes=(("dbg", 2),))
        F, I, H = s.preF[:], s.preI[:], s.preH[:]
        S1, S2 = s.sc1[:], s.sc2[:]
        act(p, S1, F, AF.Abs, reads=rF, writes=(k1,))
        act(p, S1, S1, AF.Exp, scale=-1.0, reads=(k1,), writes=(k1,))
        act(p, S1, S1, AF.Ln, bias=1.0, reads=(k1,), writes=(k1,))
        dve(p, "tensor_scalar", F, F, 0.0, None, op0=OP.min,
            reads=rF, writes=rF + (kF,))
        dve(p, "tensor_tensor", S1, S1, F, op=OP.subtract,
            reads=(k1, kF), writes=(k1,))
        act(p, S2, I, AF.Abs, reads=rI, writes=(k2,))
        act(p, S2, S2, AF.Exp, scale=-1.0, reads=(k2,), writes=(k2,))
        act(p, S2, S2, AF.Ln, bias=1.0, reads=(k2,), writes=(k2,))
        dve(p, "tensor_scalar", I, I, 0.0, None, op0=OP.min,
            reads=rI, writes=rI + (kI,))
        dve(p, "tensor_tensor", S2, S2, I, op=OP.subtract,
            reads=(k2, kI), writes=(k2,))
        dve(p, "tensor_tensor", S1, S1, S2, op=OP.subtract,
            reads=(k1, k2), writes=(k1,))
        act(p, F, S1, AF.Sigmoid, scale=-1.0, reads=(k1,), writes=rF + (kF,))
        act(p, S2, S1, AF.Sigmoid, reads=(k1,), writes=(k2,))
        dve(p, "tensor_scalar", S1, H, 0.5, None, op0=OP.add,
            reads=rH, writes=(k1,))
        act(p, H, H, AF.Sigmoid, reads=rH, writes=rH + (kH,))
        dve(p, "tensor_tensor", H, H, S1, op=OP.max,
            reads=(kH, k1), writes=(kH,))
        dve(p, "tensor_tensor", S2, S2, H, op=OP.mult,
            reads=(k2, kH), writes=(k2,))
        for e in range(8):
            dve(p, "tensor_tensor_scan", s.sc1[:, e], s.preF[:, e], s.sc2[:, e],
                s.state[:, e:e + 1], op0=OP.mult, op1=OP.add,
                reads=(kF, k2, ("state", e)), writes=(k1,))
            dve(p, "tensor_copy", s.state[:, e:e + 1], s.sc1[:, e, SLAB - 1:SLAB],
                reads=(k1,), writes=(("state", e),))
        for tt in range(NT):
            for half in range(2):
                items = [(s.pt_f[:, q * 128:(q + 1) * 128],
                          s.sc1[:, half * 4 + q, tt * 128:(tt + 1) * 128],
                          s.ident_f32[:]) for q in range(4)]
                transpose_group(p, items, reads=(k1, "identf32"),
                                write_key="ptf")
                dve(p, "tensor_tensor", s.xsl[:, tt, half * 512:(half + 1) * 512],
                    s.xsl[:, tt, half * 512:(half + 1) * 512], s.pt_f[:],
                    op=OP.add, reads=("ptf", kx), writes=(kx,))
        dma(p, "qsp", pa_r[sl], s.xsl[:], reads=(kx,),
            writes=(("partial", sl),))

    if g.get("xn_out") is not None:
        dma(p, "qsp", g["xn_out"], g["xn_dram"][0:T, :],
            reads=tuple(("xn_dram", sl) for sl in range(NSLAB)),
            writes=(("xnout",),))


def build_x1_copy(nc, p, s, g):
    NT = 4
    NSLAB = T // 512 if P1_NSLAB is None else P1_NSLAB
    pa_r = g["partial_dram"].rearrange("(n tt p) d -> n p tt d", p=128, tt=NT)
    x1_r = g["x1_dram"].rearrange("(n tt p) d -> n p tt d", p=128, tt=NT)
    for sl in range(NSLAB):
        dma(p, "qsp", s.xsl[:], pa_r[sl],
            reads=(("partial", sl),), writes=("xsl",))
        dma(p, "qsp", x1_r[sl], s.xsl[:], reads=("xsl",),
            writes=(("x1", sl),))


NEXP = 8
CAP = 512
NGRP = 4
GT = 32
INVN = 4352
XND_Z = T


def bcast_m(ap2d, n):
    return ap2d.unsqueeze(1).broadcast_to([ap2d.shape[0], n, ap2d.shape[1]])


def bcast_l(ap2d, k):
    return ap2d.unsqueeze(2).broadcast_to([ap2d.shape[0], ap2d.shape[1], k])


def build_moe_routing(nc, p, s, g, gi):
    lt = s.logits[:, GT * gi:GT * (gi + 1), :]
    prb = s.prb[:]
    oh1, oh2 = s.ohb1[:], s.ohb2[:]
    cntb = s.cntb[:]
    r = lambda i: s.rstat[:, :, i]
    rb = lambda i: bcast_l(s.rstat[:, :, i], NEXP)
    kl = tuple(("logits", t) for t in range(GT * gi, GT * (gi + 1)))
    K = lambda *names: tuple(names)

    dve(p, "tensor_reduce", r(0), lt, axis=AX.X, op=OP.max, negate=True,
        reads=kl, writes=("rs0",))
    dve(p, "tensor_tensor", prb, lt, rb(0), op=OP.add,
        reads=kl + ("rs0",), writes=("prb",))
    act(p, prb, prb, AF.Exp, reads=("prb",), writes=("prb",))
    dve(p, "tensor_reduce", r(1), prb, axis=AX.X, op=OP.add,
        reads=("prb",), writes=("rs1",))
    dve(p, "reciprocal", r(1), r(1), reads=("rs1",), writes=("rs1",))
    dve(p, "tensor_tensor", prb, prb, rb(1), op=OP.mult,
        reads=("prb", "rs1"), writes=("prb",))
    dve(p, "tensor_reduce", r(2), prb, axis=AX.X, op=OP.max,
        reads=("prb",), writes=("rs2",))
    dve(p, "tensor_tensor", oh1, prb, rb(2), op=OP.is_equal,
        reads=("prb", "rs2"), writes=("oh1",))
    dve(p, "tensor_tensor", oh1, oh1, bcast_m(s.iota8m8[:], GT), op=OP.mult,
        reads=("oh1", "iota8m8"), writes=("oh1",))
    dve(p, "tensor_scalar", oh1, oh1, 8.0, None, op0=OP.add,
        reads=("oh1",), writes=("oh1",))
    dve(p, "tensor_reduce", r(3), oh1, axis=AX.X, op=OP.min,
        reads=("oh1",), writes=("rs3",))
    dve(p, "tensor_tensor", oh1, bcast_m(s.iota8[:], GT), rb(3), op=OP.is_equal,
        reads=("iota8", "rs3"), writes=("oh1",))
    dve(p, "tensor_copy", s.pbuf[:, gi, :, 0], r(2),
        reads=("rs2",), writes=(("pbuf", gi),))
    dve(p, "scalar_tensor_tensor", prb, oh1, NEG, prb, op0=OP.mult, op1=OP.add,
        reads=("oh1", "prb"), writes=("prb",))
    dve(p, "tensor_reduce", r(4), prb, axis=AX.X, op=OP.max,
        reads=("prb",), writes=("rs4",))
    dve(p, "tensor_tensor", oh2, prb, rb(4), op=OP.is_equal,
        reads=("prb", "rs4"), writes=("oh2",))
    dve(p, "tensor_tensor", oh2, oh2, bcast_m(s.iota8m8[:], GT), op=OP.mult,
        reads=("oh2", "iota8m8"), writes=("oh2",))
    dve(p, "tensor_scalar", oh2, oh2, 8.0, None, op0=OP.add,
        reads=("oh2",), writes=("oh2",))
    dve(p, "tensor_reduce", r(5), oh2, axis=AX.X, op=OP.min,
        reads=("oh2",), writes=("rs5",))
    dve(p, "tensor_tensor", oh2, bcast_m(s.iota8[:], GT), rb(5), op=OP.is_equal,
        reads=("iota8", "rs5"), writes=("oh2",))
    dve(p, "tensor_copy", s.pbuf[:, gi, :, 1], r(4),
        reads=("rs4",), writes=(("pbuf2", gi),))

    for cpass, (ohap, kOH) in enumerate(((oh1, "oh1"), (oh2, "oh2"))):
        pool(p, "memset", s.totrow[:], 0.0, writes=("totrow",))
        for i in range(GT):
            mm_group(p, s.ps_r[:, 0:8],
                     [(s.tri[:], ohap[:, i, :]), (s.onescol[:], s.totrow[:])],
                     reads=(kOH, "tri", "onescol", "totrow"), write_key="psr")
            dve(p, "tensor_copy", cntb[:, i, :], s.ps_r[:, 0:8],
                reads=("psr",), writes=(("cntb", i),))
            mm_group(p, s.pt_f[0:1, 0:8], [(s.ones128[:], ohap[:, i, :])],
                     reads=(kOH, "ones128"), write_key="ptf",
                     accumulate=(i > 0))
            dve(p, "tensor_copy", s.totrow[:], s.pt_f[0:1, 0:8],
                reads=("ptf",), writes=("totrow",))
        if cpass == 0:
            mm_group(p, s.ps_r[:, 0:8], [(s.onescol[:], s.totrow[:])],
                     reads=("totrow", "onescol"), write_key="psr")
            dve(p, "tensor_copy", s.tot1bc[:], s.ps_r[:, 0:8],
                reads=("psr",), writes=("tot1bc",))
            dve(p, "tensor_tensor", s.prb[:], oh1, cntb, op=OP.mult,
                reads=("oh1",) + tuple(("cntb", i) for i in range(GT)),
                writes=("prb",))
            dve(p, "tensor_reduce", s.prib[:, :, 0], s.prb[:], axis=AX.X,
                op=OP.add, reads=("prb",), writes=("pri1",))
        else:
            dve(p, "tensor_tensor", s.prb[:], oh2, cntb, op=OP.mult,
                reads=("oh2",) + tuple(("cntb", i) for i in range(GT)),
                writes=("prb",))
            dve(p, "tensor_reduce", s.prib[:, :, 1], s.prb[:], axis=AX.X,
                op=OP.add, reads=("prb",), writes=("pri2",))
            dve(p, "tensor_tensor", s.prb[:], oh2, bcast_m(s.tot1bc[:], GT),
                op=OP.mult, reads=("oh2", "tot1bc"), writes=("prb",))
            dve(p, "tensor_reduce", s.rstat[:, :, 6], s.prb[:], axis=AX.X,
                op=OP.add, reads=("prb",), writes=("rs6",))
            dve(p, "tensor_tensor", s.prib[:, :, 1], s.prib[:, :, 1],
                s.rstat[:, :, 6], op=OP.add,
                reads=("pri2", "rs6"), writes=("pri2",))

    pool(p, "memset", s.invinit[:], XND_Z, writes=("invinit",))
    dma(p, "qsp",
        g["inv_dram"][INVN * gi:INVN * (gi + 1), :].rearrange("(q a) c -> q a c", a=34),
        s.invinit[:].unsqueeze(2), reads=("invinit",), writes=(("inv", gi),))

    for c, kpri, kidx in ((0, "pri1", "rs3x"), (1, "pri2", "rs5x")):
        idxs = s.rstat[:, :, 3 + 2 * c]
        kidxr = "rs3" if c == 0 else "rs5"
        pri = s.prib[:, :, c]
        t1, t2, t3 = s.rt1[:], s.rt2[:], s.rt3[:]
        dve(p, "tensor_scalar", t1, pri, 512.0, None, op0=OP.is_ge,
            reads=(kpri,), writes=("rt1",))
        dve(p, "tensor_scalar", t2, idxs, 512.0, None, op0=OP.mult,
            reads=(kidxr,), writes=("rt2",))
        dve(p, "tensor_tensor", t2, t2, pri, op=OP.add,
            reads=("rt2", kpri), writes=("rt2",))
        dve(p, "tensor_scalar", t3, t2, -1.0, 4096.0, op0=OP.mult, op1=OP.add,
            reads=("rt2",), writes=("rt3",))
        dve(p, "tensor_tensor", t3, t3, t1, op=OP.mult,
            reads=("rt3", "rt1"), writes=("rt3",))
        dve(p, "tensor_tensor", t3, t3, t2, op=OP.add,
            reads=("rt3", "rt2"), writes=("rt3",))
        dve(p, "tensor_copy", s.slotc[:, gi, :, c], t3,
            reads=("rt3",), writes=(("slotc", gi, c),))
        dve(p, "tensor_scalar", t3,
            s.pidx[:, 0:1].broadcast_to([128, GT]), 1.0, 4200.0,
            op0=OP.mult, op1=OP.add, reads=("pidx",), writes=("rt3",))
        dve(p, "tensor_tensor", t3, t3, t2, op=OP.subtract,
            reads=("rt3", "rt2"), writes=("rt3",))
        dve(p, "tensor_tensor", t3, t3, t1, op=OP.mult,
            reads=("rt3", "rt1"), writes=("rt3",))
        dve(p, "tensor_tensor", t3, t3, t2, op=OP.add,
            reads=("rt3", "rt2"), writes=("rt3",))
        dve(p, "tensor_scalar", t3, t3, float(INVN * gi), None, op0=OP.add,
            reads=("rt3",), writes=("rt3",))
        dve(p, "tensor_copy", s.sslotu[:, :, c], t3,
            reads=("rt3",), writes=(("sslotu", c),))
    for i in range(GT):
        for c in range(2):
            scatter(p, g["inv_dram"], s.sslotu[:, i, c:c + 1],
                    s.tokidx[:, GT * gi + i:GT * gi + i + 1],
                    reads=(("sslotu", c), "tokidx", ("inv", gi)),
                    writes=(("inv", gi),))


def build_moe_experts(nc, p, s, g):
    wi_r = [g["moe_wi"][e].rearrange("(k p) f -> p k f", p=128) for e in range(NEXP)]
    wo_r = [g["moe_wo"][e].rearrange("(k p) h -> p k h", p=128) for e in range(NEXP)]
    for e in range(NEXP):
        dma(p, "qpool", s.wo_sb[:], wo_r[e], writes=("wo_sb",))
        for gi in range(NGRP):
            for q in range(4):
                a = INVN * gi + e * CAP + q * 128
                dma(p, "qsp", s.idxu[:], g["inv_dram"][a:a + 128, :],
                    reads=(("inv", gi),), writes=("idxu",))
                gather(p, s.ein[:], g["xn_dram"], s.idxu[:, 0:1],
                       reads=("idxu",) + tuple(("xn_dram", sl) for sl in range(32))
                       + (("xn_dram", "z"),),
                       writes=("ein",))
                items = [(s.pt_b[:, k * 128:(k + 1) * 128],
                          s.ein[:, k * 128:(k + 1) * 128], s.ident_bf[:])
                         for k in range(8)]
                transpose_group(p, items, reads=("ein", "identbf"),
                                write_key="ptb")
                dve(p, "tensor_copy", s.einT[:, :, q * 128:(q + 1) * 128],
                    s.pt_b[:].rearrange("p (k x) -> p k x", k=8),
                    reads=("ptb",), writes=(("einT", q),))
            reinT = tuple(("einT", q) for q in range(4))
            for fk in range(8):
                wb = s.wi_ch[fk % 2]
                kwb = ("wi_ch", fk % 2)
                dma(p, "qpool", wb[:], wi_r[e][:, :, fk * 512:(fk + 1) * 512],
                    writes=(kwb,))
                for m in range(4):
                    bk = m % 4
                    mm_group(p, s.ps_mm[bk][:],
                             [(wb[:, k, m * 128:(m + 1) * 128], s.einT[:, k])
                              for k in range(8)],
                             reads=reinT + (kwb,), write_key=("psmm", bk))
                    act(p, s.h1T[:, fk * 4 + m], s.ps_mm[bk][:],
                        AF.Gelu_apprx_tanh,
                        reads=(("psmm", bk),), writes=(("h1T", fk * 4 + m),))
            rh1 = tuple(("h1T", i) for i in range(32))
            for n in range(2):
                for m in range(4):
                    bk = m
                    mm_group(p, s.ps_mm[bk][:],
                             [(s.h1T[:, k, m * 128:(m + 1) * 128],
                               s.wo_sb[:, k, n * 512:(n + 1) * 512])
                              for k in range(32)],
                             reads=rh1 + ("wo_sb",), write_key=("psmm", bk))
                    dve(p, "tensor_copy", s.stg[:], s.ps_mm[bk][:],
                        reads=(("psmm", bk),), writes=("stg",))
                    dma(p, "qsp",
                        g["out2_dram"][gi][e * CAP + m * 128:e * CAP + (m + 1) * 128,
                                           n * 512:(n + 1) * 512],
                        s.stg[:], reads=("stg",), writes=(("out2", gi, e, n, m),))


def build_moe_combine(nc, p, s, g):
    pa_t = g["partial_dram"].rearrange("(t p) d -> t p d", p=128)
    x1_t = g["x1_dram"].rearrange("(t p) d -> t p d", p=128)
    for gi in range(NGRP):
        pool(p, "memset", s.zrow_f[:], 0.0, writes=("zrowf",))
        dma(p, "qpool", g["out2_dram"][gi][NEXP * CAP:NEXP * CAP + 1, :],
            s.zrow_f[0:1, :], reads=("zrowf",), writes=(("out2z", gi),))
        rout2 = tuple(("out2", gi, e, n, m)
                      for e in range(NEXP) for n in range(2) for m in range(4))
        for i in range(GT):
            gt = GT * gi + i
            dma(p, "qsp", s.part[:], pa_t[gt], reads=(("partial", gt // 4),),
                writes=("part",))
            gather(p, s.g1t[:], g["out2_dram"][gi], s.slotc[:, gi, i, 0:1],
                   reads=(("slotc", gi, 0), ("out2z", gi)) + rout2,
                   writes=("g1t",))
            gather(p, s.g2t[:], g["out2_dram"][gi], s.slotc[:, gi, i, 1:2],
                   reads=(("slotc", gi, 1), ("out2z", gi)) + rout2,
                   writes=("g2t",))
            dve(p, "tensor_scalar", s.g1t[:], s.g1t[:], s.pbuf[:, gi, i, 0:1],
                None, op0=OP.mult,
                reads=("g1t", ("pbuf", gi)), writes=("g1t",))
            dve(p, "scalar_tensor_tensor", s.g1t[:], s.g2t[:],
                s.pbuf[:, gi, i, 1:2], s.g1t[:], op0=OP.mult, op1=OP.add,
                reads=("g2t", ("pbuf2", gi), "g1t"), writes=("g1t",))
            dve(p, "tensor_tensor", s.part[:], s.part[:], s.g1t[:], op=OP.add,
                reads=("part", "g1t"), writes=("part",))
            dma(p, "qsp", x1_t[gt], s.part[:], reads=("part",),
                writes=(("x1", gt),))


def build_moe(nc, p, s, g):
    dma(p, "qsp", s.tri[:], g["tri_f32"], writes=("tri",))
    dma(p, "qsp", s.iota8[:], g["iota8"], writes=("iota8",))
    dma(p, "qsp", s.iota8m8[:], g["iota8m8"], writes=("iota8m8",))
    dma(p, "qsp", s.onescol[:], g["onescol"], writes=("onescol",))
    dma(p, "qsp", s.ones128[:], g["ones128"], writes=("ones128",))
    dma(p, "qsp", s.pidx[:], g["pidx"], writes=("pidx",))
    dma(p, "qsp", s.tokidx[:], g["tokidx"], writes=("tokidx",))
    for gi in range(NGRP):
        build_moe_routing(nc, p, s, g, gi)
    build_moe_experts(nc, p, s, g)
    build_moe_combine(nc, p, s, g)



def _mk(nc, stack, prefix):
    def sb(name, shape, dt):
        return stack.enter_context(nc.sbuf_tensor(prefix + name, shape, dt))
    def psb(name, shape, dt):
        return stack.enter_context(nc.psum_tensor(prefix + name, shape, dt))
    return sb, psb


def alloc_persist(nc, stack, s):
    sb, psb = _mk(nc, stack, "sp_")
    s.ident_bf = sb("ident_bf", [128, 128], BF16)
    s.ident_f32 = sb("ident_f32", [128, 128], F32)
    s.logits = sb("logits", [128, 128, 8], F32)
    s.onescol32 = sb("onescol32", [1, 128], F32)
    s.ps_mm = [psb(f"ps_mm{i}", [128, 512], F32) for i in range(4)]
    s.ps_r = psb("ps_r", [128, 512], F32)
    s.pt_b = psb("pt_b", [128, 1024], BF16)
    s.pt_f = psb("pt_f", [128, 512], F32)
    return s


def alloc_phase1(nc, stack, s):
    sb, _ = _mk(nc, stack, "s1_")
    s.zrow_bf = sb("zrow_bf", [128, D], BF16)
    s.wf = sb("wf", [128, 8, D], BF16)
    s.wi = sb("wi", [128, 8, D], BF16)
    s.wh = sb("wh", [128, 8, D], BF16)
    s.wr32 = sb("wr32", [128, 8, 8], F32)
    s.bfb = sb("bfb", [128, 8], F32)
    s.bib = sb("bib", [128, 8], F32)
    s.bhb = sb("bhb", [128, 8], F32)
    s.xsl = sb("xsl", [128, 4, D], F32)
    s.xn32sl = sb("xn32sl", [128, 4, D], F32)
    s.lnxc = sb("lnxc", [128, D], F32)
    s.xnT32k = sb("xnT32k", [128, 512], F32)
    s.stats = sb("stats", [128, 8], F32)
    s.xnT = sb("xnT", [128, 8, 512], BF16)
    s.preF = sb("preF", [128, 8, 512], F32)
    s.preI = sb("preI", [128, 8, 512], F32)
    s.preH = sb("preH", [128, 8, 512], F32)
    s.sc1 = sb("sc1", [128, 8, 512], F32)
    s.sc2 = sb("sc2", [128, 8, 512], F32)
    s.state = sb("state", [128, 8], F32)
    return s


def alloc_moe(nc, stack, s):
    sb, _ = _mk(nc, stack, "sm_")
    s.tri = sb("tri", [128, 128], F32)
    s.iota8 = sb("iota8", [128, 8], F32)
    s.iota8m8 = sb("iota8m8", [128, 8], F32)
    s.onescol = sb("onescol", [1, 128], F32)
    s.ones128 = sb("ones128", [128, 1], F32)
    s.pidx = sb("pidx", [128, 1], F32)
    s.tokidx = sb("tokidx", [128, 128], U32)
    s.prb = sb("prb", [128, GT, 8], F32)
    s.ohb1 = sb("ohb1", [128, GT, 8], F32)
    s.ohb2 = sb("ohb2", [128, GT, 8], F32)
    s.cntb = sb("cntb", [128, GT, 8], F32)
    s.rstat = sb("rstat", [128, GT, 8], F32)
    s.prib = sb("prib", [128, GT, 2], F32)
    s.rt1 = sb("rt1", [128, GT], F32)
    s.rt2 = sb("rt2", [128, GT], F32)
    s.rt3 = sb("rt3", [128, GT], F32)
    s.totrow = sb("totrow", [1, 8], F32)
    s.tot1bc = sb("tot1bc", [128, 8], F32)
    s.invinit = sb("invinit", [128, 34], U32)
    s.sslotu = sb("sslotu", [128, GT, 2], U32)
    s.slotc = sb("slotc", [128, NGRP, GT, 2], U32)
    s.pbuf = sb("pbuf", [128, NGRP, GT, 2], F32)
    s.wo_sb = sb("wo_sb", [128, 32, 1024], BF16)
    s.wi_ch = [sb(f"wi_ch{i}", [128, 8, 512], BF16) for i in range(2)]
    s.ein = sb("ein", [128, 1024], BF16)
    s.einT = sb("einT", [128, 8, 512], BF16)
    s.h1T = sb("h1T", [128, 32, 512], BF16)
    s.idxu = sb("idxu", [128, 1], U32)
    s.stg = sb("stg", [128, 512], F32)
    s.g1t = sb("g1t", [128, 1024], F32)
    s.g2t = sb("g2t", [128, 1024], F32)
    s.part = sb("part", [128, 1024], F32)
    s.zrow_f = sb("zrow_f", [128, 1024], F32)
    return s



def _np(x):
    return np.asarray(x)


def declare_p1_dram(nc, g, external_x1=True):
    g["xb"] = nc.dram_tensor("xb", [T, D], F32, kind="ExternalInput").ap()
    g["Wf_t"] = nc.dram_tensor("Wf_t", [D, D], F32, kind="ExternalInput").ap()
    g["Wi_t"] = nc.dram_tensor("Wi_t", [D, D], F32, kind="ExternalInput").ap()
    g["Wh_t"] = nc.dram_tensor("Wh_t", [D, D], F32, kind="ExternalInput").ap()
    g["b_f"] = nc.dram_tensor("b_f", [128, 8], F32, kind="ExternalInput").ap()
    g["b_i"] = nc.dram_tensor("b_i", [128, 8], F32, kind="ExternalInput").ap()
    g["b_h"] = nc.dram_tensor("b_h", [128, 8], F32, kind="ExternalInput").ap()
    g["w_router"] = nc.dram_tensor("w_router", [D, 8], F32, kind="ExternalInput").ap()
    g["ident_bf"] = nc.dram_tensor("ident_bf", [128, 128], BF16, kind="ExternalInput").ap()
    g["ident_f32"] = nc.dram_tensor("ident_f32", [128, 128], F32, kind="ExternalInput").ap()
    g["moe_wi"] = nc.dram_tensor("moe_wi", [NEXP, D, 4096], F32, kind="ExternalInput").ap()
    g["moe_wo"] = nc.dram_tensor("moe_wo", [NEXP, 4096, D], F32, kind="ExternalInput").ap()
    g["tri_f32"] = nc.dram_tensor("tri_f32", [128, 128], F32, kind="ExternalInput").ap()
    g["iota8"] = nc.dram_tensor("iota8", [128, 8], F32, kind="ExternalInput").ap()
    g["iota8m8"] = nc.dram_tensor("iota8m8", [128, 8], F32, kind="ExternalInput").ap()
    g["onescol"] = nc.dram_tensor("onescol", [1, 128], F32, kind="ExternalInput").ap()
    g["ones128"] = nc.dram_tensor("ones128", [128, 1], F32, kind="ExternalInput").ap()
    g["pidx"] = nc.dram_tensor("pidx", [128, 1], F32, kind="ExternalInput").ap()
    g["tokidx"] = nc.dram_tensor("tokidx", [128, 128], U32, kind="ExternalInput").ap()
    g["xn_dram"] = nc.dram_tensor("xn_dram", [T + 128, D], BF16).ap()
    g["partial_dram"] = nc.dram_tensor("partial_dram", [T, D], F32).ap()
    g["inv_dram"] = nc.dram_tensor("inv_dram", [NGRP * INVN, 1], U32).ap()
    g["out2_dram"] = [nc.dram_tensor(f"out2_g{gi}", [NEXP * CAP + 1, D], F32).ap()
                      for gi in range(NGRP)]
    if external_x1:
        g["x1_dram"] = nc.dram_tensor("x1_out", [T, D], F32, kind="ExternalOutput").ap()
    else:
        g["x1_dram"] = nc.dram_tensor("x1_dram", [T, D], F32).ap()


def build_p1_program(with_moe=False, debug=False):
    import contextlib
    nc = bass.Bass("TRN2", target_bir_lowering=False)
    g = {}
    declare_p1_dram(nc, g, external_x1=True)
    if debug:
        g["dbg_pre"] = nc.dram_tensor("dbg_pre", [3, 128, 8, 512], F32, kind="ExternalOutput").ap()
        g["xn_out"] = nc.dram_tensor("xn_out", [T, D], BF16, kind="ExternalOutput").ap()

    with contextlib.ExitStack() as stack:
        s = S()
        alloc_persist(nc, stack, s)
        sems = {}
        for name in ENGINES + QUEUES:
            sems[name] = stack.enter_context(nc.semaphore(name))
        p = Prog(nc)
        with nc.Block() as block:
            dma(p, "qsp", s.ident_bf[:], g["ident_bf"], writes=("identbf",))
            dma(p, "qsp", s.ident_f32[:], g["ident_f32"], writes=("identf32",))
            with contextlib.ExitStack() as st1:
                alloc_phase1(nc, st1, s)
                build_phase1(nc, p, s, g)
                p.fence()
                p.materialize(block, sems)
            if with_moe:
                with contextlib.ExitStack() as st2:
                    alloc_moe(nc, st2, s)
                    build_moe(nc, p, s, g)
                    p.fence()
                    p.materialize(block, sems)
            else:
                with contextlib.ExitStack() as st2:
                    alloc_phase1(nc, st2, s)
                    build_x1_copy(nc, p, s, g)
                    p.materialize(block, sems)
            p.barrier_all()
            p.materialize(block, sems)
    return nc


def prep_p1_inputs(x_batch, inp):
    d = {}
    d["xb"] = np.ascontiguousarray(x_batch)
    d["Wf_t"] = np.ascontiguousarray(_np(inp["L0_attn_W_f"]).T)
    d["Wi_t"] = np.ascontiguousarray(_np(inp["L0_attn_W_i"]).T)
    d["Wh_t"] = np.ascontiguousarray(_np(inp["L0_attn_W_h"]).T)
    d["b_f"] = np.ascontiguousarray(_np(inp["L0_attn_b_f"]).reshape(8, 128).T)
    d["b_i"] = np.ascontiguousarray(_np(inp["L0_attn_b_i"]).reshape(8, 128).T)
    d["b_h"] = np.ascontiguousarray(_np(inp["L0_attn_b_h"]).reshape(8, 128).T)
    d["w_router"] = np.ascontiguousarray(_np(inp["L0_ffn_w_router"]))
    d["ident_bf"] = np.eye(128, dtype=BF16NP)
    d["ident_f32"] = np.eye(128, dtype=np.float32)
    d["moe_wi"] = np.ascontiguousarray(_np(inp["L0_ffn_wi"]))
    d["moe_wo"] = np.ascontiguousarray(_np(inp["L0_ffn_wo"]))
    d["tri_f32"] = np.triu(np.ones((128, 128), np.float32), 1)
    io8 = np.broadcast_to(np.arange(8, dtype=np.float32), (128, 8)).copy()
    d["iota8"] = io8
    d["iota8m8"] = io8 - 8.0
    d["onescol"] = np.ones((1, 128), np.float32)
    d["ones128"] = np.ones((128, 1), np.float32)
    d["pidx"] = np.arange(128, dtype=np.float32)[:, None].copy()
    tk = (np.arange(128, dtype=np.uint32)[None, :] * 128
          + np.arange(128, dtype=np.uint32)[:, None])
    d["tokidx"] = np.ascontiguousarray(tk)
    return d


_PROG_CACHE = {}


def kernel(**inputs):
    x = np.ascontiguousarray(np.asarray(inputs["L0_attn_x"], dtype=np.float32))
    if "p1" not in _PROG_CACHE:
        _PROG_CACHE["p1"] = build_p1_program(with_moe=True)
        _PROG_CACHE["p2"] = build_p2_program()
        _PROG_CACHE["p3"] = build_p3_program()
    d1 = [prep_p1_inputs(x[b], inputs) for b in range(B)]
    res1 = run_bass_kernel_spmd(_PROG_CACHE["p1"], [d1[c // 4] for c in range(8)],
                                core_ids=list(range(8)))
    x1 = [np.asarray(res1.results[4 * b]["x1_out"]) for b in range(B)]
    d2 = [prep_p2_inputs(x1[c // 4], inputs, c % 4) for c in range(8)]
    res2 = run_bass_kernel_spmd(_PROG_CACHE["p2"], d2, core_ids=list(range(8)))
    xn1 = [np.asarray(res2.results[4 * b]["xn1_out"]) for b in range(B)]
    attnbr = [np.asarray(res2.results[c]["attnbr_out"]) for c in range(8)]
    d3 = []
    for c in range(8):
        b, j = c // 4, c % 4
        d = prep_p3_inputs(inputs, j)
        d["xn1_in"] = np.concatenate([xn1[b], np.zeros((128, D), np.float32)], 0)
        d["x1_in"] = x1[b]
        d["attnbr_in"] = attnbr[c]
        d3.append(d)
    res3 = run_bass_kernel_spmd(_PROG_CACHE["p3"], d3, core_ids=list(range(8)))
    out = np.empty((B, T, D), np.float32)
    for c in range(8):
        b, j = c // 4, c % 4
        out[b, CHUNK * j:CHUNK * (j + 1)] = res3.results[c]["out"]
    return out



GAMMA = 0.99
BAND = 4096
VSL = 256


def alloc_phase2a(nc, stack, s):
    sb, _ = _mk(nc, stack, "s2a_")
    s.wq = sb("wq", [128, 8, D], BF16)
    s.wk = sb("wk", [128, 8, D], BF16)
    s.wv = sb("wv", [128, 8, VSL], BF16)
    s.x1sl = sb("x1sl", [128, 4, D], F32)
    s.xn1sl = sb("xn1sl", [128, 4, D], F32)
    s.lnxc2 = sb("lnxc2", [128, D], F32)
    s.stats2 = sb("stats2", [128, 8], F32)
    s.xnT2 = sb("xnT2", [128, 8, 512], BF16)
    s.qstg = sb("qstg", [128, 512], BF16)
    s.kstg = sb("kstg", [128, 512], BF16)
    s.zrow2 = sb("zrow2", [128, D], F32)


def alloc_phase2b(nc, stack, s):
    sb, _ = _mk(nc, stack, "s2b_")
    s.qt = sb("qt", [128, 8, 512], BF16)
    s.kt = [sb(f"kt{i}", [128, 8, 128], BF16) for i in range(2)]
    s.abuf = sb("abuf", [128, 36, 512], BF16)
    s.e0 = sb("e0", [128, 512], F32)
    s.em = sb("em", [128, 4, 512], F32)
    s.ostg = sb("ostg", [128, 512], F32)


def alloc_vbuf(nc, stack, s):
    sb, _ = _mk(nc, stack, "s2v_")
    s.vbuf = sb("vbuf", [128, 128, VSL], BF16)


def build_phase2a(nc, p, s, g):
    NT = 4
    NSLAB = T // 512 if P1_NSLAB is None else P1_NSLAB
    dma(p, "qpool", s.wq[:], g["W_Q"].rearrange("(k p) c -> p k c", p=128),
        writes=("wq",))
    dma(p, "qpool", s.wk[:], g["W_K"].rearrange("(k p) c -> p k c", p=128),
        writes=("wk",))
    dma(p, "qpool", s.wv[:], g["W_Vs"].rearrange("(k p) c -> p k c", p=128),
        writes=("wv",))
    pool(p, "memset", s.zrow2[:], 0.0, writes=("zrow2",))
    dma(p, "qpool", g["xn1_dram"][T:T + 1, :], s.zrow2[0:1, 0:D],
        reads=("zrow2",), writes=(("xn1_dram", "z"),))

    x1_r = g["x1_dram"].rearrange("(n tt p) d -> n p tt d", p=128, tt=NT)
    xn1_r = g["xn1_dram"][0:T, :].rearrange("(n tt p) d -> n p tt d", p=128, tt=NT)
    qt_r = g["qt_dram"].rearrange("(m p) t -> p m t", p=128)
    kt_r = g["kt_dram"].rearrange("(m p) t -> p m t", p=128)

    for sl in range(NSLAB):
        t0 = sl * 512
        dma(p, "qsp", s.x1sl[:], x1_r[sl],
            reads=(("x1", 4 * sl), ("x1", 4 * sl + 1), ("x1", 4 * sl + 2),
                   ("x1", 4 * sl + 3)),
            writes=("x1sl",))
        for tt in range(NT):
            emit_ln(p, s.lnxc2, "lnxc2", s.xn1sl[:, tt], "xn1sl",
                    s.stats2, "lnst2",
                    s.x1sl[:, tt], "x1sl", [(s.xn1sl[:, tt], "xn1sl")])
        dma(p, "qsp", xn1_r[sl], s.xn1sl[:], reads=("xn1sl",),
            writes=(("xn1_dram", sl),))
        for k in range(8):
            items = [(s.pt_f[:, tt * 128:(tt + 1) * 128],
                      s.xn1sl[:, tt, k * 128:(k + 1) * 128], s.ident_f32[:])
                     for tt in range(NT)]
            transpose_group(p, items, reads=("xn1sl", "identf32"),
                            write_key="ptf")
            dve(p, "tensor_copy", s.xnT2[:, k], s.pt_f[:, 0:512],
                reads=("ptf",), writes=(("xnT2", k),))
        rxnT = tuple(("xnT2", k) for k in range(8))
        for which, wsb_, kw, stg, kstg, dr in (
                (0, s.wq, "wq", s.qstg, "qstg", qt_r),
                (1, s.wk, "wk", s.kstg, "kstg", kt_r)):
            for m in range(8):
                bk = m % 2 if which == 0 else 2 + m % 2
                mm_group(p, s.ps_mm[bk][:],
                         [(wsb_[:, k, m * 128:(m + 1) * 128], s.xnT2[:, k])
                          for k in range(8)],
                         reads=rxnT + (kw,), write_key=("psmm", bk))
                dve(p, "tensor_copy", stg[:], s.ps_mm[bk][:],
                    reads=(("psmm", bk),), writes=(kstg,))
                dma(p, "qsp", dr[:, m, t0:t0 + 512], stg[:],
                    reads=(kstg,), writes=((("qtd" if which == 0 else "ktd"), sl, m),))
        for tt in range(NT):
            gt = sl * NT + tt
            mm_group(p, s.ps_r[:, 0:VSL],
                     [(s.xnT2[:, k, tt * 128:(tt + 1) * 128], s.wv[:, k])
                      for k in range(8)],
                     reads=rxnT + ("wv",), write_key="psr")
            dve(p, "tensor_copy", s.vbuf[:, gt, :], s.ps_r[:, 0:VSL],
                reads=("psr",), writes=(("vbuf", gt),))


def _decay_tiles():
    i = np.arange(128, dtype=np.float64)[:, None]
    jj = np.arange(512, dtype=np.float64)[None, :]
    e0 = GAMMA ** (jj - i)
    ems = []
    for delta in (0, -128, -256, -384):
        ex = delta + jj - i
        em = np.where(ex >= 0, GAMMA ** ex, 0.0)
        ems.append(em)
    return (e0.astype(np.float32),
            np.stack(ems, 1).astype(np.float32))


def build_phase2b(nc, p, s, g):
    NSLAB = T // 512 if P1_NSLAB is None else P1_NSLAB
    NQ = NSLAB
    dma(p, "qsp", s.e0[:], g["e0_t"], writes=("e0",))
    dma(p, "qsp", s.em[:], g["em_t"], writes=("em",))
    qt_r = g["qt_dram"].rearrange("(m p) t -> p m t", p=128)
    kt_r = g["kt_dram"].rearrange("(m p) t -> p m t", p=128)
    ot_r = g["outT_dram"].rearrange("(m p) t -> p m t", p=128)

    for q in range(NQ):
        tq0 = 512 * q
        dma(p, "qsp", s.qt[:], qt_r[:, :, tq0:tq0 + 512], 
            reads=tuple(("qtd", sl, m) for sl in range(NSLAB) for m in range(8)),
            writes=("qt",))
        s_tiles = list(range(max(0, tq0 - BAND), tq0 + 512, 128))
        for si, s0 in enumerate(s_tiles):
            kb = si % 2
            dma(p, "qsp", s.kt[kb][:], kt_r[:, :, s0:s0 + 128],
                reads=tuple(("ktd", sl, m) for sl in range(NSLAB) for m in range(8)),
                writes=(("kt", kb),))
            bk = si % 2
            mm_group(p, s.ps_mm[bk][:],
                     [(s.kt[kb][:, k], s.qt[:, k]) for k in range(8)],
                     reads=(("kt", kb), "qt"), write_key=("psmm", bk))
            delta = tq0 - s0
            if delta >= 128:
                dve(p, "scalar_tensor_tensor", s.abuf[:, si, :],
                    s.ps_mm[bk][:], float(GAMMA ** delta), s.e0[:],
                    op0=OP.mult, op1=OP.mult,
                    reads=(("psmm", bk), "e0"), writes=(("abuf", si),))
            else:
                mi = (-delta) // 128
                dve(p, "tensor_tensor", s.abuf[:, si, :],
                    s.ps_mm[bk][:], s.em[:, mi, :], op=OP.mult,
                    reads=(("psmm", bk), "em"), writes=(("abuf", si),))
        if q == 0 and g.get("abuf_out") is not None:
            dma(p, "qsp", g["abuf_out"], s.abuf[:],
                reads=tuple(("abuf", si) for si in range(len(s_tiles))),
                writes=("abufout",))
        for mi in range(2):
            bank = s.ps_mm[2 + mi]
            mm_group(p, bank[:],
                     [(s.vbuf[:, s0 // 128, mi * 128:(mi + 1) * 128],
                       s.abuf[:, si, :])
                      for si, s0 in enumerate(s_tiles)],
                     reads=tuple(("abuf", si) for si in range(len(s_tiles)))
                     + tuple(("vbuf", s0 // 128) for s0 in s_tiles),
                     write_key=("psmm", 2 + mi))
            dve(p, "tensor_copy", s.ostg[:], bank[:],
                reads=(("psmm", 2 + mi),), writes=("ostg",))
            dma(p, "qsp", ot_r[:, mi, tq0:tq0 + 512], s.ostg[:],
                reads=("ostg",), writes=(("otd", q, mi),))


def build_scramble(nc, p, s, g):
    rot = tuple(("otd", q, mi) for q in range(T // 512) for mi in range(2))
    for m in range(16):
        src = g["outT_dram"][0:256, 1024 * m:1024 * (m + 1)]
        dst = g["attnbr_dram"].rearrange("(u s) d -> s u d", s=16)[m]
        dma(p, "qpool", dst, src, reads=rot, writes=(("attnbr", m),))


def declare_p2_dram(nc, g, x1_external):
    if x1_external:
        g["x1_dram"] = nc.dram_tensor("x1_in", [T, D], F32, kind="ExternalInput").ap()
    g["W_Q"] = nc.dram_tensor("W_Q", [D, D], F32, kind="ExternalInput").ap()
    g["W_K"] = nc.dram_tensor("W_K", [D, D], F32, kind="ExternalInput").ap()
    g["W_Vs"] = nc.dram_tensor("W_Vs", [D, VSL], F32, kind="ExternalInput").ap()
    g["e0_t"] = nc.dram_tensor("e0_t", [128, 512], F32, kind="ExternalInput").ap()
    g["em_t"] = nc.dram_tensor("em_t", [128, 4, 512], F32, kind="ExternalInput").ap()
    g["xn1_dram"] = nc.dram_tensor("xn1_dram", [T + 128, D], F32).ap()
    g["qt_dram"] = nc.dram_tensor("qt_dram", [D, T], BF16).ap()
    g["kt_dram"] = nc.dram_tensor("kt_dram", [D, T], BF16).ap()
    g["outT_dram"] = nc.dram_tensor("outT_dram", [VSL, T], F32).ap()


def build_p2_program(debug=False):
    import contextlib
    nc = bass.Bass("TRN2", target_bir_lowering=False)
    g = {}
    g["ident_bf"] = nc.dram_tensor("ident_bf", [128, 128], BF16, kind="ExternalInput").ap()
    g["ident_f32"] = nc.dram_tensor("ident_f32", [128, 128], F32, kind="ExternalInput").ap()
    declare_p2_dram(nc, g, x1_external=True)
    g["attnbr_dram"] = nc.dram_tensor("attnbr_out", [CHUNK, D], F32,
                                      kind="ExternalOutput").ap()
    g["xn1_out"] = nc.dram_tensor("xn1_out", [T, D], F32,
                                  kind="ExternalOutput").ap()
    if debug:
        g["outT_out"] = nc.dram_tensor("outT_out", [VSL, T], F32,
                                       kind="ExternalOutput").ap()
        g["abuf_out"] = nc.dram_tensor("abuf_out", [128, 36, 512], BF16,
                                       kind="ExternalOutput").ap()
        g["qt_out"] = nc.dram_tensor("qt_out", [D, T], BF16, kind="ExternalOutput").ap()
        g["kt_out"] = nc.dram_tensor("kt_out", [D, T], BF16, kind="ExternalOutput").ap()

    with contextlib.ExitStack() as stack:
        s = S()
        alloc_persist(nc, stack, s)
        alloc_vbuf(nc, stack, s)
        sems = {}
        for name in ENGINES + QUEUES:
            sems[name] = stack.enter_context(nc.semaphore(name))
        p = Prog(nc)
        for gt in range(T // 128):
            p.track[("x1", gt)] = [None, []]
        with nc.Block() as block:
            dma(p, "qsp", s.ident_bf[:], g["ident_bf"], writes=("identbf",))
            dma(p, "qsp", s.ident_f32[:], g["ident_f32"], writes=("identf32",))
            with contextlib.ExitStack() as st1:
                alloc_phase2a(nc, st1, s)
                build_phase2a(nc, p, s, g)
                p.fence()
                p.materialize(block, sems)
            with contextlib.ExitStack() as st2:
                alloc_phase2b(nc, st2, s)
                build_phase2b(nc, p, s, g)
                p.fence()
                p.materialize(block, sems)
            build_scramble(nc, p, s, g)
            if g.get("qt_out") is not None:
                dma(p, "qsp", g["qt_out"], g["qt_dram"],
                    reads=tuple(("qtd", sl, m) for sl in range(T // 512) for m in range(8)),
                    writes=("qtout",))
                dma(p, "qsp", g["kt_out"], g["kt_dram"],
                    reads=tuple(("ktd", sl, m) for sl in range(T // 512) for m in range(8)),
                    writes=("ktout",))
            if g.get("outT_out") is not None:
                dma(p, "qsp", g["outT_out"], g["outT_dram"],
                    reads=tuple(("otd", q, mi) for q in range(T // 512)
                                for mi in range(2)),
                    writes=("outTout",))
            dma(p, "qsp", g["xn1_out"], g["xn1_dram"][0:T, :],
                reads=tuple(("xn1_dram", sl) for sl in range(T // 512)),
                writes=("xn1out",))
            p.barrier_all()
            p.materialize(block, sems)
    return nc


def prep_p2_inputs(x1, inp, j):
    d = {}
    d["x1_in"] = np.ascontiguousarray(x1)
    d["W_Q"] = np.ascontiguousarray(_np(inp["L1_attn_W_Q"]))
    d["W_K"] = np.ascontiguousarray(_np(inp["L1_attn_W_K"]))
    d["W_Vs"] = np.ascontiguousarray(_np(inp["L1_attn_W_V"])[:, VSL * j:VSL * (j + 1)])
    e0, em = _decay_tiles()
    d["e0_t"] = e0
    d["em_t"] = np.ascontiguousarray(em)
    d["ident_bf"] = np.eye(128, dtype=BF16NP)
    d["ident_f32"] = np.eye(128, dtype=np.float32)
    return d



DEPTH3 = 6
HEADS = 16
DH = 64
WIN = 256
NWIN = SLABT // WIN
NTI = SLABT // 128
DIN_P = 2816
QK_SCALE = 8.0
P3_PARANOID = False


def alloc_phase3(nc, stack, s):
    sb, _ = _mk(nc, stack, "s3_")
    s.wbig = sb("wbig", [128, 24576], BF16)
    s.wsm = sb("wsm", [128, 8192], BF16)
    s.wqkv = s.wbig[:].rearrange("p (k c) -> p k c", k=8)
    s.w2p = s.wbig[:, 0:22528].rearrange("p (k c) -> p k c", k=22)
    s.wo3 = s.wsm[:].rearrange("p (k c) -> p k c", k=8)
    s.w1ch = [s.wsm[:, 0:4096].rearrange("p (k c) -> p k c", k=8),
              s.wsm[:, 4096:8192].rearrange("p (k c) -> p k c", k=8)]
    s.xt3 = sb("xt3", [128, 1024], F32)
    s.xt3b = sb("xt3b", [128, 1024], BF16)
    s.xd3 = sb("xd3", [128, 1024], F32)
    s.lnxc3 = sb("lnxc3", [128, D], F32)
    s.stats3 = sb("stats3", [128, 8], F32)
    s.xnT3 = sb("xnT3", [128, 8, 128], BF16)
    s.qkv = sb("qkv", [128, 3072], F32)
    s.u8k = sb("u8k", [128, 4096], BF16)
    s.qkb = s.u8k[:, 0:2048]
    s.hs = sb("hs", [128, 24], F32)
    s.rot = sb("rot", [128, 4, 64], F32)
    s.qsk = sb("qsk", [128, 2, 64], F32)
    s.qT = s.u8k[:, 2048:4096].rearrange("p (k c) -> p k c", k=8)
    s.uAB = sb("uAB", [128, 12288], BF16)
    s.kT = [s.uAB[:, 0:2048].rearrange("p (k c) -> p k c", k=8),
            s.uAB[:, 2048:4096].rearrange("p (k c) -> p k c", k=8)]
    s.vb = [s.uAB[:, 4096:6144].rearrange("p (k c) -> p k c", k=2),
            s.uAB[:, 6144:8192].rearrange("p (k c) -> p k c", k=2)]
    s.mks = sb("mks", [128, 2, 2, 512], BF16)
    s.mkw = sb("mkw", [128, 2, 512], BF16)
    s.selv = sb("selv", [128, NWIN], F32)
    s.hkb = sb("hkb", [128, NTI], F32)
    s.attnb = sb("attnb", [128, 512], BF16)
    s.attnT = sb("attnT", [128, 4, 128], BF16)
    s.attno = sb("attno", [128, 1024], BF16)
    s.headsT = s.uAB[:, 8192:12288].rearrange("p (k c) -> p k c", k=8)
    s.wst = sb("wst", [128, 8, 512], F32)
    s.h2T = sb("h2T", [128, 22, 512], BF16)
    s.sga = sb("sga", [128, 512], F32)
    s.sgg = sb("sgg", [128, 512], F32)
    s.gbt = s.uAB[:, 0:4096].bitcast(F32).rearrange("p (k c) -> p k c", k=2)
    s.b2t = s.uAB[:, 4096:6144].bitcast(F32)
    s.b1t = sb("b1t", [128, 44], F32)
    s.rowst = s.uAB[0:1, 6144:12288].bitcast(F32).rearrange("p (k c) -> p k c", k=3)
    s.xnT512 = s.u8k[:].rearrange("p (k c) -> p k c", k=8)
    s.att1 = sb("att1", [128, 1024], F32)
    s.att2 = sb("att2", [128, 1024], F32)
    s.idx3 = sb("idx3", [128, 1], U32)
    s.x1t = sb("x1t", [128, 1024], F32)
    s.abt = sb("abt", [128, 1024], F32)


def _shifted_tile(p, s, g, src, gen, it, dstbuf, dstkey):
    r0 = 1 + it * 128
    k0 = ("xc", gen, it)
    km = ("xc", gen, it - 1) if it > 0 else ("xcz", gen % 2)
    dma(p, "qsp", dstbuf[:, 0:512], src[r0:r0 + 128, 0:512],
        reads=(k0,), writes=(dstkey,))
    dma(p, "qsp", dstbuf[:, 512:1024], src[r0 - 1:r0 + 127, 512:1024],
        reads=(k0, km), writes=(dstkey,))


def _residual_writeback(p, s, g, dst, gen_out, t4, branch_stage, kstage,
                        src, gen_in):
    for q in range(4):
        it = t4 + q
        r0 = 1 + it * 128
        dma(p, "qsp", s.xd3[:], src[r0:r0 + 128, :],
            reads=(("xc", gen_in, it),), writes=("xd3",))
        for half in range(2):
            items = [(s.pt_f[:, u * 128:(u + 1) * 128],
                      branch_stage[:, half * 4 + u, q * 128:(q + 1) * 128],
                      s.ident_f32[:]) for u in range(4)]
            transpose_group(p, items, reads=(kstage, "identf32"),
                            write_key="ptf")
            dve(p, "tensor_tensor", s.xd3[:, half * 512:(half + 1) * 512],
                s.xd3[:, half * 512:(half + 1) * 512], s.pt_f[:],
                op=OP.add, reads=("ptf", "xd3"), writes=("xd3",))
        dve(p, "tensor_scalar", s.xd3[:], s.xd3[:], s.hkb[:, it:it + 1],
            None, op0=OP.mult, reads=("xd3", "hkb"), writes=("xd3",))
        dma(p, "qsp", dst[r0:r0 + 128, :], s.xd3[:], reads=("xd3",),
            writes=(("xc", gen_out, it),))


def build_p3_attn(nc, p, s, g, li, src, dst):
    gen = 2 * li
    dma(p, "qpool", s.wqkv,
        g["wqkv3"][li].rearrange("(k p) c -> p k c", p=128), writes=("wbig",))
    dma(p, "qpool", s.wo3,
        g["wo3_w"][li].rearrange("(k p) c -> p k c", p=128),
        writes=(("wsm", 0), ("wsm", 1)))
    dma(p, "qsp", s.qsk[:], g["qsk3"][li], writes=("qsk",))
    pool(p, "memset", s.kT[1][:], 0.0, writes=(("kTr", 1),))
    pool(p, "memset", s.vb[1][:], 0.0, writes=(("vbr", 1),))

    for w in range(NWIN):
        cur = w % 2
        prv = 1 - cur
        for par in range(2):
            dve(p, "scalar_tensor_tensor", s.mkw[:, par, :],
                s.mks[:, 1, par, :], s.selv[:, w:w + 1], s.mks[:, 0, par, :],
                op0=OP.mult, op1=OP.add,
                reads=("mks", "selv"), writes=("mkw",))
        for sub in range(2):
            it = 2 * w + sub
            _shifted_tile(p, s, g, src, gen, it, s.xt3, "xt3")
            dve(p, "tensor_copy", s.xt3b[:], s.xt3[:], reads=("xt3",),
                writes=("xt3b",))
            items = [(s.pt_b[:, k * 128:(k + 1) * 128],
                      s.xt3b[:, k * 128:(k + 1) * 128], s.ident_bf[:])
                     for k in range(8)]
            transpose_group(p, items, reads=("xt3b", "identbf"),
                            write_key="ptb")
            dve(p, "tensor_copy", s.xnT3[:],
                s.pt_b[:].rearrange("p (k x) -> p k x", k=8),
                reads=("ptb",), writes=("xnT3",))
            for nch in range(6):
                bk = nch % 4
                mm_group(p, s.ps_mm[bk][:],
                         [(s.xnT3[:, k], s.wqkv[:, k, nch * 512:(nch + 1) * 512])
                          for k in range(8)],
                         reads=("xnT3", "wbig"), write_key=("psmm", bk))
                dve(p, "tensor_copy", s.qkv[:, nch * 512:(nch + 1) * 512],
                    s.ps_mm[bk][:], reads=(("psmm", bk),), writes=("qkv",))
            dma(p, "qsp", s.rot[:], g["rot3"][it], writes=("rot",))
            for which, off, sci, ssi in ((0, 0, 0, 1), (1, 1024, 2, 3)):
                qv = s.qkv[:, off:off + 1024]
                hv = qv.rearrange("p (h d) -> p h d", d=DH)
                dve(p, "scalar_tensor_tensor", s.att1[:], qv, 1.0, qv,
                    op0=OP.mult, op1=OP.mult, reads=("qkv",), writes=("att1",))
                dve(p, "tensor_reduce", s.hs[:, 8:24],
                    s.att1[:].rearrange("p (h d) -> p h d", d=DH),
                    axis=AX.X, op=OP.add, reads=("att1",), writes=("hsn",))
                act(p, s.hs[:, 8:24], s.hs[:, 8:24], AF.Sqrt,
                    reads=("hsn",), writes=("hsn",))
                dve(p, "tensor_scalar", s.hs[:, 8:24], s.hs[:, 8:24], 1e-12,
                    None, op0=OP.max, reads=("hsn",), writes=("hsn",))
                dve(p, "reciprocal", s.hs[:, 8:24], s.hs[:, 8:24],
                    reads=("hsn",), writes=("hsn",))
                dve(p, "tensor_tensor", hv, hv,
                    s.hs[:, 8:24].unsqueeze(2).broadcast_to([128, 16, DH]),
                    op=OP.mult, reads=("qkv", "hsn"), writes=("qkv",))
                dve(p, "tensor_tensor", hv, hv,
                    s.qsk[:, which, :].unsqueeze(1).broadcast_to([128, 16, DH]),
                    op=OP.mult, reads=("qkv", "qsk"), writes=("qkv",))
                cs = s.rot[:, sci, :]
                ss = s.rot[:, ssi, :]
                a1 = s.att1[:].rearrange("p (h d) -> p h d", d=DH)
                h1 = s.att2[:].rearrange("p (h d) -> p h d", d=DH)
                dve(p, "tensor_tensor", a1, hv,
                    cs.unsqueeze(1).broadcast_to([128, 16, DH]),
                    op=OP.mult, reads=("qkv", "rot"), writes=("att1",))
                dve(p, "tensor_tensor", h1[:, :, 0:32], hv[:, :, 32:64],
                    ss[:, 0:32].unsqueeze(1).broadcast_to([128, 16, 32]),
                    op=OP.mult, reads=("qkv", "rot"), writes=("att2",))
                dve(p, "tensor_tensor", h1[:, :, 32:64], hv[:, :, 0:32],
                    ss[:, 32:64].unsqueeze(1).broadcast_to([128, 16, 32]),
                    op=OP.mult, reads=("qkv", "rot"), writes=("att2",))
                dve(p, "tensor_tensor", hv[:, :, 0:32], a1[:, :, 0:32],
                    h1[:, :, 0:32], op=OP.subtract,
                    reads=("att1", "att2"), writes=("qkv",))
                dve(p, "tensor_tensor", hv[:, :, 32:64], a1[:, :, 32:64],
                    h1[:, :, 32:64], op=OP.add,
                    reads=("att1", "att2"), writes=("qkv",))
            if li == 0 and it == 4 and g.get("dbgq") is not None:
                dma(p, "qsp", g["dbgq"], s.qkv[:], reads=("qkv",),
                    writes=("dbgq",))
            dve(p, "tensor_copy", s.vb[cur][:, sub, :],
                s.qkv[:, 2048:3072], reads=("qkv",), writes=(("vbr", cur),))
            dve(p, "tensor_copy", s.qkb, s.qkv[:, 0:2048],
                reads=("qkv",), writes=("qkb",))
            for which, dstT, kdst in ((0, s.qT, "qT"), (1, s.kT[cur], ("kTr", cur))):
                items = [(s.pt_b[:, b * 128:(b + 1) * 128],
                          s.qkb[:, which * 1024 + b * 128:which * 1024 + (b + 1) * 128],
                          s.ident_bf[:]) for b in range(8)]
                transpose_group(p, items, reads=("qkb", "identbf"),
                                write_key="ptb")
                dve(p, "tensor_copy", dstT[:, :, sub * 128:(sub + 1) * 128],
                    s.pt_b[:].rearrange("p (b x) -> p b x", b=8),
                    reads=("ptb",), writes=(kdst,))
        for tt in range(2):
            for h in range(HEADS):
                blk = h // 2
                pb = 64 * (h % 2)
                bk = tt
                def emit_sc(eng, blk=blk, pb=pb, tt=tt, bk=bk, cur=cur, prv=prv):
                    eng.matmul(s.ps_mm[bk][:, 0:256],
                               s.qT[:, blk, tt * 128:(tt + 1) * 128][pb:pb + 64, :],
                               s.kT[prv][:, blk, :][pb:pb + 64, :],
                               start=True, stop=False)
                    eng.matmul(s.ps_mm[bk][:, 256:512],
                               s.qT[:, blk, tt * 128:(tt + 1) * 128][pb:pb + 64, :],
                               s.kT[cur][:, blk, :][pb:pb + 64, :],
                               start=False, stop=False)
                    return eng.matmul(s.ps_mm[bk][:],
                                      s.ident_bf[:], s.mkw[:, tt, :],
                                      start=False, stop=True)
                p.op("pe", emit_sc,
                     reads=("qT", ("kTr", prv), ("kTr", cur), "identbf", "mkw"),
                     writes=(("psmm", bk),))
                if li == 0 and w == 2 and h == 0 and tt == 0 and g.get("dbgs") is not None:
                    dve(p, "tensor_copy", s.wst[:, 0, :], s.ps_mm[bk][:],
                        reads=(("psmm", bk),), writes=("wst",))
                    dma(p, "qsp", g["dbgs"], s.wst[:, 0, :], reads=("wst",),
                        writes=("dbgs",))
                dve(p, "tensor_reduce", s.hs[:, 0:1], s.ps_mm[bk][:],
                    axis=AX.X, op=OP.max, negate=True,
                    reads=(("psmm", bk),), writes=("hs",))
                dve(p, "tensor_scalar", s.hs[:, 1:2], s.hs[:, 0:1],
                    QK_SCALE, None, op0=OP.mult, reads=("hs",), writes=("hs",))
                act(p, s.attnb[:], s.ps_mm[bk][:], AF.Exp,
                    bias=s.hs[:, 1:2], scale=QK_SCALE, accum_out=s.hs[:, 2:3],
                    reads=(("psmm", bk), "hs"), writes=("attnb", "hs2"))
                dve(p, "reciprocal", s.hs[:, 3:4], s.hs[:, 2:3],
                    reads=("hs2",), writes=("hs2",))
                dve(p, "tensor_scalar", s.attnb[:], s.attnb[:], s.hs[:, 3:4],
                    None, op0=OP.mult, reads=("attnb", "hs2"),
                    writes=("attnb",))
                items = [(s.pt_b[:, u * 128:(u + 1) * 128],
                          s.attnb[:, u * 128:(u + 1) * 128], s.ident_bf[:])
                         for u in range(4)]
                transpose_group(p, items, reads=("attnb", "identbf"),
                                write_key="ptb")
                dve(p, "tensor_copy", s.attnT[:],
                    s.pt_b[:, 0:512].rearrange("p (a x) -> p a x", a=4),
                    reads=("ptb",), writes=("attnT",))
                avbk = 2 + h // 8
                def emit_av(eng, h=h, avbk=avbk, cur=cur, prv=prv):
                    inst = None
                    c0 = 64 * (h % 8)
                    for a in range(4):
                        vtile = s.vb[prv if a < 2 else cur][:, a % 2,
                                                            h * 64:(h + 1) * 64]
                        inst = eng.matmul(
                            s.ps_mm[avbk][:, c0:c0 + 64],
                            s.attnT[:, a, :], vtile,
                            start=(h % 8 == 0 and a == 0),
                            stop=(h % 8 == 7 and a == 3))
                    return inst
                p.op("pe", emit_av,
                     reads=("attnT", ("vbr", prv), ("vbr", cur)),
                     writes=(("psmm", avbk),))
            for bb in range(2):
                dve(p, "tensor_copy", s.qkb[:, bb * 512:(bb + 1) * 512].bitcast(BF16)[:, 0:512] if False else s.attno[:, bb * 512:(bb + 1) * 512],
                    s.ps_mm[2 + bb][:], reads=(("psmm", 2 + bb),),
                    writes=("attno",))
            items = [(s.pt_b[:, u * 128:(u + 1) * 128],
                      s.attno[:, u * 128:(u + 1) * 128], s.ident_bf[:])
                     for u in range(8)]
            transpose_group(p, items, reads=("attno", "identbf"),
                            write_key="ptb")
            dve(p, "tensor_copy",
                s.headsT[:, :, cur * 256 + tt * 128:cur * 256 + tt * 128 + 128],
                s.pt_b[:].rearrange("p (k x) -> p k x", k=8),
                reads=("ptb",), writes=("headsT",))
        if w % 2 == 1:
            for m in range(8):
                bk = m % 4
                mm_group(p, s.ps_mm[bk][:],
                         [(s.wo3[:, k, m * 128:(m + 1) * 128], s.headsT[:, k, :])
                          for k in range(8)],
                         reads=("headsT", ("wsm", 0), ("wsm", 1)), write_key=("psmm", bk))
                dve(p, "tensor_copy", s.wst[:, m, :], s.ps_mm[bk][:],
                    reads=(("psmm", bk),), writes=("wst",))
            _residual_writeback(p, s, g, dst, gen + 1, 2 * (w - 1),
                                s.wst[:], "wst", src, gen)


def bcast_row(p, s, dst_ap, dst_key, row_ap, row_key, width=512):
    def emit1(eng):
        return eng.matmul(s.ps_r[:, 0:width], s.onescol32[:],
                          row_ap, start=True, stop=True)
    p.op("pe", emit1, reads=(row_key, "onescol32"), writes=("psr",))
    dve(p, "tensor_copy", dst_ap, s.ps_r[:, 0:width],
        reads=("psr",), writes=(dst_key,))


def build_p3_ffn(nc, p, s, g, li, src, dst):
    gen = 2 * li + 1
    dma(p, "qpool", s.w2p,
        g["w2p3"][li].rearrange("(k p) c -> p k c", p=128), writes=("wbig",))
    dma(p, "qsp", s.rowst[:], g["ffnrows"][li], writes=("rowst",))
    dma(p, "qsp", s.b1t[:], g["b1_3"][li], writes=("b1t",))
    for ci in range(2):
        for ri, (dstap, dk) in enumerate(((s.gbt[:, 0, :], "gbt"),
                                          (s.gbt[:, 1, :], "gbt"),
                                          (s.b2t[:, :], "b2t"))):
            bcast_row(p, s, dstap[:, ci * 512:(ci + 1) * 512], dk,
                      s.rowst[0:1, ri, ci * 512:(ci + 1) * 512], "rowst")

    w1_r = g["w1p3"][li].rearrange("(k p) c -> p k c", p=128)
    for t4 in range(0, NTI, 4):
        for q in range(4):
            it = t4 + q
            _shifted_tile(p, s, g, src, gen, it, s.xt3, "xt3")
            emit_ln(p, s.lnxc3, "lnxc3", s.att1, "att1", s.stats3, "lnst3",
                    s.xt3[:], "xt3", [(s.xt3[:], "xt3")])
            dve(p, "tensor_tensor", s.xt3[:], s.xt3[:], s.gbt[:, 0, :],
                op=OP.mult, reads=("xt3", "gbt"), writes=("xt3",))
            dve(p, "tensor_tensor", s.xt3[:], s.xt3[:], s.gbt[:, 1, :],
                op=OP.add, reads=("xt3", "gbt"), writes=("xt3",))
            dve(p, "tensor_copy", s.xt3b[:], s.xt3[:], reads=("xt3",),
                writes=("xt3b",))
            items = [(s.pt_b[:, k * 128:(k + 1) * 128],
                      s.xt3b[:, k * 128:(k + 1) * 128], s.ident_bf[:])
                     for k in range(8)]
            transpose_group(p, items, reads=("xt3b", "identbf"),
                            write_key="ptb")
            dve(p, "tensor_copy", s.xnT512[:, :, q * 128:(q + 1) * 128],
                s.pt_b[:].rearrange("p (k x) -> p k x", k=8),
                reads=("ptb",), writes=("qkb", "qT"))
        for ci in range(11):
            wb = s.w1ch[ci % 2]
            kwb = ("wsm", ci % 2)
            dma(p, "qpool", wb, w1_r[:, :, ci * 512:(ci + 1) * 512],
                writes=(kwb,))
            for u2 in range(2):
                u = 2 * ci + u2
                ba, bg = 2 * u2, 2 * u2 + 1
                mm_group(p, s.ps_mm[ba][:],
                         [(wb[:, k, u2 * 256:u2 * 256 + 128], s.xnT512[:, k, :])
                          for k in range(8)],
                         reads=(kwb, "qkb", "qT"), write_key=("psmm", ba))
                mm_group(p, s.ps_mm[bg][:],
                         [(wb[:, k, u2 * 256 + 128:u2 * 256 + 256], s.xnT512[:, k, :])
                          for k in range(8)],
                         reads=(kwb, "qkb", "qT"), write_key=("psmm", bg))
                dve(p, "tensor_scalar", s.sga[:], s.ps_mm[ba][:],
                    s.b1t[:, u:u + 1], None, op0=OP.add,
                    reads=(("psmm", ba), "b1t"), writes=("sga",))
                dve(p, "tensor_scalar", s.sgg[:], s.ps_mm[bg][:],
                    s.b1t[:, 22 + u:23 + u], None, op0=OP.add,
                    reads=(("psmm", bg), "b1t"), writes=("sgg",))
                act(p, s.sgg[:], s.sgg[:], AF.Gelu, reads=("sgg",),
                    writes=("sgg",))
                dve(p, "tensor_tensor", s.h2T[:, u, :], s.sga[:], s.sgg[:],
                    op=OP.mult, reads=("sga", "sgg"), writes=(("h2T", u),))
        for q in range(4):
            it = t4 + q
            for n in range(2):
                bk = n
                mm_group(p, s.ps_mm[bk][:],
                         [(s.h2T[:, k, q * 128:(q + 1) * 128],
                           s.w2p[:, k, n * 512:(n + 1) * 512])
                          for k in range(22)],
                         reads=tuple(("h2T", k) for k in range(22)) + ("wbig",),
                         write_key=("psmm", bk))
            r0 = 1 + it * 128
            dma(p, "qsp", s.xd3[:], src[r0:r0 + 128, :],
                reads=(("xc", gen, it),), writes=("xd3",))
            for n in range(2):
                dve(p, "tensor_tensor", s.xd3[:, n * 512:(n + 1) * 512],
                    s.xd3[:, n * 512:(n + 1) * 512], s.ps_mm[n][:],
                    op=OP.add, reads=(("psmm", n), "xd3"), writes=("xd3",))
            dve(p, "tensor_tensor", s.xd3[:], s.xd3[:], s.b2t[:],
                op=OP.add, reads=("xd3", "b2t"), writes=("xd3",))
            dve(p, "tensor_scalar", s.xd3[:], s.xd3[:], s.hkb[:, it:it + 1],
                None, op0=OP.mult, reads=("xd3", "hkb"), writes=("xd3",))
            dma(p, "qsp", dst[r0:r0 + 128, :], s.xd3[:], reads=("xd3",),
                writes=(("xc", gen + 1, it),))


def build_phase3(nc, p, s, g):
    xcb = [g["xc_dram"], g["xd_dram"]]
    dma(p, "qsp", s.mks[:], g["mask3"], writes=("mks",))
    dma(p, "qsp", s.selv[:], g["selv"], writes=("selv",))
    dma(p, "qsp", s.hkb[:], g["hkb"], writes=("hkb",))
    pool(p, "memset", s.xt3[:], 0.0, writes=("xt3",))
    for buf in range(2):
        dma(p, "qpool", xcb[buf][0:1, :], s.xt3[0:1, 0:D], reads=("xt3",),
            writes=(("xcz", buf),))
    for it in range(NTI):
        dma(p, "qsp", s.idx3[:], g["idx_slab"][it].unsqueeze(1), writes=("idx3",))
        gather(p, s.xt3[:], g["xn1_dram"], s.idx3[:, 0:1],
               reads=("idx3", ("xn1_dram", "z"))
               + tuple(("xn1_dram", sl) for sl in range(T // 512)),
               writes=("xt3",))
        dve(p, "tensor_scalar", s.xt3[:], s.xt3[:], s.hkb[:, it:it + 1],
            None, op0=OP.mult, reads=("xt3", "hkb"), writes=("xt3",))
        dma(p, "qsp", xcb[0][1 + it * 128:1 + (it + 1) * 128, :], s.xt3[:],
            reads=("xt3",), writes=(("xc", 0, it),))
    if g.get("dbg3") is not None:
        dma(p, "qpool", g["dbg3"][0], xcb[0][:, :],
            reads=tuple(("xc", 0, it) for it in range(NTI)), writes=("d30",))
    for li in range(DEPTH3):
        build_p3_attn(nc, p, s, g, li, xcb[0], xcb[1])
        p.fence()
        if li == 0 and g.get("dbg3") is not None:
            dma(p, "qpool", g["dbg3"][1], xcb[1][:, :],
                reads=tuple(("xc", 1, it) for it in range(NTI)), writes=("d31",))
        build_p3_ffn(nc, p, s, g, li, xcb[1], xcb[0])
        p.fence()
        if li == 0 and g.get("dbg3") is not None:
            dma(p, "qpool", g["dbg3"][2], xcb[0][:, :],
                reads=tuple(("xc", 2, it) for it in range(NTI)), writes=("d32",))


def build_final(nc, p, s, g):
    for it in range(CHUNK // 128):
        dma(p, "qsp", s.idx3[:], g["idx_own"][it].unsqueeze(1), writes=("idx3",))
        gather(p, s.x1t[:], g["x1_dram"], s.idx3[:, 0:1],
               reads=("idx3",) + tuple(("x1", gt) for gt in range(T // 128)),
               writes=("x1t",))
        dma(p, "qsp", s.abt[:],
            g["attnbr_dram"][it * 128:(it + 1) * 128, :],
            reads=tuple(("attnbr", m) for m in range(16)), writes=("abt",))
        r0 = 1 + (HALO // 128 + it) * 128
        dma(p, "qsp", s.xd3[:], g["xc_dram"][r0:r0 + 128, :],
            reads=(("xc", 2 * DEPTH3, HALO // 128 + it),), writes=("xd3",))
        dve(p, "tensor_tensor", s.x1t[:], s.x1t[:], s.abt[:], op=OP.add,
            reads=("x1t", "abt"), writes=("x1t",))
        dve(p, "tensor_tensor", s.x1t[:], s.x1t[:], s.xd3[:], op=OP.add,
            reads=("x1t", "xd3"), writes=("x1t",))
        dma(p, "qsp", g["out_dram"][it * 128:(it + 1) * 128, :], s.x1t[:],
            reads=("x1t",), writes=(("out", it),))


def declare_p3_dram(nc, g, xn1_external):
    if xn1_external:
        g["xn1_dram"] = nc.dram_tensor("xn1_in", [T + 128, D], F32,
                                       kind="ExternalInput").ap()
    g["wqkv3"] = nc.dram_tensor("wqkv3", [DEPTH3, D, 3072], F32, kind="ExternalInput").ap()
    g["wo3_w"] = nc.dram_tensor("wo3_w", [DEPTH3, D, D], F32, kind="ExternalInput").ap()
    g["w1p3"] = nc.dram_tensor("w1p3", [DEPTH3, D, 2 * DIN_P], F32, kind="ExternalInput").ap()
    g["w2p3"] = nc.dram_tensor("w2p3", [DEPTH3, DIN_P, D], F32, kind="ExternalInput").ap()
    g["qsk3"] = nc.dram_tensor("qsk3", [DEPTH3, 128, 2, DH], F32, kind="ExternalInput").ap()
    g["rot3"] = nc.dram_tensor("rot3", [NTI, 128, 4, DH], F32, kind="ExternalInput").ap()
    g["mask3"] = nc.dram_tensor("mask3", [128, 2, 2, 512], BF16, kind="ExternalInput").ap()
    g["selv"] = nc.dram_tensor("selv", [128, NWIN], F32, kind="ExternalInput").ap()
    g["hkb"] = nc.dram_tensor("hkb", [128, NTI], F32, kind="ExternalInput").ap()
    g["idx_slab"] = nc.dram_tensor("idx_slab", [NTI, 128], U32, kind="ExternalInput").ap()
    g["idx_own"] = nc.dram_tensor("idx_own", [CHUNK // 128, 128], U32, kind="ExternalInput").ap()
    g["ffnrows"] = nc.dram_tensor("ffnrows", [DEPTH3, 1, 3, D], F32, kind="ExternalInput").ap()
    g["b1_3"] = nc.dram_tensor("b1_3", [DEPTH3, 128, 44], F32, kind="ExternalInput").ap()
    g["xc_dram"] = nc.dram_tensor("xc_dram", [1 + SLABT, D], F32).ap()
    g["xd_dram"] = nc.dram_tensor("xd_dram", [1 + SLABT, D], F32).ap()


def build_p3_program(debug=False):
    import contextlib
    nc = bass.Bass("TRN2", target_bir_lowering=False)
    g = {}
    g["ident_bf"] = nc.dram_tensor("ident_bf", [128, 128], BF16, kind="ExternalInput").ap()
    g["ident_f32"] = nc.dram_tensor("ident_f32", [128, 128], F32, kind="ExternalInput").ap()
    g["onescol"] = nc.dram_tensor("onescol", [1, 128], F32, kind="ExternalInput").ap()
    declare_p3_dram(nc, g, xn1_external=True)
    g["x1_dram"] = nc.dram_tensor("x1_in", [T, D], F32, kind="ExternalInput").ap()
    g["attnbr_dram"] = nc.dram_tensor("attnbr_in", [CHUNK, D], F32, kind="ExternalInput").ap()
    g["out_dram"] = nc.dram_tensor("out", [CHUNK, D], F32, kind="ExternalOutput").ap()
    if debug:
        g["dbg3"] = nc.dram_tensor("dbg3", [3, 1 + SLABT, D], F32, kind="ExternalOutput").ap()
        g["dbgq"] = nc.dram_tensor("dbgq", [128, 3072], F32, kind="ExternalOutput").ap()
        g["dbgh"] = nc.dram_tensor("dbgh", [128, 8, 256], BF16, kind="ExternalOutput").ap()
        g["dbgs"] = nc.dram_tensor("dbgs", [128, 512], F32, kind="ExternalOutput").ap()
        g["dbga"] = nc.dram_tensor("dbga", [128, 512], BF16, kind="ExternalOutput").ap()
        g["dbgo"] = nc.dram_tensor("dbgo", [128, 256], F32, kind="ExternalOutput").ap()
        g["dbgv"] = nc.dram_tensor("dbgv", [2, 128, 2, 1024], BF16, kind="ExternalOutput").ap()

    with contextlib.ExitStack() as stack:
        s = S()
        alloc_persist(nc, stack, s)
        sems = {}
        for name in ENGINES + QUEUES:
            sems[name] = stack.enter_context(nc.semaphore(name))
        p = Prog(nc)
        with nc.Block() as block:
            dma(p, "qsp", s.ident_bf[:], g["ident_bf"], writes=("identbf",))
            dma(p, "qsp", s.ident_f32[:], g["ident_f32"], writes=("identf32",))
            dma(p, "qsp", s.onescol32[:], g["onescol"], writes=("onescol32",))
            with contextlib.ExitStack() as st1:
                alloc_phase3(nc, st1, s)
                build_phase3(nc, p, s, g)
                build_final(nc, p, s, g)
                p.fence()
                p.materialize(block, sems)
            p.barrier_all()
            p.materialize(block, sems)
    return nc


def _rotary_tables_np():
    n = T
    half = np.arange(0, DH, 2, dtype=np.float64)
    inv_freq = 1.0 / 10000.0 ** (half / DH)
    t = np.arange(n, dtype=np.float64)
    freqs = t[:, None] * inv_freq[None, :]
    freqs = np.concatenate([freqs, freqs], -1)
    base_scale = (half + 0.4 * DH) / (1.4 * DH)
    power = (t - n // 2) / (WIN // 2)
    scale = base_scale[None, :] ** power[:, None]
    scale = np.concatenate([scale, scale], -1)
    cos, sin = np.cos(freqs), np.sin(freqs)
    out = np.stack([cos * scale, sin * scale, cos / scale, sin / scale], 1)
    return out.astype(np.float32)


def _mask_tiles_np():
    out = np.zeros((128, 2, 2, 512), np.float32)
    pq = np.arange(128)
    jj = np.arange(512)
    for par in range(2):
        tq = 128 * par + pq
        allowed = (jj[None, :] >= tq[:, None]) & (jj[None, :] <= tq[:, None] + 256)
        out[:, 0, par, :] = np.where(allowed, 0.0, NEG)
        out[:, 1, par, :] = np.where(jj[None, :] < 256, NEG, 0.0)
    return out.astype(BF16NP)


def prep_p3_inputs(inp, j):
    d = {}
    d["ident_bf"] = np.eye(128, dtype=BF16NP)
    d["ident_f32"] = np.eye(128, dtype=np.float32)
    d["onescol"] = np.ones((1, 128), np.float32)
    d["wqkv3"] = np.ascontiguousarray(_np(inp["L1_ffn_Wqkv"]))
    d["wo3_w"] = np.ascontiguousarray(_np(inp["L1_ffn_Wo"]))
    W1 = _np(inp["L1_ffn_W1"])
    b1 = _np(inp["L1_ffn_b1"])
    W2 = _np(inp["L1_ffn_W2"])
    W1q = np.zeros((DEPTH3, D, 2 * DIN_P), np.float32)
    b1q = np.zeros((DEPTH3, 128, 44), np.float32)
    for u in range(22):
        c0, c1 = u * 128, min(2730, (u + 1) * 128)
        w = c1 - c0
        W1q[:, :, 256 * u:256 * u + w] = W1[:, :, c0:c1]
        W1q[:, :, 256 * u + 128:256 * u + 128 + w] = W1[:, :, 2730 + c0:2730 + c1]
        b1q[:, 0:w, u] = b1[:, c0:c1].T.reshape(w, DEPTH3).T.T if False else 0
        b1q[:, :, u][:, 0:w] = b1[:, c0:c1]
        b1q[:, :, 22 + u][:, 0:w] = b1[:, 2730 + c0:2730 + c1]
    d["w1p3"] = W1q
    d["b1_3"] = np.ascontiguousarray(np.transpose(b1q, (0, 2, 1))) if False else b1q
    W2q = np.zeros((DEPTH3, DIN_P, D), np.float32)
    W2q[:, 0:2730, :] = 0
    for u in range(22):
        c0, c1 = u * 128, min(2730, (u + 1) * 128)
        W2q[:, u * 128:u * 128 + (c1 - c0), :] = W2[:, c0:c1, :]
    d["w2p3"] = W2q
    qs = _np(inp["L1_ffn_q_scale"]).astype(np.float32)
    ks = _np(inp["L1_ffn_k_scale"]).astype(np.float32)
    qsk = np.zeros((DEPTH3, 128, 2, DH), np.float32)
    qsk[:, :, 0, :] = qs[:, None, :]
    qsk[:, :, 1, :] = ks[:, None, :]
    d["qsk3"] = qsk
    rot = _rotary_tables_np()
    start = 4096 * j - HALO
    pos = np.clip(np.arange(start, start + SLABT), 0, T - 1)
    d["rot3"] = np.ascontiguousarray(
        rot[pos].reshape(NTI, 128, 4, DH))
    d["mask3"] = _mask_tiles_np()
    selv = np.zeros((128, NWIN), np.float32)
    gw0 = (0 - start) // WIN
    if 0 <= gw0 < NWIN and start <= 0:
        selv[:, gw0] = 1.0
    d["selv"] = selv
    hk = np.ones((128, NTI), np.float32)
    if j == 0:
        for it in range(NTI):
            tpos = start + it * 128 + np.arange(128)
            hk[:, it] = (tpos >= 0).astype(np.float32)
    d["hkb"] = hk
    idx = np.empty((NTI, 128), np.uint32)
    for it in range(NTI):
        tpos = start + it * 128 + np.arange(128)
        idx[it] = np.where((tpos >= 0) & (tpos < T), tpos, T).astype(np.uint32)
    d["idx_slab"] = idx
    d["idx_own"] = (4096 * j + np.arange(CHUNK).reshape(CHUNK // 128, 128)).astype(np.uint32)
    gb = np.zeros((DEPTH3, 1, 3, D), np.float32)
    gb[:, 0, 0, :] = _np(inp["L1_ffn_ln_g"])
    gb[:, 0, 1, :] = _np(inp["L1_ffn_ln_b"])
    gb[:, 0, 2, :] = _np(inp["L1_ffn_b2"])
    d["ffnrows"] = gb
    return d


# version 02
# speedup vs baseline: 1.0131x; 1.0131x over previous
import numpy as np
import ml_dtypes
import concourse.bass as bass
import concourse.mybir as mybir
from concourse.bass_utils import run_bass_kernel_spmd
from concourse.alu_op_type import AluOpType as OP

F32 = mybir.dt.float32
BF16 = mybir.dt.bfloat16
U32 = mybir.dt.uint32
AF = mybir.ActivationFunctionType
AX = mybir.AxisListType

BF16NP = ml_dtypes.bfloat16

B = 2
T = 16384
D = 1024
CHUNK = 4096
HALO = 2048
SLABT = HALO + CHUNK
NEG = -1e30


ENGINES = ("pe", "dve", "act", "pool", "sp")
QUEUES = ("qsp", "qpool", "qact")
ENG_OF_Q = {"qsp": "sp", "qpool": "pool", "qact": "act"}


class Prog:
    def __init__(self, nc, debug_serial=False):
        self.nc = nc
        self.ops = {e: [] for e in ENGINES}
        self.count = dict.fromkeys(ENGINES, 0)
        self.qcount = dict.fromkeys(QUEUES, 0)
        self.track = {}
        self.waited = {e: {} for e in ENGINES}
        self.debug_serial = debug_serial
        self.act_last_func = None

    def _resolve(self, engine, reads, writes):
        waits = {}

        def add(tok):
            if tok is None:
                return
            sem, v = tok
            if waits.get(sem, 0) < v:
                waits[sem] = v

        for k in reads:
            ent = self.track.get(k)
            if ent:
                add(ent[0])
        for k in writes:
            ent = self.track.get(k)
            if ent:
                add(ent[0])
                for t in ent[1]:
                    add(t)
        out = []
        w = self.waited[engine]
        for sem, v in sorted(waits.items()):
            if w.get(sem, 0) < v:
                out.append((sem, v))
                w[sem] = v
        return out

    def _commit(self, tok, reads, writes):
        for k in writes:
            self.track[k] = [tok, []]
        for k in reads:
            ent = self.track.get(k)
            if ent is None:
                self.track[k] = [None, [tok]]
            else:
                ent[1].append(tok)

    def op(self, engine, emit, reads=(), writes=(), act_func=None):
        waits = self._resolve(engine, reads, writes)
        if engine == "act":
            if (self.act_last_func is not None and act_func != self.act_last_func
                    and self.count["act"] > 0):
                v = self.count["act"]
                if self.waited["act"].get("act", 0) < v:
                    waits.append(("act", v))
                    self.waited["act"]["act"] = v
            self.act_last_func = act_func
        if self.debug_serial and self.count[engine] > 0:
            v = self.count[engine]
            if self.waited[engine].get(engine, 0) < v:
                waits.append((engine, v))
                self.waited[engine][engine] = v
        self.count[engine] += 1
        tok = (engine, self.count[engine])
        self.ops[engine].append((waits, emit, tok, 1))
        self._commit(tok, reads, writes)
        return tok

    def dma(self, queue, emit, reads=(), writes=()):
        eng = ENG_OF_Q[queue]
        waits = self._resolve(eng, reads, writes)
        self.qcount[queue] += 16
        tok = (queue, self.qcount[queue])
        self.ops[eng].append((waits, emit, tok, 16))
        self._commit(tok, reads, writes)
        return tok

    def barrier_all(self, engine="sp"):
        waits = []
        for e in ENGINES:
            if e != engine and self.count[e] > 0:
                waits.append((e, self.count[e]))
        for q in QUEUES:
            if self.qcount[q] > 0:
                waits.append((q, self.qcount[q]))
        w = self.waited[engine]
        waits = [(s, v) for (s, v) in waits if w.get(s, 0) < v]
        for s, v in waits:
            w[s] = v
        self.ops[engine].append((waits, None, None, 0))

    def fence(self):
        snap_e = dict(self.count)
        snap_q = dict(self.qcount)
        for e in ENGINES:
            waits = [(o, snap_e[o]) for o in ENGINES if o != e and snap_e[o] > 0]
            waits += [(q, snap_q[q]) for q in QUEUES if snap_q[q] > 0]
            w = self.waited[e]
            waits = [(s_, v) for (s_, v) in waits if w.get(s_, 0) < v]
            for s_, v in waits:
                w[s_] = v
            if waits:
                self.ops[e].append((waits, None, None, 0))

    def materialize(self, block, sems):
        def run(ename, eng):
            for waits, emit, tok, inc in self.ops[ename]:
                for sem, v in waits:
                    eng.wait_ge(sems[sem], v)
                if emit is None:
                    continue
                inst = emit(eng)
                assert inst is not None, f"op on {ename} returned no instruction"
                inst.then_inc(sems[tok[0]], inc)
            self.ops[ename] = []

        for ename, starter in (("pe", block.tensor), ("dve", block.vector),
                               ("act", block.scalar), ("pool", block.gpsimd),
                               ("sp", block.sync)):
            if self.ops[ename]:
                starter(lambda eng, ename=ename: run(ename, eng))



def mm_group(p, ps_ap, pairs, reads, write_key, accumulate=False):
    def emit(eng):
        inst = None
        n = len(pairs)
        for i, (lhsT, rhs) in enumerate(pairs):
            inst = eng.matmul(ps_ap, lhsT, rhs,
                              start=(i == 0 and not accumulate),
                              stop=(i == n - 1))
        return inst
    return p.op("pe", emit, reads=reads, writes=(write_key,))


def transpose_group(p, items, reads, write_key):
    def emit(eng):
        inst = None
        for dst, src, ident in items:
            inst = eng.transpose(dst, src, ident)
        return inst
    return p.op("pe", emit, reads=reads, writes=(write_key,))


def dve(p, name, *a, reads=(), writes=(), **kw):
    def emit(eng):
        return getattr(eng, name)(*a, **kw)
    return p.op("dve", emit, reads=reads, writes=writes)


def pool(p, name, *a, reads=(), writes=(), **kw):
    def emit(eng):
        return getattr(eng, name)(*a, **kw)
    return p.op("pool", emit, reads=reads, writes=writes)


def act(p, out, in_, func, reads=(), writes=(), **kw):
    def emit(eng):
        return eng.activation(out, in_, func, **kw)
    return p.op("act", emit, reads=reads, writes=writes, act_func=func)


def dma(p, queue, out, in_, reads=(), writes=(), **kw):
    def emit(eng):
        return eng.dma_start(out=out, in_=in_, **kw)
    return p.dma(queue, emit, reads=reads, writes=writes)


def gather(p, out, src, idx_ap, reads=(), writes=()):
    def emit(eng):
        return eng.indirect_dma_start(
            out=out, out_offset=None, in_=src,
            in_offset=bass.IndirectOffsetOnAxis(ap=idx_ap, axis=0))
    return p.dma("qpool", emit, reads=reads, writes=writes)


def scatter(p, dst, dst_idx_ap, src, reads=(), writes=()):
    def emit(eng):
        return eng.indirect_dma_start(
            out=dst, out_offset=bass.IndirectOffsetOnAxis(ap=dst_idx_ap, axis=0),
            in_=src, in_offset=None)
    return p.dma("qpool", emit, reads=reads, writes=writes)



def emit_ln(p, xc_buf, kxc, sq_buf, ksq, stat, kst, x_ap, key_in, outs):
    nmean = stat[:, 0:1]
    ssq = stat[:, 1:2]
    var = stat[:, 2:3]
    rstd = stat[:, 3:4]
    sq = stat[:, 4:5]
    xc = xc_buf[:, 0:D]
    dve(p, "tensor_reduce", nmean, x_ap, axis=AX.X, op=OP.add, negate=True,
        reads=(key_in,), writes=((kst, "nmean"),))
    dve(p, "tensor_scalar", nmean, nmean, 1.0 / D, None, op0=OP.mult,
        reads=((kst, "nmean"),), writes=((kst, "nmean"),))
    dve(p, "tensor_scalar", xc, x_ap, nmean, None, op0=OP.add,
        reads=(key_in, (kst, "nmean")), writes=(kxc,))
    dve(p, "scalar_tensor_tensor", sq_buf[:, 0:D], xc, 1.0, xc,
        op0=OP.mult, op1=OP.mult, accum_out=ssq,
        reads=(kxc,), writes=(ksq, (kst, "ssq")))
    dve(p, "tensor_scalar", var, ssq, 1.0 / D, 1e-5, op0=OP.mult, op1=OP.add,
        reads=((kst, "ssq"),), writes=((kst, "var"),))
    act(p, sq, var, AF.Sqrt, reads=((kst, "var"),), writes=((kst, "sqv"),))
    dve(p, "reciprocal", rstd, sq, reads=((kst, "sqv"),), writes=((kst, "rstd"),))
    for out_ap, out_key in outs:
        dve(p, "tensor_scalar", out_ap, xc, rstd, None, op0=OP.mult,
            reads=(kxc, (kst, "rstd")), writes=(out_key,))


class S:
    pass


P1_NSLAB = None


def build_phase1(nc, p, s, g):
    SLAB = 512
    NT = SLAB // 128
    NSLAB = T // SLAB if P1_NSLAB is None else P1_NSLAB

    for wname, wap in (("wf", g["Wf_t"]), ("wi", g["Wi_t"]), ("wh", g["Wh_t"])):
        dma(p, "qpool", getattr(s, wname)[:],
            wap.rearrange("(k p) e -> p k e", p=128),
            writes=(wname,))
    dma(p, "qsp", s.wr32[:], g["w_router"].rearrange("(k p) e -> p k e", p=128),
        writes=("wr32",))
    for bname, bap in (("bfb", g["b_f"]), ("bib", g["b_i"]), ("bhb", g["b_h"])):
        dma(p, "qsp", getattr(s, bname)[:], bap, writes=(bname,))
    pool(p, "memset", s.zrow_bf[:], 0.0, writes=("zrowbf",))
    dma(p, "qpool", g["xn_dram"][T:T + 1, :], s.zrow_bf[0:1, 0:D],
        reads=("zrowbf",), writes=(("xn_dram", "z"),))
    pool(p, "memset", s.state[:], 0.0, writes=tuple(("state", e) for e in range(8)))

    pre = [s.preF, s.preI, s.preH]
    wsb = [s.wf, s.wi, s.wh]
    wk = ["wf", "wi", "wh"]
    bias = [s.bfb, s.bib, s.bhb]
    biask = ["bfb", "bib", "bhb"]
    kF, kI, kH = "preF", "preI", "preH"
    k1, k2 = "S1", "S2"
    kx, kxn, kxnT = "xsl", "xnsl", "xnT"

    xb_r = g["xb"].rearrange("(n tt p) d -> n p tt d", p=128, tt=NT)
    xn_r = g["xn_dram"][0:T, :].rearrange("(n tt p) d -> n p tt d", p=128, tt=NT)
    pa_r = g["partial_dram"].rearrange("(n tt p) d -> n p tt d", p=128, tt=NT)

    for sl in range(NSLAB):
        dma(p, "qsp", s.xsl[:], xb_r[sl], writes=(kx,))
        for tt in range(NT):
            emit_ln(p, s.lnxc, "lnxc", s.xn32sl[:, tt], "xn32sl",
                    s.stats, "lnst",
                    s.xsl[:, tt], kx, [(s.xn32sl[:, tt], "xn32sl")])
        dma(p, "qpool", xn_r[sl], s.xn32sl[:], reads=("xn32sl",),
            writes=(("xn_dram", sl),))
        for k in range(8):
            items = [(s.pt_f[:, tt * 128:(tt + 1) * 128],
                      s.xn32sl[:, tt, k * 128:(k + 1) * 128], s.ident_f32[:])
                     for tt in range(NT)]
            transpose_group(p, items, reads=("xn32sl", "identf32"),
                            write_key="ptf")
            dve(p, "tensor_copy", s.xnT[:, k], s.pt_f[:, 0:512],
                reads=("ptf",), writes=((kxnT, k),))
            dve(p, "tensor_copy", s.xnT32k[:], s.pt_f[:, 0:512],
                reads=("ptf",), writes=("xnT32k",))
            def emit_rmm(eng, k=k):
                inst = None
                for tt in range(NT):
                    inst = eng.matmul(s.ps_mm[tt][:, 0:8],
                                      s.xnT32k[:, tt * 128:(tt + 1) * 128],
                                      s.wr32[:, k],
                                      start=(k == 0), stop=(k == 7))
                return inst
            p.op("pe", emit_rmm, reads=("xnT32k", "wr32"),
                 writes=tuple(("psmm", tt) for tt in range(NT)))
        rxnT = tuple((kxnT, k) for k in range(8))
        for tt in range(NT):
            dve(p, "tensor_copy",
                s.logits[:, sl * NT + tt, :], s.ps_mm[tt][:, 0:8],
                reads=(("psmm", tt),),
                writes=(("logits", sl * NT + tt),))
        for w in range(3):
            for e in range(8):
                bk = (w * 8 + e) % 4
                bank = s.ps_mm[bk]
                mm_group(p, bank[:],
                         [(wsb[w][:, k, e * 128:(e + 1) * 128], s.xnT[:, k])
                          for k in range(8)],
                         reads=rxnT + (wk[w],), write_key=("psmm", bk))
                dve(p, "tensor_scalar", pre[w][:, e], bank[:],
                    bias[w][:, e:e + 1], None, op0=OP.add,
                    reads=(("psmm", bk), biask[w]),
                    writes=((wk[w] + "pre", e),))
        rF = tuple(("wfpre", e) for e in range(8))
        rI = tuple(("wipre", e) for e in range(8))
        rH = tuple(("whpre", e) for e in range(8))
        if sl == 0 and g.get("dbg_pre") is not None:
            dma(p, "qsp", g["dbg_pre"][0], s.preF[:], reads=rF, writes=(("dbg", 0),))
            dma(p, "qsp", g["dbg_pre"][1], s.preI[:], reads=rI, writes=(("dbg", 1),))
            dma(p, "qsp", g["dbg_pre"][2], s.preH[:], reads=rH, writes=(("dbg", 2),))
        F, I, H = s.preF[:], s.preI[:], s.preH[:]
        S1, S2 = s.sc1[:], s.sc2[:]
        act(p, S1, F, AF.Abs, reads=rF, writes=(k1,))
        act(p, S1, S1, AF.Exp, scale=-1.0, reads=(k1,), writes=(k1,))
        act(p, S1, S1, AF.Ln, bias=1.0, reads=(k1,), writes=(k1,))
        dve(p, "tensor_scalar", F, F, 0.0, None, op0=OP.min,
            reads=rF, writes=rF + (kF,))
        dve(p, "tensor_tensor", S1, S1, F, op=OP.subtract,
            reads=(k1, kF), writes=(k1,))
        act(p, S2, I, AF.Abs, reads=rI, writes=(k2,))
        act(p, S2, S2, AF.Exp, scale=-1.0, reads=(k2,), writes=(k2,))
        act(p, S2, S2, AF.Ln, bias=1.0, reads=(k2,), writes=(k2,))
        dve(p, "tensor_scalar", I, I, 0.0, None, op0=OP.min,
            reads=rI, writes=rI + (kI,))
        dve(p, "tensor_tensor", S2, S2, I, op=OP.subtract,
            reads=(k2, kI), writes=(k2,))
        dve(p, "tensor_tensor", S1, S1, S2, op=OP.subtract,
            reads=(k1, k2), writes=(k1,))
        act(p, F, S1, AF.Sigmoid, scale=-1.0, reads=(k1,), writes=rF + (kF,))
        act(p, S2, S1, AF.Sigmoid, reads=(k1,), writes=(k2,))
        dve(p, "tensor_scalar", S1, H, 0.5, None, op0=OP.add,
            reads=rH, writes=(k1,))
        act(p, H, H, AF.Sigmoid, reads=rH, writes=rH + (kH,))
        dve(p, "tensor_tensor", H, H, S1, op=OP.max,
            reads=(kH, k1), writes=(kH,))
        dve(p, "tensor_tensor", S2, S2, H, op=OP.mult,
            reads=(k2, kH), writes=(k2,))
        for e in range(8):
            dve(p, "tensor_tensor_scan", s.sc1[:, e], s.preF[:, e], s.sc2[:, e],
                s.state[:, e:e + 1], op0=OP.mult, op1=OP.add,
                reads=(kF, k2, ("state", e)), writes=(k1,))
            dve(p, "tensor_copy", s.state[:, e:e + 1], s.sc1[:, e, SLAB - 1:SLAB],
                reads=(k1,), writes=(("state", e),))
        for tt in range(NT):
            for half in range(2):
                items = [(s.pt_f[:, q * 128:(q + 1) * 128],
                          s.sc1[:, half * 4 + q, tt * 128:(tt + 1) * 128],
                          s.ident_f32[:]) for q in range(4)]
                transpose_group(p, items, reads=(k1, "identf32"),
                                write_key="ptf")
                dve(p, "tensor_tensor", s.xsl[:, tt, half * 512:(half + 1) * 512],
                    s.xsl[:, tt, half * 512:(half + 1) * 512], s.pt_f[:],
                    op=OP.add, reads=("ptf", kx), writes=(kx,))
        dma(p, "qsp", pa_r[sl], s.xsl[:], reads=(kx,),
            writes=(("partial", sl),))

    if g.get("xn_out") is not None:
        dma(p, "qsp", g["xn_out"], g["xn_dram"][0:T, :],
            reads=tuple(("xn_dram", sl) for sl in range(NSLAB)),
            writes=(("xnout",),))


def build_x1_copy(nc, p, s, g):
    NT = 4
    NSLAB = T // 512 if P1_NSLAB is None else P1_NSLAB
    pa_r = g["partial_dram"].rearrange("(n tt p) d -> n p tt d", p=128, tt=NT)
    x1_r = g["x1_dram"].rearrange("(n tt p) d -> n p tt d", p=128, tt=NT)
    for sl in range(NSLAB):
        dma(p, "qsp", s.xsl[:], pa_r[sl],
            reads=(("partial", sl),), writes=("xsl",))
        dma(p, "qsp", x1_r[sl], s.xsl[:], reads=("xsl",),
            writes=(("x1", sl),))


NEXP = 8
CAP = 512
NGRP = 4
GT = 32
INVN = 4352
XND_Z = T


def bcast_m(ap2d, n):
    return ap2d.unsqueeze(1).broadcast_to([ap2d.shape[0], n, ap2d.shape[1]])


def bcast_l(ap2d, k):
    return ap2d.unsqueeze(2).broadcast_to([ap2d.shape[0], ap2d.shape[1], k])


def build_moe_routing(nc, p, s, g, gi):
    lt = s.logits[:, GT * gi:GT * (gi + 1), :]
    prb = s.prb[:]
    oh1, oh2 = s.ohb1[:], s.ohb2[:]
    cntb = s.cntb[:]
    r = lambda i: s.rstat[:, :, i]
    rb = lambda i: bcast_l(s.rstat[:, :, i], NEXP)
    kl = tuple(("logits", t) for t in range(GT * gi, GT * (gi + 1)))
    K = lambda *names: tuple(names)

    dve(p, "tensor_reduce", r(0), lt, axis=AX.X, op=OP.max, negate=True,
        reads=kl, writes=("rs0",))
    dve(p, "tensor_tensor", prb, lt, rb(0), op=OP.add,
        reads=kl + ("rs0",), writes=("prb",))
    act(p, prb, prb, AF.Exp, reads=("prb",), writes=("prb",))
    dve(p, "tensor_reduce", r(1), prb, axis=AX.X, op=OP.add,
        reads=("prb",), writes=("rs1",))
    dve(p, "reciprocal", r(1), r(1), reads=("rs1",), writes=("rs1",))
    dve(p, "tensor_tensor", prb, prb, rb(1), op=OP.mult,
        reads=("prb", "rs1"), writes=("prb",))
    dve(p, "tensor_reduce", r(2), prb, axis=AX.X, op=OP.max,
        reads=("prb",), writes=("rs2",))
    dve(p, "tensor_tensor", oh1, prb, rb(2), op=OP.is_equal,
        reads=("prb", "rs2"), writes=("oh1",))
    dve(p, "tensor_tensor", oh1, oh1, bcast_m(s.iota8m8[:], GT), op=OP.mult,
        reads=("oh1", "iota8m8"), writes=("oh1",))
    dve(p, "tensor_scalar", oh1, oh1, 8.0, None, op0=OP.add,
        reads=("oh1",), writes=("oh1",))
    dve(p, "tensor_reduce", r(3), oh1, axis=AX.X, op=OP.min,
        reads=("oh1",), writes=("rs3",))
    dve(p, "tensor_tensor", oh1, bcast_m(s.iota8[:], GT), rb(3), op=OP.is_equal,
        reads=("iota8", "rs3"), writes=("oh1",))
    dve(p, "tensor_copy", s.pbuf[:, gi, :, 0], r(2),
        reads=("rs2",), writes=(("pbuf", gi),))
    dve(p, "scalar_tensor_tensor", prb, oh1, NEG, prb, op0=OP.mult, op1=OP.add,
        reads=("oh1", "prb"), writes=("prb",))
    dve(p, "tensor_reduce", r(4), prb, axis=AX.X, op=OP.max,
        reads=("prb",), writes=("rs4",))
    dve(p, "tensor_tensor", oh2, prb, rb(4), op=OP.is_equal,
        reads=("prb", "rs4"), writes=("oh2",))
    dve(p, "tensor_tensor", oh2, oh2, bcast_m(s.iota8m8[:], GT), op=OP.mult,
        reads=("oh2", "iota8m8"), writes=("oh2",))
    dve(p, "tensor_scalar", oh2, oh2, 8.0, None, op0=OP.add,
        reads=("oh2",), writes=("oh2",))
    dve(p, "tensor_reduce", r(5), oh2, axis=AX.X, op=OP.min,
        reads=("oh2",), writes=("rs5",))
    dve(p, "tensor_tensor", oh2, bcast_m(s.iota8[:], GT), rb(5), op=OP.is_equal,
        reads=("iota8", "rs5"), writes=("oh2",))
    dve(p, "tensor_copy", s.pbuf[:, gi, :, 1], r(4),
        reads=("rs4",), writes=(("pbuf2", gi),))

    for cpass, (ohap, kOH) in enumerate(((oh1, "oh1"), (oh2, "oh2"))):
        pool(p, "memset", s.totrow[:], 0.0, writes=("totrow",))
        for i in range(GT):
            mm_group(p, s.ps_r[:, 0:8],
                     [(s.tri[:], ohap[:, i, :]), (s.onescol[:], s.totrow[:])],
                     reads=(kOH, "tri", "onescol", "totrow"), write_key="psr")
            dve(p, "tensor_copy", cntb[:, i, :], s.ps_r[:, 0:8],
                reads=("psr",), writes=(("cntb", i),))
            mm_group(p, s.pt_f[0:1, 0:8], [(s.ones128[:], ohap[:, i, :])],
                     reads=(kOH, "ones128"), write_key="ptf",
                     accumulate=(i > 0))
            dve(p, "tensor_copy", s.totrow[:], s.pt_f[0:1, 0:8],
                reads=("ptf",), writes=("totrow",))
        if cpass == 0:
            mm_group(p, s.ps_r[:, 0:8], [(s.onescol[:], s.totrow[:])],
                     reads=("totrow", "onescol"), write_key="psr")
            dve(p, "tensor_copy", s.tot1bc[:], s.ps_r[:, 0:8],
                reads=("psr",), writes=("tot1bc",))
            dve(p, "tensor_tensor", s.prb[:], oh1, cntb, op=OP.mult,
                reads=("oh1",) + tuple(("cntb", i) for i in range(GT)),
                writes=("prb",))
            dve(p, "tensor_reduce", s.prib[:, :, 0], s.prb[:], axis=AX.X,
                op=OP.add, reads=("prb",), writes=("pri1",))
        else:
            dve(p, "tensor_tensor", s.prb[:], oh2, cntb, op=OP.mult,
                reads=("oh2",) + tuple(("cntb", i) for i in range(GT)),
                writes=("prb",))
            dve(p, "tensor_reduce", s.prib[:, :, 1], s.prb[:], axis=AX.X,
                op=OP.add, reads=("prb",), writes=("pri2",))
            dve(p, "tensor_tensor", s.prb[:], oh2, bcast_m(s.tot1bc[:], GT),
                op=OP.mult, reads=("oh2", "tot1bc"), writes=("prb",))
            dve(p, "tensor_reduce", s.rstat[:, :, 6], s.prb[:], axis=AX.X,
                op=OP.add, reads=("prb",), writes=("rs6",))
            dve(p, "tensor_tensor", s.prib[:, :, 1], s.prib[:, :, 1],
                s.rstat[:, :, 6], op=OP.add,
                reads=("pri2", "rs6"), writes=("pri2",))

    pool(p, "memset", s.invinit[:], XND_Z, writes=("invinit",))
    dma(p, "qsp",
        g["inv_dram"][INVN * gi:INVN * (gi + 1), :].rearrange("(q a) c -> q a c", a=34),
        s.invinit[:].unsqueeze(2), reads=("invinit",), writes=(("inv", gi),))

    for c, kpri, kidx in ((0, "pri1", "rs3x"), (1, "pri2", "rs5x")):
        idxs = s.rstat[:, :, 3 + 2 * c]
        kidxr = "rs3" if c == 0 else "rs5"
        pri = s.prib[:, :, c]
        t1, t2, t3 = s.rt1[:], s.rt2[:], s.rt3[:]
        dve(p, "tensor_scalar", t1, pri, 512.0, None, op0=OP.is_ge,
            reads=(kpri,), writes=("rt1",))
        dve(p, "tensor_scalar", t2, idxs, 512.0, None, op0=OP.mult,
            reads=(kidxr,), writes=("rt2",))
        dve(p, "tensor_tensor", t2, t2, pri, op=OP.add,
            reads=("rt2", kpri), writes=("rt2",))
        dve(p, "tensor_scalar", t3, t2, -1.0, 4096.0, op0=OP.mult, op1=OP.add,
            reads=("rt2",), writes=("rt3",))
        dve(p, "tensor_tensor", t3, t3, t1, op=OP.mult,
            reads=("rt3", "rt1"), writes=("rt3",))
        dve(p, "tensor_tensor", t3, t3, t2, op=OP.add,
            reads=("rt3", "rt2"), writes=("rt3",))
        dve(p, "tensor_copy", s.slotc[:, gi, :, c], t3,
            reads=("rt3",), writes=(("slotc", gi, c),))
        dve(p, "tensor_scalar", t3,
            s.pidx[:, 0:1].broadcast_to([128, GT]), 1.0, 4200.0,
            op0=OP.mult, op1=OP.add, reads=("pidx",), writes=("rt3",))
        dve(p, "tensor_tensor", t3, t3, t2, op=OP.subtract,
            reads=("rt3", "rt2"), writes=("rt3",))
        dve(p, "tensor_tensor", t3, t3, t1, op=OP.mult,
            reads=("rt3", "rt1"), writes=("rt3",))
        dve(p, "tensor_tensor", t3, t3, t2, op=OP.add,
            reads=("rt3", "rt2"), writes=("rt3",))
        dve(p, "tensor_scalar", t3, t3, float(INVN * gi), None, op0=OP.add,
            reads=("rt3",), writes=("rt3",))
        dve(p, "tensor_copy", s.sslotu[:, :, c], t3,
            reads=("rt3",), writes=(("sslotu", c),))
    for i in range(GT):
        for c in range(2):
            scatter(p, g["inv_dram"], s.sslotu[:, i, c:c + 1],
                    s.tokidx[:, GT * gi + i:GT * gi + i + 1],
                    reads=(("sslotu", c), "tokidx", ("inv", gi)),
                    writes=(("inv", gi),))


def build_moe_experts(nc, p, s, g):
    wi_r = [g["moe_wi"][e].rearrange("(k p) f -> p k f", p=128) for e in range(NEXP)]
    wo_r = [g["moe_wo"][e].rearrange("(k p) h -> p k h", p=128) for e in range(NEXP)]
    for e in range(NEXP):
        dma(p, "qpool", s.wo_sb[:], wo_r[e], writes=("wo_sb",))
        for gi in range(NGRP):
            for q in range(4):
                a = INVN * gi + e * CAP + q * 128
                dma(p, "qsp", s.idxu[:], g["inv_dram"][a:a + 128, :],
                    reads=(("inv", gi),), writes=("idxu",))
                gather(p, s.ein[:], g["xn_dram"], s.idxu[:, 0:1],
                       reads=("idxu",) + tuple(("xn_dram", sl) for sl in range(32))
                       + (("xn_dram", "z"),),
                       writes=("ein",))
                items = [(s.pt_b[:, k * 128:(k + 1) * 128],
                          s.ein[:, k * 128:(k + 1) * 128], s.ident_bf[:])
                         for k in range(8)]
                transpose_group(p, items, reads=("ein", "identbf"),
                                write_key="ptb")
                dve(p, "tensor_copy", s.einT[:, :, q * 128:(q + 1) * 128],
                    s.pt_b[:].rearrange("p (k x) -> p k x", k=8),
                    reads=("ptb",), writes=(("einT", q),))
            reinT = tuple(("einT", q) for q in range(4))
            for fk in range(8):
                wb = s.wi_ch[fk % 2]
                kwb = ("wi_ch", fk % 2)
                dma(p, "qpool", wb[:], wi_r[e][:, :, fk * 512:(fk + 1) * 512],
                    writes=(kwb,))
                for m in range(4):
                    bk = m % 4
                    mm_group(p, s.ps_mm[bk][:],
                             [(wb[:, k, m * 128:(m + 1) * 128], s.einT[:, k])
                              for k in range(8)],
                             reads=reinT + (kwb,), write_key=("psmm", bk))
                    act(p, s.h1T[:, fk * 4 + m], s.ps_mm[bk][:],
                        AF.Gelu_apprx_tanh,
                        reads=(("psmm", bk),), writes=(("h1T", fk * 4 + m),))
            rh1 = tuple(("h1T", i) for i in range(32))
            for n in range(2):
                for m in range(4):
                    bk = m
                    mm_group(p, s.ps_mm[bk][:],
                             [(s.h1T[:, k, m * 128:(m + 1) * 128],
                               s.wo_sb[:, k, n * 512:(n + 1) * 512])
                              for k in range(32)],
                             reads=rh1 + ("wo_sb",), write_key=("psmm", bk))
                    dve(p, "tensor_copy", s.stg[:], s.ps_mm[bk][:],
                        reads=(("psmm", bk),), writes=("stg",))
                    dma(p, "qsp",
                        g["out2_dram"][gi][e * CAP + m * 128:e * CAP + (m + 1) * 128,
                                           n * 512:(n + 1) * 512],
                        s.stg[:], reads=("stg",), writes=(("out2", gi, e, n, m),))


def build_moe_combine(nc, p, s, g):
    pa_t = g["partial_dram"].rearrange("(t p) d -> t p d", p=128)
    x1_t = g["x1_dram"].rearrange("(t p) d -> t p d", p=128)
    for gi in range(NGRP):
        pool(p, "memset", s.zrow_f[:], 0.0, writes=("zrowf",))
        dma(p, "qpool", g["out2_dram"][gi][NEXP * CAP:NEXP * CAP + 1, :],
            s.zrow_f[0:1, :], reads=("zrowf",), writes=(("out2z", gi),))
        rout2 = tuple(("out2", gi, e, n, m)
                      for e in range(NEXP) for n in range(2) for m in range(4))
        for i in range(GT):
            gt = GT * gi + i
            dma(p, "qsp", s.part[:], pa_t[gt], reads=(("partial", gt // 4),),
                writes=("part",))
            gather(p, s.g1t[:], g["out2_dram"][gi], s.slotc[:, gi, i, 0:1],
                   reads=(("slotc", gi, 0), ("out2z", gi)) + rout2,
                   writes=("g1t",))
            gather(p, s.g2t[:], g["out2_dram"][gi], s.slotc[:, gi, i, 1:2],
                   reads=(("slotc", gi, 1), ("out2z", gi)) + rout2,
                   writes=("g2t",))
            dve(p, "tensor_scalar", s.g1t[:], s.g1t[:], s.pbuf[:, gi, i, 0:1],
                None, op0=OP.mult,
                reads=("g1t", ("pbuf", gi)), writes=("g1t",))
            dve(p, "scalar_tensor_tensor", s.g1t[:], s.g2t[:],
                s.pbuf[:, gi, i, 1:2], s.g1t[:], op0=OP.mult, op1=OP.add,
                reads=("g2t", ("pbuf2", gi), "g1t"), writes=("g1t",))
            dve(p, "tensor_tensor", s.part[:], s.part[:], s.g1t[:], op=OP.add,
                reads=("part", "g1t"), writes=("part",))
            dma(p, "qsp", x1_t[gt], s.part[:], reads=("part",),
                writes=(("x1", gt),))


def build_moe(nc, p, s, g):
    dma(p, "qsp", s.tri[:], g["tri_f32"], writes=("tri",))
    dma(p, "qsp", s.iota8[:], g["iota8"], writes=("iota8",))
    dma(p, "qsp", s.iota8m8[:], g["iota8m8"], writes=("iota8m8",))
    dma(p, "qsp", s.onescol[:], g["onescol"], writes=("onescol",))
    dma(p, "qsp", s.ones128[:], g["ones128"], writes=("ones128",))
    dma(p, "qsp", s.pidx[:], g["pidx"], writes=("pidx",))
    dma(p, "qsp", s.tokidx[:], g["tokidx"], writes=("tokidx",))
    for gi in range(NGRP):
        build_moe_routing(nc, p, s, g, gi)
    build_moe_experts(nc, p, s, g)
    build_moe_combine(nc, p, s, g)



def _mk(nc, stack, prefix):
    def sb(name, shape, dt):
        return stack.enter_context(nc.sbuf_tensor(prefix + name, shape, dt))
    def psb(name, shape, dt):
        return stack.enter_context(nc.psum_tensor(prefix + name, shape, dt))
    return sb, psb


def alloc_persist(nc, stack, s):
    sb, psb = _mk(nc, stack, "sp_")
    s.ident_bf = sb("ident_bf", [128, 128], BF16)
    s.ident_f32 = sb("ident_f32", [128, 128], F32)
    s.logits = sb("logits", [128, 128, 8], F32)
    s.onescol32 = sb("onescol32", [1, 128], F32)
    s.ps_mm = [psb(f"ps_mm{i}", [128, 512], F32) for i in range(4)]
    s.ps_r = psb("ps_r", [128, 512], F32)
    s.pt_b = psb("pt_b", [128, 1024], BF16)
    s.pt_f = psb("pt_f", [128, 512], F32)
    return s


def alloc_phase1(nc, stack, s):
    sb, _ = _mk(nc, stack, "s1_")
    s.zrow_bf = sb("zrow_bf", [128, D], BF16)
    s.wf = sb("wf", [128, 8, D], BF16)
    s.wi = sb("wi", [128, 8, D], BF16)
    s.wh = sb("wh", [128, 8, D], BF16)
    s.wr32 = sb("wr32", [128, 8, 8], F32)
    s.bfb = sb("bfb", [128, 8], F32)
    s.bib = sb("bib", [128, 8], F32)
    s.bhb = sb("bhb", [128, 8], F32)
    s.xsl = sb("xsl", [128, 4, D], F32)
    s.xn32sl = sb("xn32sl", [128, 4, D], F32)
    s.lnxc = sb("lnxc", [128, D], F32)
    s.xnT32k = sb("xnT32k", [128, 512], F32)
    s.stats = sb("stats", [128, 8], F32)
    s.xnT = sb("xnT", [128, 8, 512], BF16)
    s.preF = sb("preF", [128, 8, 512], F32)
    s.preI = sb("preI", [128, 8, 512], F32)
    s.preH = sb("preH", [128, 8, 512], F32)
    s.sc1 = sb("sc1", [128, 8, 512], F32)
    s.sc2 = sb("sc2", [128, 8, 512], F32)
    s.state = sb("state", [128, 8], F32)
    return s


def alloc_moe(nc, stack, s):
    sb, _ = _mk(nc, stack, "sm_")
    s.tri = sb("tri", [128, 128], F32)
    s.iota8 = sb("iota8", [128, 8], F32)
    s.iota8m8 = sb("iota8m8", [128, 8], F32)
    s.onescol = sb("onescol", [1, 128], F32)
    s.ones128 = sb("ones128", [128, 1], F32)
    s.pidx = sb("pidx", [128, 1], F32)
    s.tokidx = sb("tokidx", [128, 128], U32)
    s.prb = sb("prb", [128, GT, 8], F32)
    s.ohb1 = sb("ohb1", [128, GT, 8], F32)
    s.ohb2 = sb("ohb2", [128, GT, 8], F32)
    s.cntb = sb("cntb", [128, GT, 8], F32)
    s.rstat = sb("rstat", [128, GT, 8], F32)
    s.prib = sb("prib", [128, GT, 2], F32)
    s.rt1 = sb("rt1", [128, GT], F32)
    s.rt2 = sb("rt2", [128, GT], F32)
    s.rt3 = sb("rt3", [128, GT], F32)
    s.totrow = sb("totrow", [1, 8], F32)
    s.tot1bc = sb("tot1bc", [128, 8], F32)
    s.invinit = sb("invinit", [128, 34], U32)
    s.sslotu = sb("sslotu", [128, GT, 2], U32)
    s.slotc = sb("slotc", [128, NGRP, GT, 2], U32)
    s.pbuf = sb("pbuf", [128, NGRP, GT, 2], F32)
    s.wo_sb = sb("wo_sb", [128, 32, 1024], BF16)
    s.wi_ch = [sb(f"wi_ch{i}", [128, 8, 512], BF16) for i in range(2)]
    s.ein = sb("ein", [128, 1024], BF16)
    s.einT = sb("einT", [128, 8, 512], BF16)
    s.h1T = sb("h1T", [128, 32, 512], BF16)
    s.idxu = sb("idxu", [128, 1], U32)
    s.stg = sb("stg", [128, 512], F32)
    s.g1t = sb("g1t", [128, 1024], F32)
    s.g2t = sb("g2t", [128, 1024], F32)
    s.part = sb("part", [128, 1024], F32)
    s.zrow_f = sb("zrow_f", [128, 1024], F32)
    return s



def _np(x):
    return np.asarray(x)


def declare_p1_dram(nc, g, external_x1=True):
    g["xb"] = nc.dram_tensor("xb", [T, D], F32, kind="ExternalInput").ap()
    g["Wf_t"] = nc.dram_tensor("Wf_t", [D, D], F32, kind="ExternalInput").ap()
    g["Wi_t"] = nc.dram_tensor("Wi_t", [D, D], F32, kind="ExternalInput").ap()
    g["Wh_t"] = nc.dram_tensor("Wh_t", [D, D], F32, kind="ExternalInput").ap()
    g["b_f"] = nc.dram_tensor("b_f", [128, 8], F32, kind="ExternalInput").ap()
    g["b_i"] = nc.dram_tensor("b_i", [128, 8], F32, kind="ExternalInput").ap()
    g["b_h"] = nc.dram_tensor("b_h", [128, 8], F32, kind="ExternalInput").ap()
    g["w_router"] = nc.dram_tensor("w_router", [D, 8], F32, kind="ExternalInput").ap()
    g["ident_bf"] = nc.dram_tensor("ident_bf", [128, 128], BF16, kind="ExternalInput").ap()
    g["ident_f32"] = nc.dram_tensor("ident_f32", [128, 128], F32, kind="ExternalInput").ap()
    g["moe_wi"] = nc.dram_tensor("moe_wi", [NEXP, D, 4096], F32, kind="ExternalInput").ap()
    g["moe_wo"] = nc.dram_tensor("moe_wo", [NEXP, 4096, D], F32, kind="ExternalInput").ap()
    g["tri_f32"] = nc.dram_tensor("tri_f32", [128, 128], F32, kind="ExternalInput").ap()
    g["iota8"] = nc.dram_tensor("iota8", [128, 8], F32, kind="ExternalInput").ap()
    g["iota8m8"] = nc.dram_tensor("iota8m8", [128, 8], F32, kind="ExternalInput").ap()
    g["onescol"] = nc.dram_tensor("onescol", [1, 128], F32, kind="ExternalInput").ap()
    g["ones128"] = nc.dram_tensor("ones128", [128, 1], F32, kind="ExternalInput").ap()
    g["pidx"] = nc.dram_tensor("pidx", [128, 1], F32, kind="ExternalInput").ap()
    g["tokidx"] = nc.dram_tensor("tokidx", [128, 128], U32, kind="ExternalInput").ap()
    g["xn_dram"] = nc.dram_tensor("xn_dram", [T + 128, D], BF16).ap()
    g["partial_dram"] = nc.dram_tensor("partial_dram", [T, D], F32).ap()
    g["inv_dram"] = nc.dram_tensor("inv_dram", [NGRP * INVN, 1], U32).ap()
    g["out2_dram"] = [nc.dram_tensor(f"out2_g{gi}", [NEXP * CAP + 1, D], F32).ap()
                      for gi in range(NGRP)]
    if external_x1:
        g["x1_dram"] = nc.dram_tensor("x1_out", [T, D], F32, kind="ExternalOutput").ap()
    else:
        g["x1_dram"] = nc.dram_tensor("x1_dram", [T, D], F32).ap()


def build_p1_program(with_moe=False, debug=False):
    import contextlib
    nc = bass.Bass("TRN2", target_bir_lowering=False)
    g = {}
    declare_p1_dram(nc, g, external_x1=True)
    if debug:
        g["dbg_pre"] = nc.dram_tensor("dbg_pre", [3, 128, 8, 512], F32, kind="ExternalOutput").ap()
        g["xn_out"] = nc.dram_tensor("xn_out", [T, D], BF16, kind="ExternalOutput").ap()

    with contextlib.ExitStack() as stack:
        s = S()
        alloc_persist(nc, stack, s)
        sems = {}
        for name in ENGINES + QUEUES:
            sems[name] = stack.enter_context(nc.semaphore(name))
        p = Prog(nc)
        with nc.Block() as block:
            dma(p, "qsp", s.ident_bf[:], g["ident_bf"], writes=("identbf",))
            dma(p, "qsp", s.ident_f32[:], g["ident_f32"], writes=("identf32",))
            with contextlib.ExitStack() as st1:
                alloc_phase1(nc, st1, s)
                build_phase1(nc, p, s, g)
                p.fence()
                p.materialize(block, sems)
            if with_moe:
                with contextlib.ExitStack() as st2:
                    alloc_moe(nc, st2, s)
                    build_moe(nc, p, s, g)
                    p.fence()
                    p.materialize(block, sems)
            else:
                with contextlib.ExitStack() as st2:
                    alloc_phase1(nc, st2, s)
                    build_x1_copy(nc, p, s, g)
                    p.materialize(block, sems)
            p.barrier_all()
            p.materialize(block, sems)
    return nc


def prep_p1_inputs(x_batch, inp):
    d = {}
    d["xb"] = np.ascontiguousarray(x_batch)
    d["Wf_t"] = np.ascontiguousarray(_np(inp["L0_attn_W_f"]).T)
    d["Wi_t"] = np.ascontiguousarray(_np(inp["L0_attn_W_i"]).T)
    d["Wh_t"] = np.ascontiguousarray(_np(inp["L0_attn_W_h"]).T)
    d["b_f"] = np.ascontiguousarray(_np(inp["L0_attn_b_f"]).reshape(8, 128).T)
    d["b_i"] = np.ascontiguousarray(_np(inp["L0_attn_b_i"]).reshape(8, 128).T)
    d["b_h"] = np.ascontiguousarray(_np(inp["L0_attn_b_h"]).reshape(8, 128).T)
    d["w_router"] = np.ascontiguousarray(_np(inp["L0_ffn_w_router"]))
    d["ident_bf"] = np.eye(128, dtype=BF16NP)
    d["ident_f32"] = np.eye(128, dtype=np.float32)
    d["moe_wi"] = np.ascontiguousarray(_np(inp["L0_ffn_wi"]))
    d["moe_wo"] = np.ascontiguousarray(_np(inp["L0_ffn_wo"]))
    d["tri_f32"] = np.triu(np.ones((128, 128), np.float32), 1)
    io8 = np.broadcast_to(np.arange(8, dtype=np.float32), (128, 8)).copy()
    d["iota8"] = io8
    d["iota8m8"] = io8 - 8.0
    d["onescol"] = np.ones((1, 128), np.float32)
    d["ones128"] = np.ones((128, 1), np.float32)
    d["pidx"] = np.arange(128, dtype=np.float32)[:, None].copy()
    tk = (np.arange(128, dtype=np.uint32)[None, :] * 128
          + np.arange(128, dtype=np.uint32)[:, None])
    d["tokidx"] = np.ascontiguousarray(tk)
    return d


_PROG_CACHE = {}


def kernel(**inputs):
    if FUSED:
        return kernel_fused(**inputs)
    x = np.ascontiguousarray(np.asarray(inputs["L0_attn_x"], dtype=np.float32))
    if "p1" not in _PROG_CACHE:
        _PROG_CACHE["p1"] = build_p1_program(with_moe=True)
        _PROG_CACHE["p2"] = build_p2_program()
        _PROG_CACHE["p3"] = build_p3_program()
    d1 = [prep_p1_inputs(x[b], inputs) for b in range(B)]
    res1 = run_bass_kernel_spmd(_PROG_CACHE["p1"], [d1[c // 4] for c in range(8)],
                                core_ids=list(range(8)))
    x1 = [np.asarray(res1.results[4 * b]["x1_out"]) for b in range(B)]
    d2 = [prep_p2_inputs(x1[c // 4], inputs, c % 4) for c in range(8)]
    res2 = run_bass_kernel_spmd(_PROG_CACHE["p2"], d2, core_ids=list(range(8)))
    xn1 = [np.asarray(res2.results[4 * b]["xn1_out"]) for b in range(B)]
    attnbr = [np.asarray(res2.results[c]["attnbr_out"]) for c in range(8)]
    d3 = []
    for c in range(8):
        b, j = c // 4, c % 4
        d = prep_p3_inputs(inputs, j)
        d["xn1_in"] = np.concatenate([xn1[b], np.zeros((128, D), np.float32)], 0)
        d["x1_in"] = x1[b]
        d["attnbr_in"] = attnbr[c]
        d3.append(d)
    res3 = run_bass_kernel_spmd(_PROG_CACHE["p3"], d3, core_ids=list(range(8)))
    out = np.empty((B, T, D), np.float32)
    for c in range(8):
        b, j = c // 4, c % 4
        out[b, CHUNK * j:CHUNK * (j + 1)] = res3.results[c]["out"]
    return out



GAMMA = 0.99
BAND = 4096
VSL = 256


def alloc_phase2a(nc, stack, s):
    sb, _ = _mk(nc, stack, "s2a_")
    s.wq = sb("wq", [128, 8, D], BF16)
    s.wk = sb("wk", [128, 8, D], BF16)
    s.wv = sb("wv", [128, 8, VSL], BF16)
    s.x1sl = sb("x1sl", [128, 4, D], F32)
    s.xn1sl = sb("xn1sl", [128, 4, D], F32)
    s.lnxc2 = sb("lnxc2", [128, D], F32)
    s.stats2 = sb("stats2", [128, 8], F32)
    s.xnT2 = sb("xnT2", [128, 8, 512], BF16)
    s.qstg = sb("qstg", [128, 512], BF16)
    s.kstg = sb("kstg", [128, 512], BF16)
    s.zrow2 = sb("zrow2", [128, D], F32)


def alloc_phase2b(nc, stack, s):
    sb, _ = _mk(nc, stack, "s2b_")
    s.qt = sb("qt", [128, 8, 512], BF16)
    s.kt = [sb(f"kt{i}", [128, 8, 128], BF16) for i in range(2)]
    s.abuf = sb("abuf", [128, 36, 512], BF16)
    s.e0 = sb("e0", [128, 512], F32)
    s.em = sb("em", [128, 4, 512], F32)
    s.ostg = sb("ostg", [128, 512], F32)


def alloc_vbuf(nc, stack, s):
    sb, _ = _mk(nc, stack, "s2v_")
    s.vbuf = sb("vbuf", [128, 128, VSL], BF16)


def build_phase2a(nc, p, s, g):
    NT = 4
    NSLAB = T // 512 if P1_NSLAB is None else P1_NSLAB
    dma(p, "qpool", s.wq[:], g["W_Q"].rearrange("(k p) c -> p k c", p=128),
        writes=("wq",))
    dma(p, "qpool", s.wk[:], g["W_K"].rearrange("(k p) c -> p k c", p=128),
        writes=("wk",))
    dma(p, "qpool", s.wv[:], g["W_Vs"].rearrange("(k p) c -> p k c", p=128),
        writes=("wv",))
    pool(p, "memset", s.zrow2[:], 0.0, writes=("zrow2",))
    dma(p, "qpool", g["xn1_dram"][T:T + 1, :], s.zrow2[0:1, 0:D],
        reads=("zrow2",), writes=(("xn1_dram", "z"),))

    x1_r = g["x1_dram"].rearrange("(n tt p) d -> n p tt d", p=128, tt=NT)
    xn1_r = g["xn1_dram"][0:T, :].rearrange("(n tt p) d -> n p tt d", p=128, tt=NT)
    qt_r = g["qt_dram"].rearrange("(m p) t -> p m t", p=128)
    kt_r = g["kt_dram"].rearrange("(m p) t -> p m t", p=128)

    for sl in range(NSLAB):
        t0 = sl * 512
        dma(p, "qsp", s.x1sl[:], x1_r[sl],
            reads=(("x1", 4 * sl), ("x1", 4 * sl + 1), ("x1", 4 * sl + 2),
                   ("x1", 4 * sl + 3)),
            writes=("x1sl",))
        for tt in range(NT):
            emit_ln(p, s.lnxc2, "lnxc2", s.xn1sl[:, tt], "xn1sl",
                    s.stats2, "lnst2",
                    s.x1sl[:, tt], "x1sl", [(s.xn1sl[:, tt], "xn1sl")])
        dma(p, "qsp", xn1_r[sl], s.xn1sl[:], reads=("xn1sl",),
            writes=(("xn1_dram", sl),))
        for k in range(8):
            items = [(s.pt_f[:, tt * 128:(tt + 1) * 128],
                      s.xn1sl[:, tt, k * 128:(k + 1) * 128], s.ident_f32[:])
                     for tt in range(NT)]
            transpose_group(p, items, reads=("xn1sl", "identf32"),
                            write_key="ptf")
            dve(p, "tensor_copy", s.xnT2[:, k], s.pt_f[:, 0:512],
                reads=("ptf",), writes=(("xnT2", k),))
        rxnT = tuple(("xnT2", k) for k in range(8))
        for which, wsb_, kw, stg, kstg, dr in (
                (0, s.wq, "wq", s.qstg, "qstg", qt_r),
                (1, s.wk, "wk", s.kstg, "kstg", kt_r)):
            for m in range(8):
                bk = m % 2 if which == 0 else 2 + m % 2
                mm_group(p, s.ps_mm[bk][:],
                         [(wsb_[:, k, m * 128:(m + 1) * 128], s.xnT2[:, k])
                          for k in range(8)],
                         reads=rxnT + (kw,), write_key=("psmm", bk))
                dve(p, "tensor_copy", stg[:], s.ps_mm[bk][:],
                    reads=(("psmm", bk),), writes=(kstg,))
                dma(p, "qsp", dr[:, m, t0:t0 + 512], stg[:],
                    reads=(kstg,), writes=((("qtd" if which == 0 else "ktd"), sl, m),))
        for tt in range(NT):
            gt = sl * NT + tt
            mm_group(p, s.ps_r[:, 0:VSL],
                     [(s.xnT2[:, k, tt * 128:(tt + 1) * 128], s.wv[:, k])
                      for k in range(8)],
                     reads=rxnT + ("wv",), write_key="psr")
            dve(p, "tensor_copy", s.vbuf[:, gt, :], s.ps_r[:, 0:VSL],
                reads=("psr",), writes=(("vbuf", gt),))


def _decay_tiles():
    i = np.arange(128, dtype=np.float64)[:, None]
    jj = np.arange(512, dtype=np.float64)[None, :]
    e0 = GAMMA ** (jj - i)
    ems = []
    for delta in (0, -128, -256, -384):
        ex = delta + jj - i
        em = np.where(ex >= 0, GAMMA ** ex, 0.0)
        ems.append(em)
    return (e0.astype(np.float32),
            np.stack(ems, 1).astype(np.float32))


def build_phase2b(nc, p, s, g):
    NSLAB = T // 512 if P1_NSLAB is None else P1_NSLAB
    NQ = NSLAB
    dma(p, "qsp", s.e0[:], g["e0_t"], writes=("e0",))
    dma(p, "qsp", s.em[:], g["em_t"], writes=("em",))
    qt_r = g["qt_dram"].rearrange("(m p) t -> p m t", p=128)
    kt_r = g["kt_dram"].rearrange("(m p) t -> p m t", p=128)
    ot_r = g["outT_dram"].rearrange("(m p) t -> p m t", p=128)

    for q in range(NQ):
        tq0 = 512 * q
        dma(p, "qsp", s.qt[:], qt_r[:, :, tq0:tq0 + 512], 
            reads=tuple(("qtd", sl, m) for sl in range(NSLAB) for m in range(8)),
            writes=("qt",))
        s_tiles = list(range(max(0, tq0 - BAND), tq0 + 512, 128))
        for si, s0 in enumerate(s_tiles):
            kb = si % 2
            dma(p, "qsp", s.kt[kb][:], kt_r[:, :, s0:s0 + 128],
                reads=tuple(("ktd", sl, m) for sl in range(NSLAB) for m in range(8)),
                writes=(("kt", kb),))
            bk = si % 2
            mm_group(p, s.ps_mm[bk][:],
                     [(s.kt[kb][:, k], s.qt[:, k]) for k in range(8)],
                     reads=(("kt", kb), "qt"), write_key=("psmm", bk))
            delta = tq0 - s0
            if delta >= 128:
                dve(p, "scalar_tensor_tensor", s.abuf[:, si, :],
                    s.ps_mm[bk][:], float(GAMMA ** delta), s.e0[:],
                    op0=OP.mult, op1=OP.mult,
                    reads=(("psmm", bk), "e0"), writes=(("abuf", si),))
            else:
                mi = (-delta) // 128
                dve(p, "tensor_tensor", s.abuf[:, si, :],
                    s.ps_mm[bk][:], s.em[:, mi, :], op=OP.mult,
                    reads=(("psmm", bk), "em"), writes=(("abuf", si),))
        if q == 0 and g.get("abuf_out") is not None:
            dma(p, "qsp", g["abuf_out"], s.abuf[:],
                reads=tuple(("abuf", si) for si in range(len(s_tiles))),
                writes=("abufout",))
        for mi in range(2):
            bank = s.ps_mm[2 + mi]
            mm_group(p, bank[:],
                     [(s.vbuf[:, s0 // 128, mi * 128:(mi + 1) * 128],
                       s.abuf[:, si, :])
                      for si, s0 in enumerate(s_tiles)],
                     reads=tuple(("abuf", si) for si in range(len(s_tiles)))
                     + tuple(("vbuf", s0 // 128) for s0 in s_tiles),
                     write_key=("psmm", 2 + mi))
            dve(p, "tensor_copy", s.ostg[:], bank[:],
                reads=(("psmm", 2 + mi),), writes=("ostg",))
            dma(p, "qsp", ot_r[:, mi, tq0:tq0 + 512], s.ostg[:],
                reads=("ostg",), writes=(("otd", q, mi),))


def build_scramble(nc, p, s, g):
    rot = tuple(("otd", q, mi) for q in range(T // 512) for mi in range(2))
    for m in range(16):
        src = g["outT_dram"][0:256, 1024 * m:1024 * (m + 1)]
        dst = g["attnbr_dram"].rearrange("(u s) d -> s u d", s=16)[m]
        dma(p, "qpool", dst, src, reads=rot, writes=(("attnbr", m),))


def declare_p2_dram(nc, g, x1_external):
    if x1_external:
        g["x1_dram"] = nc.dram_tensor("x1_in", [T, D], F32, kind="ExternalInput").ap()
    g["W_Q"] = nc.dram_tensor("W_Q", [D, D], F32, kind="ExternalInput").ap()
    g["W_K"] = nc.dram_tensor("W_K", [D, D], F32, kind="ExternalInput").ap()
    g["W_Vs"] = nc.dram_tensor("W_Vs", [D, VSL], F32, kind="ExternalInput").ap()
    g["e0_t"] = nc.dram_tensor("e0_t", [128, 512], F32, kind="ExternalInput").ap()
    g["em_t"] = nc.dram_tensor("em_t", [128, 4, 512], F32, kind="ExternalInput").ap()
    g["xn1_dram"] = nc.dram_tensor("xn1_dram", [T + 128, D], F32).ap()
    g["qt_dram"] = nc.dram_tensor("qt_dram", [D, T], BF16).ap()
    g["kt_dram"] = nc.dram_tensor("kt_dram", [D, T], BF16).ap()
    g["outT_dram"] = nc.dram_tensor("outT_dram", [VSL, T], F32).ap()


def build_p2_program(debug=False):
    import contextlib
    nc = bass.Bass("TRN2", target_bir_lowering=False)
    g = {}
    g["ident_bf"] = nc.dram_tensor("ident_bf", [128, 128], BF16, kind="ExternalInput").ap()
    g["ident_f32"] = nc.dram_tensor("ident_f32", [128, 128], F32, kind="ExternalInput").ap()
    declare_p2_dram(nc, g, x1_external=True)
    g["attnbr_dram"] = nc.dram_tensor("attnbr_out", [CHUNK, D], F32,
                                      kind="ExternalOutput").ap()
    g["xn1_out"] = nc.dram_tensor("xn1_out", [T, D], F32,
                                  kind="ExternalOutput").ap()
    if debug:
        g["outT_out"] = nc.dram_tensor("outT_out", [VSL, T], F32,
                                       kind="ExternalOutput").ap()
        g["abuf_out"] = nc.dram_tensor("abuf_out", [128, 36, 512], BF16,
                                       kind="ExternalOutput").ap()
        g["qt_out"] = nc.dram_tensor("qt_out", [D, T], BF16, kind="ExternalOutput").ap()
        g["kt_out"] = nc.dram_tensor("kt_out", [D, T], BF16, kind="ExternalOutput").ap()

    with contextlib.ExitStack() as stack:
        s = S()
        alloc_persist(nc, stack, s)
        alloc_vbuf(nc, stack, s)
        sems = {}
        for name in ENGINES + QUEUES:
            sems[name] = stack.enter_context(nc.semaphore(name))
        p = Prog(nc)
        for gt in range(T // 128):
            p.track[("x1", gt)] = [None, []]
        with nc.Block() as block:
            dma(p, "qsp", s.ident_bf[:], g["ident_bf"], writes=("identbf",))
            dma(p, "qsp", s.ident_f32[:], g["ident_f32"], writes=("identf32",))
            with contextlib.ExitStack() as st1:
                alloc_phase2a(nc, st1, s)
                build_phase2a(nc, p, s, g)
                p.fence()
                p.materialize(block, sems)
            with contextlib.ExitStack() as st2:
                alloc_phase2b(nc, st2, s)
                build_phase2b(nc, p, s, g)
                p.fence()
                p.materialize(block, sems)
            build_scramble(nc, p, s, g)
            if g.get("qt_out") is not None:
                dma(p, "qsp", g["qt_out"], g["qt_dram"],
                    reads=tuple(("qtd", sl, m) for sl in range(T // 512) for m in range(8)),
                    writes=("qtout",))
                dma(p, "qsp", g["kt_out"], g["kt_dram"],
                    reads=tuple(("ktd", sl, m) for sl in range(T // 512) for m in range(8)),
                    writes=("ktout",))
            if g.get("outT_out") is not None:
                dma(p, "qsp", g["outT_out"], g["outT_dram"],
                    reads=tuple(("otd", q, mi) for q in range(T // 512)
                                for mi in range(2)),
                    writes=("outTout",))
            dma(p, "qsp", g["xn1_out"], g["xn1_dram"][0:T, :],
                reads=tuple(("xn1_dram", sl) for sl in range(T // 512)),
                writes=("xn1out",))
            p.barrier_all()
            p.materialize(block, sems)
    return nc


def prep_p2_inputs(x1, inp, j):
    d = {}
    d["x1_in"] = np.ascontiguousarray(x1)
    d["W_Q"] = np.ascontiguousarray(_np(inp["L1_attn_W_Q"]))
    d["W_K"] = np.ascontiguousarray(_np(inp["L1_attn_W_K"]))
    d["W_Vs"] = np.ascontiguousarray(_np(inp["L1_attn_W_V"])[:, VSL * j:VSL * (j + 1)])
    e0, em = _decay_tiles()
    d["e0_t"] = e0
    d["em_t"] = np.ascontiguousarray(em)
    d["ident_bf"] = np.eye(128, dtype=BF16NP)
    d["ident_f32"] = np.eye(128, dtype=np.float32)
    return d



DEPTH3 = 6
HEADS = 16
DH = 64
WIN = 256
NWIN = SLABT // WIN
NTI = SLABT // 128
DIN_P = 2816
QK_SCALE = 8.0
P3_PARANOID = False


def alloc_phase3(nc, stack, s):
    sb, _ = _mk(nc, stack, "s3_")
    s.wbig = sb("wbig", [128, 24576], BF16)
    s.wsm = sb("wsm", [128, 8192], BF16)
    s.wqkv = s.wbig[:].rearrange("p (k c) -> p k c", k=8)
    s.w2p = s.wbig[:, 0:22528].rearrange("p (k c) -> p k c", k=22)
    s.wo3 = s.wsm[:].rearrange("p (k c) -> p k c", k=8)
    s.w1ch = [s.wsm[:, 0:4096].rearrange("p (k c) -> p k c", k=8),
              s.wsm[:, 4096:8192].rearrange("p (k c) -> p k c", k=8)]
    s.xt3 = sb("xt3", [128, 1024], F32)
    s.xt3b = sb("xt3b", [128, 1024], BF16)
    s.xd3 = sb("xd3", [128, 1024], F32)
    s.lnxc3 = sb("lnxc3", [128, D], F32)
    s.stats3 = sb("stats3", [128, 8], F32)
    s.xnT3 = sb("xnT3", [128, 8, 128], BF16)
    s.qkv = sb("qkv", [128, 3072], F32)
    s.u8k = sb("u8k", [128, 4096], BF16)
    s.qkb = s.u8k[:, 0:2048]
    s.hs = sb("hs", [128, 24], F32)
    s.rot = sb("rot", [128, 4, 64], F32)
    s.qsk = sb("qsk", [128, 2, 64], F32)
    s.qT = s.u8k[:, 2048:4096].rearrange("p (k c) -> p k c", k=8)
    s.uAB = sb("uAB", [128, 12288], BF16)
    s.kT = [s.uAB[:, 0:2048].rearrange("p (k c) -> p k c", k=8),
            s.uAB[:, 2048:4096].rearrange("p (k c) -> p k c", k=8)]
    s.vb = [s.uAB[:, 4096:6144].rearrange("p (k c) -> p k c", k=2),
            s.uAB[:, 6144:8192].rearrange("p (k c) -> p k c", k=2)]
    s.mks = sb("mks", [128, 2, 2, 512], BF16)
    s.mkw = sb("mkw", [128, 2, 512], BF16)
    s.selv = sb("selv", [128, NWIN], F32)
    s.hkb = sb("hkb", [128, NTI], F32)
    s.attnb = sb("attnb", [128, 512], BF16)
    s.attnT = sb("attnT", [128, 4, 128], BF16)
    s.attno = sb("attno", [128, 1024], BF16)
    s.headsT = s.uAB[:, 8192:12288].rearrange("p (k c) -> p k c", k=8)
    s.wst = sb("wst", [128, 8, 512], F32)
    s.h2T = sb("h2T", [128, 22, 512], BF16)
    s.sga = sb("sga", [128, 512], F32)
    s.sgg = sb("sgg", [128, 512], F32)
    s.gbt = s.uAB[:, 0:4096].bitcast(F32).rearrange("p (k c) -> p k c", k=2)
    s.b2t = s.uAB[:, 4096:6144].bitcast(F32)
    s.b1t = sb("b1t", [128, 44], F32)
    s.rowst = s.uAB[0:1, 6144:12288].bitcast(F32).rearrange("p (k c) -> p k c", k=3)
    s.xnT512 = s.u8k[:].rearrange("p (k c) -> p k c", k=8)
    s.att1 = sb("att1", [128, 1024], F32)
    s.att2 = sb("att2", [128, 1024], F32)
    s.idx3 = sb("idx3", [128, 1], U32)
    s.x1t = sb("x1t", [128, 1024], F32)
    s.abt = sb("abt", [128, 1024], F32)


def _shifted_tile(p, s, g, src, gen, it, dstbuf, dstkey):
    r0 = 1 + it * 128
    k0 = ("xc", gen, it)
    km = ("xc", gen, it - 1) if it > 0 else ("xcz", gen % 2)
    dma(p, "qsp", dstbuf[:, 0:512], src[r0:r0 + 128, 0:512],
        reads=(k0,), writes=(dstkey,))
    dma(p, "qsp", dstbuf[:, 512:1024], src[r0 - 1:r0 + 127, 512:1024],
        reads=(k0, km), writes=(dstkey,))


def _residual_writeback(p, s, g, dst, gen_out, t4, branch_stage, kstage,
                        src, gen_in):
    for q in range(4):
        it = t4 + q
        r0 = 1 + it * 128
        dma(p, "qsp", s.xd3[:], src[r0:r0 + 128, :],
            reads=(("xc", gen_in, it),), writes=("xd3",))
        for half in range(2):
            items = [(s.pt_f[:, u * 128:(u + 1) * 128],
                      branch_stage[:, half * 4 + u, q * 128:(q + 1) * 128],
                      s.ident_f32[:]) for u in range(4)]
            transpose_group(p, items, reads=(kstage, "identf32"),
                            write_key="ptf")
            dve(p, "tensor_tensor", s.xd3[:, half * 512:(half + 1) * 512],
                s.xd3[:, half * 512:(half + 1) * 512], s.pt_f[:],
                op=OP.add, reads=("ptf", "xd3"), writes=("xd3",))
        dve(p, "tensor_scalar", s.xd3[:], s.xd3[:], s.hkb[:, it:it + 1],
            None, op0=OP.mult, reads=("xd3", "hkb"), writes=("xd3",))
        dma(p, "qsp", dst[r0:r0 + 128, :], s.xd3[:], reads=("xd3",),
            writes=(("xc", gen_out, it),))


def build_p3_attn(nc, p, s, g, li, src, dst):
    gen = 2 * li
    dma(p, "qpool", s.wqkv,
        g["wqkv3"][li].rearrange("(k p) c -> p k c", p=128), writes=("wbig",))
    dma(p, "qpool", s.wo3,
        g["wo3_w"][li].rearrange("(k p) c -> p k c", p=128),
        writes=(("wsm", 0), ("wsm", 1)))
    dma(p, "qsp", s.qsk[:], g["qsk3"][li], writes=("qsk",))
    pool(p, "memset", s.kT[1][:], 0.0, writes=(("kTr", 1),))
    pool(p, "memset", s.vb[1][:], 0.0, writes=(("vbr", 1),))

    for w in range(NWIN):
        cur = w % 2
        prv = 1 - cur
        for par in range(2):
            dve(p, "scalar_tensor_tensor", s.mkw[:, par, :],
                s.mks[:, 1, par, :], s.selv[:, w:w + 1], s.mks[:, 0, par, :],
                op0=OP.mult, op1=OP.add,
                reads=("mks", "selv"), writes=("mkw",))
        for sub in range(2):
            it = 2 * w + sub
            _shifted_tile(p, s, g, src, gen, it, s.xt3, "xt3")
            dve(p, "tensor_copy", s.xt3b[:], s.xt3[:], reads=("xt3",),
                writes=("xt3b",))
            items = [(s.pt_b[:, k * 128:(k + 1) * 128],
                      s.xt3b[:, k * 128:(k + 1) * 128], s.ident_bf[:])
                     for k in range(8)]
            transpose_group(p, items, reads=("xt3b", "identbf"),
                            write_key="ptb")
            dve(p, "tensor_copy", s.xnT3[:],
                s.pt_b[:].rearrange("p (k x) -> p k x", k=8),
                reads=("ptb",), writes=("xnT3",))
            for nch in range(6):
                bk = nch % 4
                mm_group(p, s.ps_mm[bk][:],
                         [(s.xnT3[:, k], s.wqkv[:, k, nch * 512:(nch + 1) * 512])
                          for k in range(8)],
                         reads=("xnT3", "wbig"), write_key=("psmm", bk))
                dve(p, "tensor_copy", s.qkv[:, nch * 512:(nch + 1) * 512],
                    s.ps_mm[bk][:], reads=(("psmm", bk),), writes=("qkv",))
            dma(p, "qsp", s.rot[:], g["rot3"][it], writes=("rot",))
            for which, off, sci, ssi in ((0, 0, 0, 1), (1, 1024, 2, 3)):
                qv = s.qkv[:, off:off + 1024]
                hv = qv.rearrange("p (h d) -> p h d", d=DH)
                dve(p, "scalar_tensor_tensor", s.att1[:], qv, 1.0, qv,
                    op0=OP.mult, op1=OP.mult, reads=("qkv",), writes=("att1",))
                dve(p, "tensor_reduce", s.hs[:, 8:24],
                    s.att1[:].rearrange("p (h d) -> p h d", d=DH),
                    axis=AX.X, op=OP.add, reads=("att1",), writes=("hsn",))
                act(p, s.hs[:, 8:24], s.hs[:, 8:24], AF.Sqrt,
                    reads=("hsn",), writes=("hsn",))
                dve(p, "tensor_scalar", s.hs[:, 8:24], s.hs[:, 8:24], 1e-12,
                    None, op0=OP.max, reads=("hsn",), writes=("hsn",))
                dve(p, "reciprocal", s.hs[:, 8:24], s.hs[:, 8:24],
                    reads=("hsn",), writes=("hsn",))
                dve(p, "tensor_tensor", hv, hv,
                    s.hs[:, 8:24].unsqueeze(2).broadcast_to([128, 16, DH]),
                    op=OP.mult, reads=("qkv", "hsn"), writes=("qkv",))
                dve(p, "tensor_tensor", hv, hv,
                    s.qsk[:, which, :].unsqueeze(1).broadcast_to([128, 16, DH]),
                    op=OP.mult, reads=("qkv", "qsk"), writes=("qkv",))
                cs = s.rot[:, sci, :]
                ss = s.rot[:, ssi, :]
                a1 = s.att1[:].rearrange("p (h d) -> p h d", d=DH)
                h1 = s.att2[:].rearrange("p (h d) -> p h d", d=DH)
                dve(p, "tensor_tensor", a1, hv,
                    cs.unsqueeze(1).broadcast_to([128, 16, DH]),
                    op=OP.mult, reads=("qkv", "rot"), writes=("att1",))
                dve(p, "tensor_tensor", h1[:, :, 0:32], hv[:, :, 32:64],
                    ss[:, 0:32].unsqueeze(1).broadcast_to([128, 16, 32]),
                    op=OP.mult, reads=("qkv", "rot"), writes=("att2",))
                dve(p, "tensor_tensor", h1[:, :, 32:64], hv[:, :, 0:32],
                    ss[:, 32:64].unsqueeze(1).broadcast_to([128, 16, 32]),
                    op=OP.mult, reads=("qkv", "rot"), writes=("att2",))
                dve(p, "tensor_tensor", hv[:, :, 0:32], a1[:, :, 0:32],
                    h1[:, :, 0:32], op=OP.subtract,
                    reads=("att1", "att2"), writes=("qkv",))
                dve(p, "tensor_tensor", hv[:, :, 32:64], a1[:, :, 32:64],
                    h1[:, :, 32:64], op=OP.add,
                    reads=("att1", "att2"), writes=("qkv",))
            if li == 0 and it == 4 and g.get("dbgq") is not None:
                dma(p, "qsp", g["dbgq"], s.qkv[:], reads=("qkv",),
                    writes=("dbgq",))
            dve(p, "tensor_copy", s.vb[cur][:, sub, :],
                s.qkv[:, 2048:3072], reads=("qkv",), writes=(("vbr", cur),))
            dve(p, "tensor_copy", s.qkb, s.qkv[:, 0:2048],
                reads=("qkv",), writes=("qkb",))
            for which, dstT, kdst in ((0, s.qT, "qT"), (1, s.kT[cur], ("kTr", cur))):
                items = [(s.pt_b[:, b * 128:(b + 1) * 128],
                          s.qkb[:, which * 1024 + b * 128:which * 1024 + (b + 1) * 128],
                          s.ident_bf[:]) for b in range(8)]
                transpose_group(p, items, reads=("qkb", "identbf"),
                                write_key="ptb")
                dve(p, "tensor_copy", dstT[:, :, sub * 128:(sub + 1) * 128],
                    s.pt_b[:].rearrange("p (b x) -> p b x", b=8),
                    reads=("ptb",), writes=(kdst,))
        for tt in range(2):
            for h in range(HEADS):
                blk = h // 2
                pb = 64 * (h % 2)
                bk = tt
                def emit_sc(eng, blk=blk, pb=pb, tt=tt, bk=bk, cur=cur, prv=prv):
                    eng.matmul(s.ps_mm[bk][:, 0:256],
                               s.qT[:, blk, tt * 128:(tt + 1) * 128][pb:pb + 64, :],
                               s.kT[prv][:, blk, :][pb:pb + 64, :],
                               start=True, stop=False)
                    eng.matmul(s.ps_mm[bk][:, 256:512],
                               s.qT[:, blk, tt * 128:(tt + 1) * 128][pb:pb + 64, :],
                               s.kT[cur][:, blk, :][pb:pb + 64, :],
                               start=False, stop=False)
                    return eng.matmul(s.ps_mm[bk][:],
                                      s.ident_bf[:], s.mkw[:, tt, :],
                                      start=False, stop=True)
                p.op("pe", emit_sc,
                     reads=("qT", ("kTr", prv), ("kTr", cur), "identbf", "mkw"),
                     writes=(("psmm", bk),))
                if li == 0 and w == 2 and h == 0 and tt == 0 and g.get("dbgs") is not None:
                    dve(p, "tensor_copy", s.wst[:, 0, :], s.ps_mm[bk][:],
                        reads=(("psmm", bk),), writes=("wst",))
                    dma(p, "qsp", g["dbgs"], s.wst[:, 0, :], reads=("wst",),
                        writes=("dbgs",))
                dve(p, "tensor_reduce", s.hs[:, 0:1], s.ps_mm[bk][:],
                    axis=AX.X, op=OP.max, negate=True,
                    reads=(("psmm", bk),), writes=("hs",))
                dve(p, "tensor_scalar", s.hs[:, 1:2], s.hs[:, 0:1],
                    QK_SCALE, None, op0=OP.mult, reads=("hs",), writes=("hs",))
                act(p, s.attnb[:], s.ps_mm[bk][:], AF.Exp,
                    bias=s.hs[:, 1:2], scale=QK_SCALE, accum_out=s.hs[:, 2:3],
                    reads=(("psmm", bk), "hs"), writes=("attnb", "hs2"))
                dve(p, "reciprocal", s.hs[:, 3:4], s.hs[:, 2:3],
                    reads=("hs2",), writes=("hs2",))
                dve(p, "tensor_scalar", s.attnb[:], s.attnb[:], s.hs[:, 3:4],
                    None, op0=OP.mult, reads=("attnb", "hs2"),
                    writes=("attnb",))
                items = [(s.pt_b[:, u * 128:(u + 1) * 128],
                          s.attnb[:, u * 128:(u + 1) * 128], s.ident_bf[:])
                         for u in range(4)]
                transpose_group(p, items, reads=("attnb", "identbf"),
                                write_key="ptb")
                dve(p, "tensor_copy", s.attnT[:],
                    s.pt_b[:, 0:512].rearrange("p (a x) -> p a x", a=4),
                    reads=("ptb",), writes=("attnT",))
                avbk = 2 + h // 8
                def emit_av(eng, h=h, avbk=avbk, cur=cur, prv=prv):
                    inst = None
                    c0 = 64 * (h % 8)
                    for a in range(4):
                        vtile = s.vb[prv if a < 2 else cur][:, a % 2,
                                                            h * 64:(h + 1) * 64]
                        inst = eng.matmul(
                            s.ps_mm[avbk][:, c0:c0 + 64],
                            s.attnT[:, a, :], vtile,
                            start=(h % 8 == 0 and a == 0),
                            stop=(h % 8 == 7 and a == 3))
                    return inst
                p.op("pe", emit_av,
                     reads=("attnT", ("vbr", prv), ("vbr", cur)),
                     writes=(("psmm", avbk),))
            for bb in range(2):
                dve(p, "tensor_copy", s.qkb[:, bb * 512:(bb + 1) * 512].bitcast(BF16)[:, 0:512] if False else s.attno[:, bb * 512:(bb + 1) * 512],
                    s.ps_mm[2 + bb][:], reads=(("psmm", 2 + bb),),
                    writes=("attno",))
            items = [(s.pt_b[:, u * 128:(u + 1) * 128],
                      s.attno[:, u * 128:(u + 1) * 128], s.ident_bf[:])
                     for u in range(8)]
            transpose_group(p, items, reads=("attno", "identbf"),
                            write_key="ptb")
            dve(p, "tensor_copy",
                s.headsT[:, :, cur * 256 + tt * 128:cur * 256 + tt * 128 + 128],
                s.pt_b[:].rearrange("p (k x) -> p k x", k=8),
                reads=("ptb",), writes=("headsT",))
        if w % 2 == 1:
            for m in range(8):
                bk = m % 4
                mm_group(p, s.ps_mm[bk][:],
                         [(s.wo3[:, k, m * 128:(m + 1) * 128], s.headsT[:, k, :])
                          for k in range(8)],
                         reads=("headsT", ("wsm", 0), ("wsm", 1)), write_key=("psmm", bk))
                dve(p, "tensor_copy", s.wst[:, m, :], s.ps_mm[bk][:],
                    reads=(("psmm", bk),), writes=("wst",))
            _residual_writeback(p, s, g, dst, gen + 1, 2 * (w - 1),
                                s.wst[:], "wst", src, gen)


def bcast_row(p, s, dst_ap, dst_key, row_ap, row_key, width=512):
    def emit1(eng):
        return eng.matmul(s.ps_r[:, 0:width], s.onescol32[:],
                          row_ap, start=True, stop=True)
    p.op("pe", emit1, reads=(row_key, "onescol32"), writes=("psr",))
    dve(p, "tensor_copy", dst_ap, s.ps_r[:, 0:width],
        reads=("psr",), writes=(dst_key,))


def build_p3_ffn(nc, p, s, g, li, src, dst):
    gen = 2 * li + 1
    dma(p, "qpool", s.w2p,
        g["w2p3"][li].rearrange("(k p) c -> p k c", p=128), writes=("wbig",))
    dma(p, "qsp", s.rowst[:], g["ffnrows"][li], writes=("rowst",))
    dma(p, "qsp", s.b1t[:], g["b1_3"][li], writes=("b1t",))
    for ci in range(2):
        for ri, (dstap, dk) in enumerate(((s.gbt[:, 0, :], "gbt"),
                                          (s.gbt[:, 1, :], "gbt"),
                                          (s.b2t[:, :], "b2t"))):
            bcast_row(p, s, dstap[:, ci * 512:(ci + 1) * 512], dk,
                      s.rowst[0:1, ri, ci * 512:(ci + 1) * 512], "rowst")

    w1_r = g["w1p3"][li].rearrange("(k p) c -> p k c", p=128)
    for t4 in range(0, NTI, 4):
        for q in range(4):
            it = t4 + q
            _shifted_tile(p, s, g, src, gen, it, s.xt3, "xt3")
            emit_ln(p, s.lnxc3, "lnxc3", s.att1, "att1", s.stats3, "lnst3",
                    s.xt3[:], "xt3", [(s.xt3[:], "xt3")])
            dve(p, "tensor_tensor", s.xt3[:], s.xt3[:], s.gbt[:, 0, :],
                op=OP.mult, reads=("xt3", "gbt"), writes=("xt3",))
            dve(p, "tensor_tensor", s.xt3[:], s.xt3[:], s.gbt[:, 1, :],
                op=OP.add, reads=("xt3", "gbt"), writes=("xt3",))
            dve(p, "tensor_copy", s.xt3b[:], s.xt3[:], reads=("xt3",),
                writes=("xt3b",))
            items = [(s.pt_b[:, k * 128:(k + 1) * 128],
                      s.xt3b[:, k * 128:(k + 1) * 128], s.ident_bf[:])
                     for k in range(8)]
            transpose_group(p, items, reads=("xt3b", "identbf"),
                            write_key="ptb")
            dve(p, "tensor_copy", s.xnT512[:, :, q * 128:(q + 1) * 128],
                s.pt_b[:].rearrange("p (k x) -> p k x", k=8),
                reads=("ptb",), writes=("qkb", "qT"))
        for ci in range(11):
            wb = s.w1ch[ci % 2]
            kwb = ("wsm", ci % 2)
            dma(p, "qpool", wb, w1_r[:, :, ci * 512:(ci + 1) * 512],
                writes=(kwb,))
            for u2 in range(2):
                u = 2 * ci + u2
                ba, bg = 2 * u2, 2 * u2 + 1
                mm_group(p, s.ps_mm[ba][:],
                         [(wb[:, k, u2 * 256:u2 * 256 + 128], s.xnT512[:, k, :])
                          for k in range(8)],
                         reads=(kwb, "qkb", "qT"), write_key=("psmm", ba))
                mm_group(p, s.ps_mm[bg][:],
                         [(wb[:, k, u2 * 256 + 128:u2 * 256 + 256], s.xnT512[:, k, :])
                          for k in range(8)],
                         reads=(kwb, "qkb", "qT"), write_key=("psmm", bg))
                dve(p, "tensor_scalar", s.sga[:], s.ps_mm[ba][:],
                    s.b1t[:, u:u + 1], None, op0=OP.add,
                    reads=(("psmm", ba), "b1t"), writes=("sga",))
                dve(p, "tensor_scalar", s.sgg[:], s.ps_mm[bg][:],
                    s.b1t[:, 22 + u:23 + u], None, op0=OP.add,
                    reads=(("psmm", bg), "b1t"), writes=("sgg",))
                act(p, s.sgg[:], s.sgg[:], AF.Gelu, reads=("sgg",),
                    writes=("sgg",))
                dve(p, "tensor_tensor", s.h2T[:, u, :], s.sga[:], s.sgg[:],
                    op=OP.mult, reads=("sga", "sgg"), writes=(("h2T", u),))
        for q in range(4):
            it = t4 + q
            for n in range(2):
                bk = n
                mm_group(p, s.ps_mm[bk][:],
                         [(s.h2T[:, k, q * 128:(q + 1) * 128],
                           s.w2p[:, k, n * 512:(n + 1) * 512])
                          for k in range(22)],
                         reads=tuple(("h2T", k) for k in range(22)) + ("wbig",),
                         write_key=("psmm", bk))
            r0 = 1 + it * 128
            dma(p, "qsp", s.xd3[:], src[r0:r0 + 128, :],
                reads=(("xc", gen, it),), writes=("xd3",))
            for n in range(2):
                dve(p, "tensor_tensor", s.xd3[:, n * 512:(n + 1) * 512],
                    s.xd3[:, n * 512:(n + 1) * 512], s.ps_mm[n][:],
                    op=OP.add, reads=(("psmm", n), "xd3"), writes=("xd3",))
            dve(p, "tensor_tensor", s.xd3[:], s.xd3[:], s.b2t[:],
                op=OP.add, reads=("xd3", "b2t"), writes=("xd3",))
            dve(p, "tensor_scalar", s.xd3[:], s.xd3[:], s.hkb[:, it:it + 1],
                None, op0=OP.mult, reads=("xd3", "hkb"), writes=("xd3",))
            dma(p, "qsp", dst[r0:r0 + 128, :], s.xd3[:], reads=("xd3",),
                writes=(("xc", gen + 1, it),))


def build_phase3(nc, p, s, g):
    xcb = [g["xc_dram"], g["xd_dram"]]
    dma(p, "qsp", s.mks[:], g["mask3"], writes=("mks",))
    dma(p, "qsp", s.selv[:], g["selv"], writes=("selv",))
    dma(p, "qsp", s.hkb[:], g["hkb"], writes=("hkb",))
    pool(p, "memset", s.xt3[:], 0.0, writes=("xt3",))
    for buf in range(2):
        dma(p, "qpool", xcb[buf][0:1, :], s.xt3[0:1, 0:D], reads=("xt3",),
            writes=(("xcz", buf),))
    for it in range(NTI):
        dma(p, "qsp", s.idx3[:], g["idx_slab"][it].unsqueeze(1), writes=("idx3",))
        gather(p, s.xt3[:], g["xn1_dram"], s.idx3[:, 0:1],
               reads=("idx3", ("xn1_dram", "z"))
               + tuple(("xn1_dram", sl) for sl in range(T // 512)),
               writes=("xt3",))
        dve(p, "tensor_scalar", s.xt3[:], s.xt3[:], s.hkb[:, it:it + 1],
            None, op0=OP.mult, reads=("xt3", "hkb"), writes=("xt3",))
        dma(p, "qsp", xcb[0][1 + it * 128:1 + (it + 1) * 128, :], s.xt3[:],
            reads=("xt3",), writes=(("xc", 0, it),))
    if g.get("dbg3") is not None:
        dma(p, "qpool", g["dbg3"][0], xcb[0][:, :],
            reads=tuple(("xc", 0, it) for it in range(NTI)), writes=("d30",))
    for li in range(DEPTH3):
        build_p3_attn(nc, p, s, g, li, xcb[0], xcb[1])
        p.fence()
        if li == 0 and g.get("dbg3") is not None:
            dma(p, "qpool", g["dbg3"][1], xcb[1][:, :],
                reads=tuple(("xc", 1, it) for it in range(NTI)), writes=("d31",))
        build_p3_ffn(nc, p, s, g, li, xcb[1], xcb[0])
        p.fence()
        if li == 0 and g.get("dbg3") is not None:
            dma(p, "qpool", g["dbg3"][2], xcb[0][:, :],
                reads=tuple(("xc", 2, it) for it in range(NTI)), writes=("d32",))


def build_final(nc, p, s, g):
    for it in range(CHUNK // 128):
        dma(p, "qsp", s.idx3[:], g["idx_own"][it].unsqueeze(1), writes=("idx3",))
        gather(p, s.x1t[:], g["x1_dram"], s.idx3[:, 0:1],
               reads=("idx3",) + tuple(("x1", gt) for gt in range(T // 128)),
               writes=("x1t",))
        dma(p, "qsp", s.abt[:],
            g["attnbr_dram"][it * 128:(it + 1) * 128, :],
            reads=tuple(("attnbr", m) for m in range(16)), writes=("abt",))
        r0 = 1 + (HALO // 128 + it) * 128
        dma(p, "qsp", s.xd3[:], g["xc_dram"][r0:r0 + 128, :],
            reads=(("xc", 2 * DEPTH3, HALO // 128 + it),), writes=("xd3",))
        dve(p, "tensor_tensor", s.x1t[:], s.x1t[:], s.abt[:], op=OP.add,
            reads=("x1t", "abt"), writes=("x1t",))
        dve(p, "tensor_tensor", s.x1t[:], s.x1t[:], s.xd3[:], op=OP.add,
            reads=("x1t", "xd3"), writes=("x1t",))
        dma(p, "qsp", g["out_dram"][it * 128:(it + 1) * 128, :], s.x1t[:],
            reads=("x1t",), writes=(("out", it),))


def declare_p3_dram(nc, g, xn1_external):
    if xn1_external:
        g["xn1_dram"] = nc.dram_tensor("xn1_in", [T + 128, D], F32,
                                       kind="ExternalInput").ap()
    g["wqkv3"] = nc.dram_tensor("wqkv3", [DEPTH3, D, 3072], F32, kind="ExternalInput").ap()
    g["wo3_w"] = nc.dram_tensor("wo3_w", [DEPTH3, D, D], F32, kind="ExternalInput").ap()
    g["w1p3"] = nc.dram_tensor("w1p3", [DEPTH3, D, 2 * DIN_P], F32, kind="ExternalInput").ap()
    g["w2p3"] = nc.dram_tensor("w2p3", [DEPTH3, DIN_P, D], F32, kind="ExternalInput").ap()
    g["qsk3"] = nc.dram_tensor("qsk3", [DEPTH3, 128, 2, DH], F32, kind="ExternalInput").ap()
    g["rot3"] = nc.dram_tensor("rot3", [NTI, 128, 4, DH], F32, kind="ExternalInput").ap()
    g["mask3"] = nc.dram_tensor("mask3", [128, 2, 2, 512], BF16, kind="ExternalInput").ap()
    g["selv"] = nc.dram_tensor("selv", [128, NWIN], F32, kind="ExternalInput").ap()
    g["hkb"] = nc.dram_tensor("hkb", [128, NTI], F32, kind="ExternalInput").ap()
    g["idx_slab"] = nc.dram_tensor("idx_slab", [NTI, 128], U32, kind="ExternalInput").ap()
    g["idx_own"] = nc.dram_tensor("idx_own", [CHUNK // 128, 128], U32, kind="ExternalInput").ap()
    g["ffnrows"] = nc.dram_tensor("ffnrows", [DEPTH3, 1, 3, D], F32, kind="ExternalInput").ap()
    g["b1_3"] = nc.dram_tensor("b1_3", [DEPTH3, 128, 44], F32, kind="ExternalInput").ap()
    g["xc_dram"] = nc.dram_tensor("xc_dram", [1 + SLABT, D], F32).ap()
    g["xd_dram"] = nc.dram_tensor("xd_dram", [1 + SLABT, D], F32).ap()


def build_p3_program(debug=False):
    import contextlib
    nc = bass.Bass("TRN2", target_bir_lowering=False)
    g = {}
    g["ident_bf"] = nc.dram_tensor("ident_bf", [128, 128], BF16, kind="ExternalInput").ap()
    g["ident_f32"] = nc.dram_tensor("ident_f32", [128, 128], F32, kind="ExternalInput").ap()
    g["onescol"] = nc.dram_tensor("onescol", [1, 128], F32, kind="ExternalInput").ap()
    declare_p3_dram(nc, g, xn1_external=True)
    g["x1_dram"] = nc.dram_tensor("x1_in", [T, D], F32, kind="ExternalInput").ap()
    g["attnbr_dram"] = nc.dram_tensor("attnbr_in", [CHUNK, D], F32, kind="ExternalInput").ap()
    g["out_dram"] = nc.dram_tensor("out", [CHUNK, D], F32, kind="ExternalOutput").ap()
    if debug:
        g["dbg3"] = nc.dram_tensor("dbg3", [3, 1 + SLABT, D], F32, kind="ExternalOutput").ap()
        g["dbgq"] = nc.dram_tensor("dbgq", [128, 3072], F32, kind="ExternalOutput").ap()
        g["dbgh"] = nc.dram_tensor("dbgh", [128, 8, 256], BF16, kind="ExternalOutput").ap()
        g["dbgs"] = nc.dram_tensor("dbgs", [128, 512], F32, kind="ExternalOutput").ap()
        g["dbga"] = nc.dram_tensor("dbga", [128, 512], BF16, kind="ExternalOutput").ap()
        g["dbgo"] = nc.dram_tensor("dbgo", [128, 256], F32, kind="ExternalOutput").ap()
        g["dbgv"] = nc.dram_tensor("dbgv", [2, 128, 2, 1024], BF16, kind="ExternalOutput").ap()

    with contextlib.ExitStack() as stack:
        s = S()
        alloc_persist(nc, stack, s)
        sems = {}
        for name in ENGINES + QUEUES:
            sems[name] = stack.enter_context(nc.semaphore(name))
        p = Prog(nc)
        with nc.Block() as block:
            dma(p, "qsp", s.ident_bf[:], g["ident_bf"], writes=("identbf",))
            dma(p, "qsp", s.ident_f32[:], g["ident_f32"], writes=("identf32",))
            dma(p, "qsp", s.onescol32[:], g["onescol"], writes=("onescol32",))
            with contextlib.ExitStack() as st1:
                alloc_phase3(nc, st1, s)
                build_phase3(nc, p, s, g)
                build_final(nc, p, s, g)
                p.fence()
                p.materialize(block, sems)
            p.barrier_all()
            p.materialize(block, sems)
    return nc


def _rotary_tables_np():
    n = T
    half = np.arange(0, DH, 2, dtype=np.float64)
    inv_freq = 1.0 / 10000.0 ** (half / DH)
    t = np.arange(n, dtype=np.float64)
    freqs = t[:, None] * inv_freq[None, :]
    freqs = np.concatenate([freqs, freqs], -1)
    base_scale = (half + 0.4 * DH) / (1.4 * DH)
    power = (t - n // 2) / (WIN // 2)
    scale = base_scale[None, :] ** power[:, None]
    scale = np.concatenate([scale, scale], -1)
    cos, sin = np.cos(freqs), np.sin(freqs)
    out = np.stack([cos * scale, sin * scale, cos / scale, sin / scale], 1)
    return out.astype(np.float32)


def _mask_tiles_np():
    out = np.zeros((128, 2, 2, 512), np.float32)
    pq = np.arange(128)
    jj = np.arange(512)
    for par in range(2):
        tq = 128 * par + pq
        allowed = (jj[None, :] >= tq[:, None]) & (jj[None, :] <= tq[:, None] + 256)
        out[:, 0, par, :] = np.where(allowed, 0.0, NEG)
        out[:, 1, par, :] = np.where(jj[None, :] < 256, NEG, 0.0)
    return out.astype(BF16NP)


def prep_p3_inputs(inp, j):
    d = {}
    d["ident_bf"] = np.eye(128, dtype=BF16NP)
    d["ident_f32"] = np.eye(128, dtype=np.float32)
    d["onescol"] = np.ones((1, 128), np.float32)
    d["wqkv3"] = np.ascontiguousarray(_np(inp["L1_ffn_Wqkv"]))
    d["wo3_w"] = np.ascontiguousarray(_np(inp["L1_ffn_Wo"]))
    W1 = _np(inp["L1_ffn_W1"])
    b1 = _np(inp["L1_ffn_b1"])
    W2 = _np(inp["L1_ffn_W2"])
    W1q = np.zeros((DEPTH3, D, 2 * DIN_P), np.float32)
    b1q = np.zeros((DEPTH3, 128, 44), np.float32)
    for u in range(22):
        c0, c1 = u * 128, min(2730, (u + 1) * 128)
        w = c1 - c0
        W1q[:, :, 256 * u:256 * u + w] = W1[:, :, c0:c1]
        W1q[:, :, 256 * u + 128:256 * u + 128 + w] = W1[:, :, 2730 + c0:2730 + c1]
        b1q[:, 0:w, u] = b1[:, c0:c1].T.reshape(w, DEPTH3).T.T if False else 0
        b1q[:, :, u][:, 0:w] = b1[:, c0:c1]
        b1q[:, :, 22 + u][:, 0:w] = b1[:, 2730 + c0:2730 + c1]
    d["w1p3"] = W1q
    d["b1_3"] = np.ascontiguousarray(np.transpose(b1q, (0, 2, 1))) if False else b1q
    W2q = np.zeros((DEPTH3, DIN_P, D), np.float32)
    W2q[:, 0:2730, :] = 0
    for u in range(22):
        c0, c1 = u * 128, min(2730, (u + 1) * 128)
        W2q[:, u * 128:u * 128 + (c1 - c0), :] = W2[:, c0:c1, :]
    d["w2p3"] = W2q
    qs = _np(inp["L1_ffn_q_scale"]).astype(np.float32)
    ks = _np(inp["L1_ffn_k_scale"]).astype(np.float32)
    qsk = np.zeros((DEPTH3, 128, 2, DH), np.float32)
    qsk[:, :, 0, :] = qs[:, None, :]
    qsk[:, :, 1, :] = ks[:, None, :]
    d["qsk3"] = qsk
    rot = _rotary_tables_np()
    start = 4096 * j - HALO
    pos = np.clip(np.arange(start, start + SLABT), 0, T - 1)
    d["rot3"] = np.ascontiguousarray(
        rot[pos].reshape(NTI, 128, 4, DH))
    d["mask3"] = _mask_tiles_np()
    selv = np.zeros((128, NWIN), np.float32)
    gw0 = (0 - start) // WIN
    if 0 <= gw0 < NWIN and start <= 0:
        selv[:, gw0] = 1.0
    d["selv"] = selv
    hk = np.ones((128, NTI), np.float32)
    if j == 0:
        for it in range(NTI):
            tpos = start + it * 128 + np.arange(128)
            hk[:, it] = (tpos >= 0).astype(np.float32)
    d["hkb"] = hk
    idx = np.empty((NTI, 128), np.uint32)
    for it in range(NTI):
        tpos = start + it * 128 + np.arange(128)
        idx[it] = np.where((tpos >= 0) & (tpos < T), tpos, T).astype(np.uint32)
    d["idx_slab"] = idx
    d["idx_own"] = (4096 * j + np.arange(CHUNK).reshape(CHUNK // 128, 128)).astype(np.uint32)
    gb = np.zeros((DEPTH3, 1, 3, D), np.float32)
    gb[:, 0, 0, :] = _np(inp["L1_ffn_ln_g"])
    gb[:, 0, 1, :] = _np(inp["L1_ffn_ln_b"])
    gb[:, 0, 2, :] = _np(inp["L1_ffn_b2"])
    d["ffnrows"] = gb
    return d



def build_full_program():
    import contextlib
    nc = bass.Bass("TRN2", target_bir_lowering=False)
    g = {}
    declare_p1_dram(nc, g, external_x1=False)
    declare_p2_dram(nc, g, x1_external=False)
    declare_p3_dram(nc, g, xn1_external=False)
    g["attnbr_dram"] = nc.dram_tensor("attnbr_dram", [CHUNK, D], F32).ap()
    g["out_dram"] = nc.dram_tensor("out", [CHUNK, D], F32, kind="ExternalOutput").ap()

    with contextlib.ExitStack() as stack:
        s = S()
        alloc_persist(nc, stack, s)
        sems = {}
        for name in ENGINES + QUEUES:
            sems[name] = stack.enter_context(nc.semaphore(name))
        p = Prog(nc)
        with nc.Block() as block:
            dma(p, "qsp", s.ident_bf[:], g["ident_bf"], writes=("identbf",))
            dma(p, "qsp", s.ident_f32[:], g["ident_f32"], writes=("identf32",))
            dma(p, "qsp", s.onescol32[:], g["onescol"], writes=("onescol32",))
            with contextlib.ExitStack() as st1:
                alloc_phase1(nc, st1, s)
                build_phase1(nc, p, s, g)
                p.fence()
                p.materialize(block, sems)
            with contextlib.ExitStack() as st1b:
                alloc_moe(nc, st1b, s)
                build_moe(nc, p, s, g)
                p.fence()
                p.materialize(block, sems)
            with contextlib.ExitStack() as stv:
                alloc_vbuf(nc, stv, s)
                with contextlib.ExitStack() as st2a:
                    alloc_phase2a(nc, st2a, s)
                    build_phase2a(nc, p, s, g)
                    p.fence()
                    p.materialize(block, sems)
                with contextlib.ExitStack() as st2b:
                    alloc_phase2b(nc, st2b, s)
                    build_phase2b(nc, p, s, g)
                    p.fence()
                    p.materialize(block, sems)
            build_scramble(nc, p, s, g)
            with contextlib.ExitStack() as st3:
                alloc_phase3(nc, st3, s)
                build_phase3(nc, p, s, g)
                build_final(nc, p, s, g)
                p.fence()
                p.materialize(block, sems)
            p.barrier_all()
            p.materialize(block, sems)
    return nc


def prep_full_inputs(x_batch, inputs, j):
    d = prep_p1_inputs(x_batch, inputs)
    d["W_Q"] = np.ascontiguousarray(_np(inputs["L1_attn_W_Q"]))
    d["W_K"] = np.ascontiguousarray(_np(inputs["L1_attn_W_K"]))
    d["W_Vs"] = np.ascontiguousarray(_np(inputs["L1_attn_W_V"])[:, VSL * j:VSL * (j + 1)])
    e0, em = _decay_tiles()
    d["e0_t"] = e0
    d["em_t"] = np.ascontiguousarray(em)
    d3 = prep_p3_inputs(inputs, j)
    for k in ("ident_bf", "ident_f32", "onescol"):
        d3.pop(k, None)
    d.update(d3)
    d["onescol"] = np.ones((1, 128), np.float32)
    return d


FUSED = True


def kernel_fused(**inputs):
    x = np.ascontiguousarray(np.asarray(inputs["L0_attn_x"], dtype=np.float32))
    if "full" not in _PROG_CACHE:
        _PROG_CACHE["full"] = build_full_program()
    in_maps = [prep_full_inputs(x[c // 4], inputs, c % 4) for c in range(8)]
    res = run_bass_kernel_spmd(_PROG_CACHE["full"], in_maps,
                               core_ids=list(range(8)))
    out = np.empty((B, T, D), np.float32)
    for c in range(8):
        b, j = c // 4, c % 4
        out[b, CHUNK * j:CHUNK * (j + 1)] = res.results[c]["out"]
    return out


# version 03
# speedup vs baseline: 1.0414x; 1.0279x over previous
import numpy as np
import ml_dtypes
import concourse.bass as bass
import concourse.mybir as mybir
from concourse.bass_utils import run_bass_kernel_spmd
from concourse.alu_op_type import AluOpType as OP

F32 = mybir.dt.float32
BF16 = mybir.dt.bfloat16
U32 = mybir.dt.uint32
AF = mybir.ActivationFunctionType
AX = mybir.AxisListType

BF16NP = ml_dtypes.bfloat16

B = 2
T = 16384
D = 1024
CHUNK = 4096
HALO = 2048
SLABT = HALO + CHUNK
NEG = -1e30


ENGINES = ("pe", "dve", "act", "pool", "sp")
QUEUES = ("qsp", "qpool", "qact")
ENG_OF_Q = {"qsp": "sp", "qpool": "pool", "qact": "act"}


class Prog:
    def __init__(self, nc, debug_serial=False):
        self.nc = nc
        self.ops = {e: [] for e in ENGINES}
        self.count = dict.fromkeys(ENGINES, 0)
        self.qcount = dict.fromkeys(QUEUES, 0)
        self.track = {}
        self.waited = {e: {} for e in ENGINES}
        self.debug_serial = debug_serial
        self.act_last_func = None

    def _resolve(self, engine, reads, writes):
        waits = {}

        def add(tok):
            if tok is None:
                return
            sem, v = tok
            if waits.get(sem, 0) < v:
                waits[sem] = v

        for k in reads:
            ent = self.track.get(k)
            if ent:
                add(ent[0])
        for k in writes:
            ent = self.track.get(k)
            if ent:
                add(ent[0])
                for t in ent[1]:
                    add(t)
        out = []
        w = self.waited[engine]
        for sem, v in sorted(waits.items()):
            if w.get(sem, 0) < v:
                out.append((sem, v))
                w[sem] = v
        return out

    def _commit(self, tok, reads, writes):
        for k in writes:
            self.track[k] = [tok, []]
        for k in reads:
            ent = self.track.get(k)
            if ent is None:
                self.track[k] = [None, [tok]]
            else:
                ent[1].append(tok)

    def op(self, engine, emit, reads=(), writes=(), act_func=None):
        waits = self._resolve(engine, reads, writes)
        if engine == "act":
            if (self.act_last_func is not None and act_func != self.act_last_func
                    and self.count["act"] > 0):
                v = self.count["act"]
                if self.waited["act"].get("act", 0) < v:
                    waits.append(("act", v))
                    self.waited["act"]["act"] = v
            self.act_last_func = act_func
        if self.debug_serial and self.count[engine] > 0:
            v = self.count[engine]
            if self.waited[engine].get(engine, 0) < v:
                waits.append((engine, v))
                self.waited[engine][engine] = v
        self.count[engine] += 1
        tok = (engine, self.count[engine])
        self.ops[engine].append((waits, emit, tok, 1))
        self._commit(tok, reads, writes)
        return tok

    def dma(self, queue, emit, reads=(), writes=()):
        eng = ENG_OF_Q[queue]
        waits = self._resolve(eng, reads, writes)
        self.qcount[queue] += 16
        tok = (queue, self.qcount[queue])
        self.ops[eng].append((waits, emit, tok, 16))
        self._commit(tok, reads, writes)
        return tok

    def barrier_all(self, engine="sp"):
        waits = []
        for e in ENGINES:
            if e != engine and self.count[e] > 0:
                waits.append((e, self.count[e]))
        for q in QUEUES:
            if self.qcount[q] > 0:
                waits.append((q, self.qcount[q]))
        w = self.waited[engine]
        waits = [(s, v) for (s, v) in waits if w.get(s, 0) < v]
        for s, v in waits:
            w[s] = v
        self.ops[engine].append((waits, None, None, 0))

    def fence(self):
        snap_e = dict(self.count)
        snap_q = dict(self.qcount)
        for e in ENGINES:
            waits = [(o, snap_e[o]) for o in ENGINES if o != e and snap_e[o] > 0]
            waits += [(q, snap_q[q]) for q in QUEUES if snap_q[q] > 0]
            w = self.waited[e]
            waits = [(s_, v) for (s_, v) in waits if w.get(s_, 0) < v]
            for s_, v in waits:
                w[s_] = v
            if waits:
                self.ops[e].append((waits, None, None, 0))

    def materialize(self, block, sems):
        def run(ename, eng):
            for waits, emit, tok, inc in self.ops[ename]:
                for sem, v in waits:
                    eng.wait_ge(sems[sem], v)
                if emit is None:
                    continue
                inst = emit(eng)
                assert inst is not None, f"op on {ename} returned no instruction"
                inst.then_inc(sems[tok[0]], inc)
            self.ops[ename] = []

        for ename, starter in (("pe", block.tensor), ("dve", block.vector),
                               ("act", block.scalar), ("pool", block.gpsimd),
                               ("sp", block.sync)):
            if self.ops[ename]:
                starter(lambda eng, ename=ename: run(ename, eng))



def mm_group(p, ps_ap, pairs, reads, write_key, accumulate=False):
    def emit(eng):
        inst = None
        n = len(pairs)
        for i, (lhsT, rhs) in enumerate(pairs):
            inst = eng.matmul(ps_ap, lhsT, rhs,
                              start=(i == 0 and not accumulate),
                              stop=(i == n - 1))
        return inst
    return p.op("pe", emit, reads=reads, writes=(write_key,))


def transpose_group(p, items, reads, write_key):
    def emit(eng):
        inst = None
        for dst, src, ident in items:
            inst = eng.transpose(dst, src, ident)
        return inst
    return p.op("pe", emit, reads=reads, writes=(write_key,))


def dve(p, name, *a, reads=(), writes=(), **kw):
    def emit(eng):
        return getattr(eng, name)(*a, **kw)
    return p.op("dve", emit, reads=reads, writes=writes)


def pool(p, name, *a, reads=(), writes=(), **kw):
    def emit(eng):
        return getattr(eng, name)(*a, **kw)
    return p.op("pool", emit, reads=reads, writes=writes)


def act(p, out, in_, func, reads=(), writes=(), **kw):
    def emit(eng):
        return eng.activation(out, in_, func, **kw)
    return p.op("act", emit, reads=reads, writes=writes, act_func=func)


def dma(p, queue, out, in_, reads=(), writes=(), **kw):
    def emit(eng):
        return eng.dma_start(out=out, in_=in_, **kw)
    return p.dma(queue, emit, reads=reads, writes=writes)


def gather(p, out, src, idx_ap, reads=(), writes=()):
    def emit(eng):
        return eng.indirect_dma_start(
            out=out, out_offset=None, in_=src,
            in_offset=bass.IndirectOffsetOnAxis(ap=idx_ap, axis=0))
    return p.dma("qpool", emit, reads=reads, writes=writes)


def scatter(p, dst, dst_idx_ap, src, reads=(), writes=()):
    def emit(eng):
        return eng.indirect_dma_start(
            out=dst, out_offset=bass.IndirectOffsetOnAxis(ap=dst_idx_ap, axis=0),
            in_=src, in_offset=None)
    return p.dma("qpool", emit, reads=reads, writes=writes)



def emit_ln(p, xc_buf, kxc, sq_buf, ksq, stat, kst, x_ap, key_in, outs):
    nmean = stat[:, 0:1]
    ssq = stat[:, 1:2]
    var = stat[:, 2:3]
    rstd = stat[:, 3:4]
    sq = stat[:, 4:5]
    xc = xc_buf[:, 0:D]
    dve(p, "tensor_reduce", nmean, x_ap, axis=AX.X, op=OP.add, negate=True,
        reads=(key_in,), writes=((kst, "nmean"),))
    dve(p, "tensor_scalar", nmean, nmean, 1.0 / D, None, op0=OP.mult,
        reads=((kst, "nmean"),), writes=((kst, "nmean"),))
    dve(p, "tensor_scalar", xc, x_ap, nmean, None, op0=OP.add,
        reads=(key_in, (kst, "nmean")), writes=(kxc,))
    dve(p, "scalar_tensor_tensor", sq_buf[:, 0:D], xc, 1.0, xc,
        op0=OP.mult, op1=OP.mult, accum_out=ssq,
        reads=(kxc,), writes=(ksq, (kst, "ssq")))
    dve(p, "tensor_scalar", var, ssq, 1.0 / D, 1e-5, op0=OP.mult, op1=OP.add,
        reads=((kst, "ssq"),), writes=((kst, "var"),))
    act(p, sq, var, AF.Sqrt, reads=((kst, "var"),), writes=((kst, "sqv"),))
    dve(p, "reciprocal", rstd, sq, reads=((kst, "sqv"),), writes=((kst, "rstd"),))
    for out_ap, out_key in outs:
        dve(p, "tensor_scalar", out_ap, xc, rstd, None, op0=OP.mult,
            reads=(kxc, (kst, "rstd")), writes=(out_key,))


class S:
    pass


P1_NSLAB = None


def build_phase1(nc, p, s, g):
    SLAB = 512
    NT = SLAB // 128
    NSLAB = T // SLAB if P1_NSLAB is None else P1_NSLAB

    for wname, wap in (("wf", g["Wf_t"]), ("wi", g["Wi_t"]), ("wh", g["Wh_t"])):
        dma(p, "qpool", getattr(s, wname)[:],
            wap.rearrange("(k p) e -> p k e", p=128),
            writes=(wname,))
    dma(p, "qsp", s.wr32[:], g["w_router"].rearrange("(k p) e -> p k e", p=128),
        writes=("wr32",))
    for bname, bap in (("bfb", g["b_f"]), ("bib", g["b_i"]), ("bhb", g["b_h"])):
        dma(p, "qsp", getattr(s, bname)[:], bap, writes=(bname,))
    pool(p, "memset", s.zrow_bf[:], 0.0, writes=("zrowbf",))
    dma(p, "qpool", g["xn_dram"][T:T + 1, :], s.zrow_bf[0:1, 0:D],
        reads=("zrowbf",), writes=(("xn_dram", "z"),))
    pool(p, "memset", s.state[:], 0.0, writes=tuple(("state", e) for e in range(8)))

    pre = [s.preF, s.preI, s.preH]
    wsb = [s.wf, s.wi, s.wh]
    wk = ["wf", "wi", "wh"]
    bias = [s.bfb, s.bib, s.bhb]
    biask = ["bfb", "bib", "bhb"]
    kF, kI, kH = "preF", "preI", "preH"
    k1, k2 = "S1", "S2"
    kx, kxn, kxnT = "xsl", "xnsl", "xnT"

    xb_r = g["xb"].rearrange("(n tt p) d -> n p tt d", p=128, tt=NT)
    xn_r = g["xn_dram"][0:T, :].rearrange("(n tt p) d -> n p tt d", p=128, tt=NT)
    pa_r = g["partial_dram"].rearrange("(n tt p) d -> n p tt d", p=128, tt=NT)

    for sl in range(NSLAB):
        dma(p, "qsp", s.xsl[:], xb_r[sl], writes=(kx,))
        for tt in range(NT):
            emit_ln(p, s.lnxc, "lnxc", s.xn32sl[:, tt], "xn32sl",
                    s.stats, "lnst",
                    s.xsl[:, tt], kx, [(s.xn32sl[:, tt], "xn32sl")])
        dma(p, "qpool", xn_r[sl], s.xn32sl[:], reads=("xn32sl",),
            writes=(("xn_dram", sl),))
        for k in range(8):
            items = [(s.pt_f[:, tt * 128:(tt + 1) * 128],
                      s.xn32sl[:, tt, k * 128:(k + 1) * 128], s.ident_f32[:])
                     for tt in range(NT)]
            transpose_group(p, items, reads=("xn32sl", "identf32"),
                            write_key="ptf")
            dve(p, "tensor_copy", s.xnT[:, k], s.pt_f[:, 0:512],
                reads=("ptf",), writes=((kxnT, k),))
            dve(p, "tensor_copy", s.xnT32k[:], s.pt_f[:, 0:512],
                reads=("ptf",), writes=("xnT32k",))
            def emit_rmm(eng, k=k):
                inst = None
                for tt in range(NT):
                    inst = eng.matmul(s.ps_mm[tt][:, 0:8],
                                      s.xnT32k[:, tt * 128:(tt + 1) * 128],
                                      s.wr32[:, k],
                                      start=(k == 0), stop=(k == 7))
                return inst
            p.op("pe", emit_rmm, reads=("xnT32k", "wr32"),
                 writes=tuple(("psmm", tt) for tt in range(NT)))
        rxnT = tuple((kxnT, k) for k in range(8))
        for tt in range(NT):
            dve(p, "tensor_copy",
                s.logits[:, sl * NT + tt, :], s.ps_mm[tt][:, 0:8],
                reads=(("psmm", tt),),
                writes=(("logits", sl * NT + tt),))
        for w in range(3):
            for e in range(8):
                bk = (w * 8 + e) % 4
                bank = s.ps_mm[bk]
                mm_group(p, bank[:],
                         [(wsb[w][:, k, e * 128:(e + 1) * 128], s.xnT[:, k])
                          for k in range(8)],
                         reads=rxnT + (wk[w],), write_key=("psmm", bk))
                dve(p, "tensor_scalar", pre[w][:, e], bank[:],
                    bias[w][:, e:e + 1], None, op0=OP.add,
                    reads=(("psmm", bk), biask[w]),
                    writes=((wk[w] + "pre", e),))
        rF = tuple(("wfpre", e) for e in range(8))
        rI = tuple(("wipre", e) for e in range(8))
        rH = tuple(("whpre", e) for e in range(8))
        if sl == 0 and g.get("dbg_pre") is not None:
            dma(p, "qsp", g["dbg_pre"][0], s.preF[:], reads=rF, writes=(("dbg", 0),))
            dma(p, "qsp", g["dbg_pre"][1], s.preI[:], reads=rI, writes=(("dbg", 1),))
            dma(p, "qsp", g["dbg_pre"][2], s.preH[:], reads=rH, writes=(("dbg", 2),))
        F, I, H = s.preF[:], s.preI[:], s.preH[:]
        S1, S2 = s.sc1[:], s.sc2[:]
        act(p, S1, F, AF.Abs, reads=rF, writes=(k1,))
        act(p, S1, S1, AF.Exp, scale=-1.0, reads=(k1,), writes=(k1,))
        act(p, S1, S1, AF.Ln, bias=1.0, reads=(k1,), writes=(k1,))
        dve(p, "tensor_scalar", F, F, 0.0, None, op0=OP.min,
            reads=rF, writes=rF + (kF,))
        dve(p, "tensor_tensor", S1, S1, F, op=OP.subtract,
            reads=(k1, kF), writes=(k1,))
        act(p, S2, I, AF.Abs, reads=rI, writes=(k2,))
        act(p, S2, S2, AF.Exp, scale=-1.0, reads=(k2,), writes=(k2,))
        act(p, S2, S2, AF.Ln, bias=1.0, reads=(k2,), writes=(k2,))
        dve(p, "tensor_scalar", I, I, 0.0, None, op0=OP.min,
            reads=rI, writes=rI + (kI,))
        dve(p, "tensor_tensor", S2, S2, I, op=OP.subtract,
            reads=(k2, kI), writes=(k2,))
        dve(p, "tensor_tensor", S1, S1, S2, op=OP.subtract,
            reads=(k1, k2), writes=(k1,))
        act(p, F, S1, AF.Sigmoid, scale=-1.0, reads=(k1,), writes=rF + (kF,))
        act(p, S2, S1, AF.Sigmoid, reads=(k1,), writes=(k2,))
        dve(p, "tensor_scalar", S1, H, 0.5, None, op0=OP.add,
            reads=rH, writes=(k1,))
        act(p, H, H, AF.Sigmoid, reads=rH, writes=rH + (kH,))
        dve(p, "tensor_tensor", H, H, S1, op=OP.max,
            reads=(kH, k1), writes=(kH,))
        dve(p, "tensor_tensor", S2, S2, H, op=OP.mult,
            reads=(k2, kH), writes=(k2,))
        for e in range(8):
            dve(p, "tensor_tensor_scan", s.sc1[:, e], s.preF[:, e], s.sc2[:, e],
                s.state[:, e:e + 1], op0=OP.mult, op1=OP.add,
                reads=(kF, k2, ("state", e)), writes=(k1,))
            dve(p, "tensor_copy", s.state[:, e:e + 1], s.sc1[:, e, SLAB - 1:SLAB],
                reads=(k1,), writes=(("state", e),))
        for tt in range(NT):
            for half in range(2):
                items = [(s.pt_f[:, q * 128:(q + 1) * 128],
                          s.sc1[:, half * 4 + q, tt * 128:(tt + 1) * 128],
                          s.ident_f32[:]) for q in range(4)]
                transpose_group(p, items, reads=(k1, "identf32"),
                                write_key="ptf")
                dve(p, "tensor_tensor", s.xsl[:, tt, half * 512:(half + 1) * 512],
                    s.xsl[:, tt, half * 512:(half + 1) * 512], s.pt_f[:],
                    op=OP.add, reads=("ptf", kx), writes=(kx,))
        dma(p, "qsp", pa_r[sl], s.xsl[:], reads=(kx,),
            writes=(("partial", sl),))

    if g.get("xn_out") is not None:
        dma(p, "qsp", g["xn_out"], g["xn_dram"][0:T, :],
            reads=tuple(("xn_dram", sl) for sl in range(NSLAB)),
            writes=(("xnout",),))


def build_x1_copy(nc, p, s, g):
    NT = 4
    NSLAB = T // 512 if P1_NSLAB is None else P1_NSLAB
    pa_r = g["partial_dram"].rearrange("(n tt p) d -> n p tt d", p=128, tt=NT)
    x1_r = g["x1_dram"].rearrange("(n tt p) d -> n p tt d", p=128, tt=NT)
    for sl in range(NSLAB):
        dma(p, "qsp", s.xsl[:], pa_r[sl],
            reads=(("partial", sl),), writes=("xsl",))
        dma(p, "qsp", x1_r[sl], s.xsl[:], reads=("xsl",),
            writes=(("x1", sl),))


NEXP = 8
CAP = 512
NGRP = 4
GT = 32
INVN = 4352
XND_Z = T


def bcast_m(ap2d, n):
    return ap2d.unsqueeze(1).broadcast_to([ap2d.shape[0], n, ap2d.shape[1]])


def bcast_l(ap2d, k):
    return ap2d.unsqueeze(2).broadcast_to([ap2d.shape[0], ap2d.shape[1], k])


def build_moe_routing(nc, p, s, g, gi):
    lt = s.logits[:, GT * gi:GT * (gi + 1), :]
    prb = s.prb[:]
    oh1, oh2 = s.ohb1[:], s.ohb2[:]
    cntb = s.cntb[:]
    r = lambda i: s.rstat[:, :, i]
    rb = lambda i: bcast_l(s.rstat[:, :, i], NEXP)
    kl = tuple(("logits", t) for t in range(GT * gi, GT * (gi + 1)))
    K = lambda *names: tuple(names)

    dve(p, "tensor_reduce", r(0), lt, axis=AX.X, op=OP.max, negate=True,
        reads=kl, writes=("rs0",))
    dve(p, "tensor_tensor", prb, lt, rb(0), op=OP.add,
        reads=kl + ("rs0",), writes=("prb",))
    act(p, prb, prb, AF.Exp, reads=("prb",), writes=("prb",))
    dve(p, "tensor_reduce", r(1), prb, axis=AX.X, op=OP.add,
        reads=("prb",), writes=("rs1",))
    dve(p, "reciprocal", r(1), r(1), reads=("rs1",), writes=("rs1",))
    dve(p, "tensor_tensor", prb, prb, rb(1), op=OP.mult,
        reads=("prb", "rs1"), writes=("prb",))
    dve(p, "tensor_reduce", r(2), prb, axis=AX.X, op=OP.max,
        reads=("prb",), writes=("rs2",))
    dve(p, "tensor_tensor", oh1, prb, rb(2), op=OP.is_equal,
        reads=("prb", "rs2"), writes=("oh1",))
    dve(p, "tensor_tensor", oh1, oh1, bcast_m(s.iota8m8[:], GT), op=OP.mult,
        reads=("oh1", "iota8m8"), writes=("oh1",))
    dve(p, "tensor_scalar", oh1, oh1, 8.0, None, op0=OP.add,
        reads=("oh1",), writes=("oh1",))
    dve(p, "tensor_reduce", r(3), oh1, axis=AX.X, op=OP.min,
        reads=("oh1",), writes=("rs3",))
    dve(p, "tensor_tensor", oh1, bcast_m(s.iota8[:], GT), rb(3), op=OP.is_equal,
        reads=("iota8", "rs3"), writes=("oh1",))
    dve(p, "tensor_copy", s.pbuf[:, gi, :, 0], r(2),
        reads=("rs2",), writes=(("pbuf", gi),))
    dve(p, "scalar_tensor_tensor", prb, oh1, NEG, prb, op0=OP.mult, op1=OP.add,
        reads=("oh1", "prb"), writes=("prb",))
    dve(p, "tensor_reduce", r(4), prb, axis=AX.X, op=OP.max,
        reads=("prb",), writes=("rs4",))
    dve(p, "tensor_tensor", oh2, prb, rb(4), op=OP.is_equal,
        reads=("prb", "rs4"), writes=("oh2",))
    dve(p, "tensor_tensor", oh2, oh2, bcast_m(s.iota8m8[:], GT), op=OP.mult,
        reads=("oh2", "iota8m8"), writes=("oh2",))
    dve(p, "tensor_scalar", oh2, oh2, 8.0, None, op0=OP.add,
        reads=("oh2",), writes=("oh2",))
    dve(p, "tensor_reduce", r(5), oh2, axis=AX.X, op=OP.min,
        reads=("oh2",), writes=("rs5",))
    dve(p, "tensor_tensor", oh2, bcast_m(s.iota8[:], GT), rb(5), op=OP.is_equal,
        reads=("iota8", "rs5"), writes=("oh2",))
    dve(p, "tensor_copy", s.pbuf[:, gi, :, 1], r(4),
        reads=("rs4",), writes=(("pbuf2", gi),))

    for cpass, (ohap, kOH) in enumerate(((oh1, "oh1"), (oh2, "oh2"))):
        pool(p, "memset", s.totrow[:], 0.0, writes=("totrow",))
        for i in range(GT):
            mm_group(p, s.ps_r[:, 0:8],
                     [(s.tri[:], ohap[:, i, :]), (s.onescol[:], s.totrow[:])],
                     reads=(kOH, "tri", "onescol", "totrow"), write_key="psr")
            dve(p, "tensor_copy", cntb[:, i, :], s.ps_r[:, 0:8],
                reads=("psr",), writes=(("cntb", i),))
            mm_group(p, s.pt_f[0:1, 0:8], [(s.ones128[:], ohap[:, i, :])],
                     reads=(kOH, "ones128"), write_key="ptf",
                     accumulate=(i > 0))
            dve(p, "tensor_copy", s.totrow[:], s.pt_f[0:1, 0:8],
                reads=("ptf",), writes=("totrow",))
        if cpass == 0:
            mm_group(p, s.ps_r[:, 0:8], [(s.onescol[:], s.totrow[:])],
                     reads=("totrow", "onescol"), write_key="psr")
            dve(p, "tensor_copy", s.tot1bc[:], s.ps_r[:, 0:8],
                reads=("psr",), writes=("tot1bc",))
            dve(p, "tensor_tensor", s.prb[:], oh1, cntb, op=OP.mult,
                reads=("oh1",) + tuple(("cntb", i) for i in range(GT)),
                writes=("prb",))
            dve(p, "tensor_reduce", s.prib[:, :, 0], s.prb[:], axis=AX.X,
                op=OP.add, reads=("prb",), writes=("pri1",))
        else:
            dve(p, "tensor_tensor", s.prb[:], oh2, cntb, op=OP.mult,
                reads=("oh2",) + tuple(("cntb", i) for i in range(GT)),
                writes=("prb",))
            dve(p, "tensor_reduce", s.prib[:, :, 1], s.prb[:], axis=AX.X,
                op=OP.add, reads=("prb",), writes=("pri2",))
            dve(p, "tensor_tensor", s.prb[:], oh2, bcast_m(s.tot1bc[:], GT),
                op=OP.mult, reads=("oh2", "tot1bc"), writes=("prb",))
            dve(p, "tensor_reduce", s.rstat[:, :, 6], s.prb[:], axis=AX.X,
                op=OP.add, reads=("prb",), writes=("rs6",))
            dve(p, "tensor_tensor", s.prib[:, :, 1], s.prib[:, :, 1],
                s.rstat[:, :, 6], op=OP.add,
                reads=("pri2", "rs6"), writes=("pri2",))

    pool(p, "memset", s.invinit[:], XND_Z, writes=("invinit",))
    dma(p, "qsp",
        g["inv_dram"][INVN * gi:INVN * (gi + 1), :].rearrange("(q a) c -> q a c", a=34),
        s.invinit[:].unsqueeze(2), reads=("invinit",), writes=(("inv", gi),))

    for c, kpri, kidx in ((0, "pri1", "rs3x"), (1, "pri2", "rs5x")):
        idxs = s.rstat[:, :, 3 + 2 * c]
        kidxr = "rs3" if c == 0 else "rs5"
        pri = s.prib[:, :, c]
        t1, t2, t3 = s.rt1[:], s.rt2[:], s.rt3[:]
        dve(p, "tensor_scalar", t1, pri, 512.0, None, op0=OP.is_ge,
            reads=(kpri,), writes=("rt1",))
        dve(p, "tensor_scalar", t2, idxs, 512.0, None, op0=OP.mult,
            reads=(kidxr,), writes=("rt2",))
        dve(p, "tensor_tensor", t2, t2, pri, op=OP.add,
            reads=("rt2", kpri), writes=("rt2",))
        dve(p, "tensor_scalar", t3, t2, -1.0, 4096.0, op0=OP.mult, op1=OP.add,
            reads=("rt2",), writes=("rt3",))
        dve(p, "tensor_tensor", t3, t3, t1, op=OP.mult,
            reads=("rt3", "rt1"), writes=("rt3",))
        dve(p, "tensor_tensor", t3, t3, t2, op=OP.add,
            reads=("rt3", "rt2"), writes=("rt3",))
        dve(p, "tensor_copy", s.slotc[:, gi, :, c], t3,
            reads=("rt3",), writes=(("slotc", gi, c),))
        dve(p, "tensor_scalar", t3,
            s.pidx[:, 0:1].broadcast_to([128, GT]), 1.0, 4200.0,
            op0=OP.mult, op1=OP.add, reads=("pidx",), writes=("rt3",))
        dve(p, "tensor_tensor", t3, t3, t2, op=OP.subtract,
            reads=("rt3", "rt2"), writes=("rt3",))
        dve(p, "tensor_tensor", t3, t3, t1, op=OP.mult,
            reads=("rt3", "rt1"), writes=("rt3",))
        dve(p, "tensor_tensor", t3, t3, t2, op=OP.add,
            reads=("rt3", "rt2"), writes=("rt3",))
        dve(p, "tensor_scalar", t3, t3, float(INVN * gi), None, op0=OP.add,
            reads=("rt3",), writes=("rt3",))
        dve(p, "tensor_copy", s.sslotu[:, :, c], t3,
            reads=("rt3",), writes=(("sslotu", c),))
    for i in range(GT):
        for c in range(2):
            scatter(p, g["inv_dram"], s.sslotu[:, i, c:c + 1],
                    s.tokidx[:, GT * gi + i:GT * gi + i + 1],
                    reads=(("sslotu", c), "tokidx", ("inv", gi)),
                    writes=(("inv", gi),))


def build_moe_experts(nc, p, s, g):
    wi_r = [g["moe_wi"][e].rearrange("(k p) f -> p k f", p=128) for e in range(NEXP)]
    wo_r = [g["moe_wo"][e].rearrange("(k p) h -> p k h", p=128) for e in range(NEXP)]
    for e in range(NEXP):
        dma(p, "qpool", s.wo_sb[:], wo_r[e], writes=("wo_sb",))
        for gi in range(NGRP):
            for q in range(4):
                a = INVN * gi + e * CAP + q * 128
                dma(p, "qsp", s.idxu[:], g["inv_dram"][a:a + 128, :],
                    reads=(("inv", gi),), writes=("idxu",))
                gather(p, s.ein[:], g["xn_dram"], s.idxu[:, 0:1],
                       reads=("idxu",) + tuple(("xn_dram", sl) for sl in range(32))
                       + (("xn_dram", "z"),),
                       writes=("ein",))
                items = [(s.pt_b[:, k * 128:(k + 1) * 128],
                          s.ein[:, k * 128:(k + 1) * 128], s.ident_bf[:])
                         for k in range(8)]
                transpose_group(p, items, reads=("ein", "identbf"),
                                write_key="ptb")
                dve(p, "tensor_copy", s.einT[:, :, q * 128:(q + 1) * 128],
                    s.pt_b[:].rearrange("p (k x) -> p k x", k=8),
                    reads=("ptb",), writes=(("einT", q),))
            reinT = tuple(("einT", q) for q in range(4))
            for fk in range(8):
                wb = s.wi_ch[fk % 2]
                kwb = ("wi_ch", fk % 2)
                dma(p, "qpool", wb[:], wi_r[e][:, :, fk * 512:(fk + 1) * 512],
                    writes=(kwb,))
                for m in range(4):
                    bk = m % 4
                    mm_group(p, s.ps_mm[bk][:],
                             [(wb[:, k, m * 128:(m + 1) * 128], s.einT[:, k])
                              for k in range(8)],
                             reads=reinT + (kwb,), write_key=("psmm", bk))
                    act(p, s.h1T[:, fk * 4 + m], s.ps_mm[bk][:],
                        AF.Gelu_apprx_tanh,
                        reads=(("psmm", bk),), writes=(("h1T", fk * 4 + m),))
            rh1 = tuple(("h1T", i) for i in range(32))
            for n in range(2):
                for m in range(4):
                    bk = m
                    mm_group(p, s.ps_mm[bk][:],
                             [(s.h1T[:, k, m * 128:(m + 1) * 128],
                               s.wo_sb[:, k, n * 512:(n + 1) * 512])
                              for k in range(32)],
                             reads=rh1 + ("wo_sb",), write_key=("psmm", bk))
                    dve(p, "tensor_copy", s.stg[:], s.ps_mm[bk][:],
                        reads=(("psmm", bk),), writes=("stg",))
                    dma(p, "qsp",
                        g["out2_dram"][gi][e * CAP + m * 128:e * CAP + (m + 1) * 128,
                                           n * 512:(n + 1) * 512],
                        s.stg[:], reads=("stg",), writes=(("out2", gi, e, n, m),))


def build_moe_combine(nc, p, s, g):
    pa_t = g["partial_dram"].rearrange("(t p) d -> t p d", p=128)
    x1_t = g["x1_dram"].rearrange("(t p) d -> t p d", p=128)
    for gi in range(NGRP):
        pool(p, "memset", s.zrow_f[:], 0.0, writes=("zrowf",))
        dma(p, "qpool", g["out2_dram"][gi][NEXP * CAP:NEXP * CAP + 1, :],
            s.zrow_f[0:1, :], reads=("zrowf",), writes=(("out2z", gi),))
        rout2 = tuple(("out2", gi, e, n, m)
                      for e in range(NEXP) for n in range(2) for m in range(4))
        for i in range(GT):
            gt = GT * gi + i
            dma(p, "qsp", s.part[:], pa_t[gt], reads=(("partial", gt // 4),),
                writes=("part",))
            gather(p, s.g1t[:], g["out2_dram"][gi], s.slotc[:, gi, i, 0:1],
                   reads=(("slotc", gi, 0), ("out2z", gi)) + rout2,
                   writes=("g1t",))
            gather(p, s.g2t[:], g["out2_dram"][gi], s.slotc[:, gi, i, 1:2],
                   reads=(("slotc", gi, 1), ("out2z", gi)) + rout2,
                   writes=("g2t",))
            dve(p, "tensor_scalar", s.g1t[:], s.g1t[:], s.pbuf[:, gi, i, 0:1],
                None, op0=OP.mult,
                reads=("g1t", ("pbuf", gi)), writes=("g1t",))
            dve(p, "scalar_tensor_tensor", s.g1t[:], s.g2t[:],
                s.pbuf[:, gi, i, 1:2], s.g1t[:], op0=OP.mult, op1=OP.add,
                reads=("g2t", ("pbuf2", gi), "g1t"), writes=("g1t",))
            dve(p, "tensor_tensor", s.part[:], s.part[:], s.g1t[:], op=OP.add,
                reads=("part", "g1t"), writes=("part",))
            dma(p, "qsp", x1_t[gt], s.part[:], reads=("part",),
                writes=(("x1", gt),))


def build_moe(nc, p, s, g):
    dma(p, "qsp", s.tri[:], g["tri_f32"], writes=("tri",))
    dma(p, "qsp", s.iota8[:], g["iota8"], writes=("iota8",))
    dma(p, "qsp", s.iota8m8[:], g["iota8m8"], writes=("iota8m8",))
    dma(p, "qsp", s.onescol[:], g["onescol"], writes=("onescol",))
    dma(p, "qsp", s.ones128[:], g["ones128"], writes=("ones128",))
    dma(p, "qsp", s.pidx[:], g["pidx"], writes=("pidx",))
    dma(p, "qsp", s.tokidx[:], g["tokidx"], writes=("tokidx",))
    for gi in range(NGRP):
        build_moe_routing(nc, p, s, g, gi)
    build_moe_experts(nc, p, s, g)
    build_moe_combine(nc, p, s, g)



def _mk(nc, stack, prefix):
    def sb(name, shape, dt):
        return stack.enter_context(nc.sbuf_tensor(prefix + name, shape, dt))
    def psb(name, shape, dt):
        return stack.enter_context(nc.psum_tensor(prefix + name, shape, dt))
    return sb, psb


def alloc_persist(nc, stack, s):
    sb, psb = _mk(nc, stack, "sp_")
    s.ident_bf = sb("ident_bf", [128, 128], BF16)
    s.ident_f32 = sb("ident_f32", [128, 128], F32)
    s.logits = sb("logits", [128, 128, 8], F32)
    s.onescol32 = sb("onescol32", [1, 128], F32)
    s.ps_mm = [psb(f"ps_mm{i}", [128, 512], F32) for i in range(4)]
    s.ps_r = psb("ps_r", [128, 512], F32)
    s.pt_b = psb("pt_b", [128, 1024], BF16)
    s.pt_f = psb("pt_f", [128, 512], F32)
    return s


def alloc_phase1(nc, stack, s):
    sb, _ = _mk(nc, stack, "s1_")
    s.zrow_bf = sb("zrow_bf", [128, D], BF16)
    s.wf = sb("wf", [128, 8, D], BF16)
    s.wi = sb("wi", [128, 8, D], BF16)
    s.wh = sb("wh", [128, 8, D], BF16)
    s.wr32 = sb("wr32", [128, 8, 8], F32)
    s.bfb = sb("bfb", [128, 8], F32)
    s.bib = sb("bib", [128, 8], F32)
    s.bhb = sb("bhb", [128, 8], F32)
    s.xsl = sb("xsl", [128, 4, D], F32)
    s.xn32sl = sb("xn32sl", [128, 4, D], F32)
    s.lnxc = sb("lnxc", [128, D], F32)
    s.xnT32k = sb("xnT32k", [128, 512], F32)
    s.stats = sb("stats", [128, 8], F32)
    s.xnT = sb("xnT", [128, 8, 512], BF16)
    s.preF = sb("preF", [128, 8, 512], F32)
    s.preI = sb("preI", [128, 8, 512], F32)
    s.preH = sb("preH", [128, 8, 512], F32)
    s.sc1 = sb("sc1", [128, 8, 512], F32)
    s.sc2 = sb("sc2", [128, 8, 512], F32)
    s.state = sb("state", [128, 8], F32)
    return s


def alloc_moe(nc, stack, s):
    sb, _ = _mk(nc, stack, "sm_")
    s.tri = sb("tri", [128, 128], F32)
    s.iota8 = sb("iota8", [128, 8], F32)
    s.iota8m8 = sb("iota8m8", [128, 8], F32)
    s.onescol = sb("onescol", [1, 128], F32)
    s.ones128 = sb("ones128", [128, 1], F32)
    s.pidx = sb("pidx", [128, 1], F32)
    s.tokidx = sb("tokidx", [128, 128], U32)
    s.prb = sb("prb", [128, GT, 8], F32)
    s.ohb1 = sb("ohb1", [128, GT, 8], F32)
    s.ohb2 = sb("ohb2", [128, GT, 8], F32)
    s.cntb = sb("cntb", [128, GT, 8], F32)
    s.rstat = sb("rstat", [128, GT, 8], F32)
    s.prib = sb("prib", [128, GT, 2], F32)
    s.rt1 = sb("rt1", [128, GT], F32)
    s.rt2 = sb("rt2", [128, GT], F32)
    s.rt3 = sb("rt3", [128, GT], F32)
    s.totrow = sb("totrow", [1, 8], F32)
    s.tot1bc = sb("tot1bc", [128, 8], F32)
    s.invinit = sb("invinit", [128, 34], U32)
    s.sslotu = sb("sslotu", [128, GT, 2], U32)
    s.slotc = sb("slotc", [128, NGRP, GT, 2], U32)
    s.pbuf = sb("pbuf", [128, NGRP, GT, 2], F32)
    s.wo_sb = sb("wo_sb", [128, 32, 1024], BF16)
    s.wi_ch = [sb(f"wi_ch{i}", [128, 8, 512], BF16) for i in range(2)]
    s.ein = sb("ein", [128, 1024], BF16)
    s.einT = sb("einT", [128, 8, 512], BF16)
    s.h1T = sb("h1T", [128, 32, 512], BF16)
    s.idxu = sb("idxu", [128, 1], U32)
    s.stg = sb("stg", [128, 512], F32)
    s.g1t = sb("g1t", [128, 1024], F32)
    s.g2t = sb("g2t", [128, 1024], F32)
    s.part = sb("part", [128, 1024], F32)
    s.zrow_f = sb("zrow_f", [128, 1024], F32)
    return s



def _np(x):
    return np.asarray(x)


def declare_p1_dram(nc, g, external_x1=True):
    g["xb"] = nc.dram_tensor("xb", [T, D], F32, kind="ExternalInput").ap()
    g["Wf_t"] = nc.dram_tensor("Wf_t", [D, D], F32, kind="ExternalInput").ap()
    g["Wi_t"] = nc.dram_tensor("Wi_t", [D, D], F32, kind="ExternalInput").ap()
    g["Wh_t"] = nc.dram_tensor("Wh_t", [D, D], F32, kind="ExternalInput").ap()
    g["b_f"] = nc.dram_tensor("b_f", [128, 8], F32, kind="ExternalInput").ap()
    g["b_i"] = nc.dram_tensor("b_i", [128, 8], F32, kind="ExternalInput").ap()
    g["b_h"] = nc.dram_tensor("b_h", [128, 8], F32, kind="ExternalInput").ap()
    g["w_router"] = nc.dram_tensor("w_router", [D, 8], F32, kind="ExternalInput").ap()
    g["ident_bf"] = nc.dram_tensor("ident_bf", [128, 128], BF16, kind="ExternalInput").ap()
    g["ident_f32"] = nc.dram_tensor("ident_f32", [128, 128], F32, kind="ExternalInput").ap()
    g["moe_wi"] = nc.dram_tensor("moe_wi", [NEXP, D, 4096], F32, kind="ExternalInput").ap()
    g["moe_wo"] = nc.dram_tensor("moe_wo", [NEXP, 4096, D], F32, kind="ExternalInput").ap()
    g["tri_f32"] = nc.dram_tensor("tri_f32", [128, 128], F32, kind="ExternalInput").ap()
    g["iota8"] = nc.dram_tensor("iota8", [128, 8], F32, kind="ExternalInput").ap()
    g["iota8m8"] = nc.dram_tensor("iota8m8", [128, 8], F32, kind="ExternalInput").ap()
    g["onescol"] = nc.dram_tensor("onescol", [1, 128], F32, kind="ExternalInput").ap()
    g["ones128"] = nc.dram_tensor("ones128", [128, 1], F32, kind="ExternalInput").ap()
    g["pidx"] = nc.dram_tensor("pidx", [128, 1], F32, kind="ExternalInput").ap()
    g["tokidx"] = nc.dram_tensor("tokidx", [128, 128], U32, kind="ExternalInput").ap()
    g["xn_dram"] = nc.dram_tensor("xn_dram", [T + 128, D], BF16).ap()
    g["partial_dram"] = nc.dram_tensor("partial_dram", [T, D], F32).ap()
    g["inv_dram"] = nc.dram_tensor("inv_dram", [NGRP * INVN, 1], U32).ap()
    g["out2_dram"] = [nc.dram_tensor(f"out2_g{gi}", [NEXP * CAP + 1, D], F32).ap()
                      for gi in range(NGRP)]
    if external_x1:
        g["x1_dram"] = nc.dram_tensor("x1_out", [T, D], F32, kind="ExternalOutput").ap()
    else:
        g["x1_dram"] = nc.dram_tensor("x1_dram", [T, D], F32).ap()


def build_p1_program(with_moe=False, debug=False):
    import contextlib
    nc = bass.Bass("TRN2", target_bir_lowering=False)
    g = {}
    declare_p1_dram(nc, g, external_x1=True)
    if debug:
        g["dbg_pre"] = nc.dram_tensor("dbg_pre", [3, 128, 8, 512], F32, kind="ExternalOutput").ap()
        g["xn_out"] = nc.dram_tensor("xn_out", [T, D], BF16, kind="ExternalOutput").ap()

    with contextlib.ExitStack() as stack:
        s = S()
        alloc_persist(nc, stack, s)
        sems = {}
        for name in ENGINES + QUEUES:
            sems[name] = stack.enter_context(nc.semaphore(name))
        p = Prog(nc)
        with nc.Block() as block:
            dma(p, "qsp", s.ident_bf[:], g["ident_bf"], writes=("identbf",))
            dma(p, "qsp", s.ident_f32[:], g["ident_f32"], writes=("identf32",))
            with contextlib.ExitStack() as st1:
                alloc_phase1(nc, st1, s)
                build_phase1(nc, p, s, g)
                p.fence()
                p.materialize(block, sems)
            if with_moe:
                with contextlib.ExitStack() as st2:
                    alloc_moe(nc, st2, s)
                    build_moe(nc, p, s, g)
                    p.fence()
                    p.materialize(block, sems)
            else:
                with contextlib.ExitStack() as st2:
                    alloc_phase1(nc, st2, s)
                    build_x1_copy(nc, p, s, g)
                    p.materialize(block, sems)
            p.barrier_all()
            p.materialize(block, sems)
    return nc


def prep_p1_inputs(x_batch, inp):
    d = {}
    d["xb"] = np.ascontiguousarray(x_batch)
    d["Wf_t"] = np.ascontiguousarray(_np(inp["L0_attn_W_f"]).T)
    d["Wi_t"] = np.ascontiguousarray(_np(inp["L0_attn_W_i"]).T)
    d["Wh_t"] = np.ascontiguousarray(_np(inp["L0_attn_W_h"]).T)
    d["b_f"] = np.ascontiguousarray(_np(inp["L0_attn_b_f"]).reshape(8, 128).T)
    d["b_i"] = np.ascontiguousarray(_np(inp["L0_attn_b_i"]).reshape(8, 128).T)
    d["b_h"] = np.ascontiguousarray(_np(inp["L0_attn_b_h"]).reshape(8, 128).T)
    d["w_router"] = np.ascontiguousarray(_np(inp["L0_ffn_w_router"]))
    d["ident_bf"] = np.eye(128, dtype=BF16NP)
    d["ident_f32"] = np.eye(128, dtype=np.float32)
    d["moe_wi"] = np.ascontiguousarray(_np(inp["L0_ffn_wi"]))
    d["moe_wo"] = np.ascontiguousarray(_np(inp["L0_ffn_wo"]))
    d["tri_f32"] = np.triu(np.ones((128, 128), np.float32), 1)
    io8 = np.broadcast_to(np.arange(8, dtype=np.float32), (128, 8)).copy()
    d["iota8"] = io8
    d["iota8m8"] = io8 - 8.0
    d["onescol"] = np.ones((1, 128), np.float32)
    d["ones128"] = np.ones((128, 1), np.float32)
    d["pidx"] = np.arange(128, dtype=np.float32)[:, None].copy()
    tk = (np.arange(128, dtype=np.uint32)[None, :] * 128
          + np.arange(128, dtype=np.uint32)[:, None])
    d["tokidx"] = np.ascontiguousarray(tk)
    return d


_PROG_CACHE = {}


def kernel(**inputs):
    if FUSED:
        return kernel_fused(**inputs)
    x = np.ascontiguousarray(np.asarray(inputs["L0_attn_x"], dtype=np.float32))
    if "p1" not in _PROG_CACHE:
        _PROG_CACHE["p1"] = build_p1_program(with_moe=True)
        _PROG_CACHE["p2"] = build_p2_program()
        _PROG_CACHE["p3"] = build_p3_program()
    d1 = [prep_p1_inputs(x[b], inputs) for b in range(B)]
    res1 = run_bass_kernel_spmd(_PROG_CACHE["p1"], [d1[c // 4] for c in range(8)],
                                core_ids=list(range(8)))
    x1 = [np.asarray(res1.results[4 * b]["x1_out"]) for b in range(B)]
    d2 = [prep_p2_inputs(x1[c // 4], inputs, c % 4) for c in range(8)]
    res2 = run_bass_kernel_spmd(_PROG_CACHE["p2"], d2, core_ids=list(range(8)))
    xn1 = [np.asarray(res2.results[4 * b]["xn1_out"]) for b in range(B)]
    attnbr = [np.asarray(res2.results[c]["attnbr_out"]) for c in range(8)]
    d3 = []
    for c in range(8):
        b, j = c // 4, c % 4
        d = prep_p3_inputs(inputs, j)
        d["xn1_in"] = np.concatenate([xn1[b], np.zeros((128, D), np.float32)], 0)
        d["x1_in"] = x1[b]
        d["attnbr_in"] = attnbr[c]
        d3.append(d)
    res3 = run_bass_kernel_spmd(_PROG_CACHE["p3"], d3, core_ids=list(range(8)))
    out = np.empty((B, T, D), np.float32)
    for c in range(8):
        b, j = c // 4, c % 4
        out[b, CHUNK * j:CHUNK * (j + 1)] = res3.results[c]["out"]
    return out



GAMMA = 0.99
BAND = 4096
VSL = 256


def alloc_phase2a(nc, stack, s):
    sb, _ = _mk(nc, stack, "s2a_")
    s.wq = sb("wq", [128, 8, D], BF16)
    s.wk = sb("wk", [128, 8, D], BF16)
    s.wv = sb("wv", [128, 8, VSL], BF16)
    s.x1sl = sb("x1sl", [128, 4, D], F32)
    s.xn1sl = sb("xn1sl", [128, 4, D], F32)
    s.lnxc2 = sb("lnxc2", [128, D], F32)
    s.stats2 = sb("stats2", [128, 8], F32)
    s.xnT2 = sb("xnT2", [128, 8, 512], BF16)
    s.qstg = sb("qstg", [128, 512], BF16)
    s.kstg = sb("kstg", [128, 512], BF16)
    s.zrow2 = sb("zrow2", [128, D], F32)


def alloc_phase2b(nc, stack, s):
    sb, _ = _mk(nc, stack, "s2b_")
    s.qt = sb("qt", [128, 8, 512], BF16)
    s.kt = [sb(f"kt{i}", [128, 8, 128], BF16) for i in range(2)]
    s.abuf = sb("abuf", [128, 36, 512], BF16)
    s.e0 = sb("e0", [128, 512], F32)
    s.em = sb("em", [128, 4, 512], F32)
    s.ostg = sb("ostg", [128, 512], F32)


def alloc_vbuf(nc, stack, s):
    sb, _ = _mk(nc, stack, "s2v_")
    s.vbuf = sb("vbuf", [128, 128, VSL], BF16)


def build_phase2a(nc, p, s, g):
    NT = 4
    NSLAB = T // 512 if P1_NSLAB is None else P1_NSLAB
    dma(p, "qpool", s.wq[:], g["W_Q"].rearrange("(k p) c -> p k c", p=128),
        writes=("wq",))
    dma(p, "qpool", s.wk[:], g["W_K"].rearrange("(k p) c -> p k c", p=128),
        writes=("wk",))
    dma(p, "qpool", s.wv[:], g["W_Vs"].rearrange("(k p) c -> p k c", p=128),
        writes=("wv",))
    pool(p, "memset", s.zrow2[:], 0.0, writes=("zrow2",))
    dma(p, "qpool", g["xn1_dram"][T:T + 1, :], s.zrow2[0:1, 0:D],
        reads=("zrow2",), writes=(("xn1_dram", "z"),))

    x1_r = g["x1_dram"].rearrange("(n tt p) d -> n p tt d", p=128, tt=NT)
    xn1_r = g["xn1_dram"][0:T, :].rearrange("(n tt p) d -> n p tt d", p=128, tt=NT)
    qt_r = g["qt_dram"].rearrange("(m p) t -> p m t", p=128)
    kt_r = g["kt_dram"].rearrange("(m p) t -> p m t", p=128)

    for sl in range(NSLAB):
        t0 = sl * 512
        dma(p, "qsp", s.x1sl[:], x1_r[sl],
            reads=(("x1", 4 * sl), ("x1", 4 * sl + 1), ("x1", 4 * sl + 2),
                   ("x1", 4 * sl + 3)),
            writes=("x1sl",))
        for tt in range(NT):
            emit_ln(p, s.lnxc2, "lnxc2", s.xn1sl[:, tt], "xn1sl",
                    s.stats2, "lnst2",
                    s.x1sl[:, tt], "x1sl", [(s.xn1sl[:, tt], "xn1sl")])
        dma(p, "qsp", xn1_r[sl], s.xn1sl[:], reads=("xn1sl",),
            writes=(("xn1_dram", sl),))
        for k in range(8):
            items = [(s.pt_f[:, tt * 128:(tt + 1) * 128],
                      s.xn1sl[:, tt, k * 128:(k + 1) * 128], s.ident_f32[:])
                     for tt in range(NT)]
            transpose_group(p, items, reads=("xn1sl", "identf32"),
                            write_key="ptf")
            dve(p, "tensor_copy", s.xnT2[:, k], s.pt_f[:, 0:512],
                reads=("ptf",), writes=(("xnT2", k),))
        rxnT = tuple(("xnT2", k) for k in range(8))
        for which, wsb_, kw, stg, kstg, dr in (
                (0, s.wq, "wq", s.qstg, "qstg", qt_r),
                (1, s.wk, "wk", s.kstg, "kstg", kt_r)):
            for m in range(8):
                bk = m % 2 if which == 0 else 2 + m % 2
                mm_group(p, s.ps_mm[bk][:],
                         [(wsb_[:, k, m * 128:(m + 1) * 128], s.xnT2[:, k])
                          for k in range(8)],
                         reads=rxnT + (kw,), write_key=("psmm", bk))
                dve(p, "tensor_copy", stg[:], s.ps_mm[bk][:],
                    reads=(("psmm", bk),), writes=(kstg,))
                dma(p, "qsp", dr[:, m, t0:t0 + 512], stg[:],
                    reads=(kstg,), writes=((("qtd" if which == 0 else "ktd"), sl, m),))
        for tt in range(NT):
            gt = sl * NT + tt
            mm_group(p, s.ps_r[:, 0:VSL],
                     [(s.xnT2[:, k, tt * 128:(tt + 1) * 128], s.wv[:, k])
                      for k in range(8)],
                     reads=rxnT + ("wv",), write_key="psr")
            dve(p, "tensor_copy", s.vbuf[:, gt, :], s.ps_r[:, 0:VSL],
                reads=("psr",), writes=(("vbuf", gt),))


def _decay_tiles():
    i = np.arange(128, dtype=np.float64)[:, None]
    jj = np.arange(512, dtype=np.float64)[None, :]
    e0 = GAMMA ** (jj - i)
    ems = []
    for delta in (0, -128, -256, -384):
        ex = delta + jj - i
        em = np.where(ex >= 0, GAMMA ** ex, 0.0)
        ems.append(em)
    return (e0.astype(np.float32),
            np.stack(ems, 1).astype(np.float32))


def build_phase2b(nc, p, s, g):
    NSLAB = T // 512 if P1_NSLAB is None else P1_NSLAB
    NQ = NSLAB
    dma(p, "qsp", s.e0[:], g["e0_t"], writes=("e0",))
    dma(p, "qsp", s.em[:], g["em_t"], writes=("em",))
    qt_r = g["qt_dram"].rearrange("(m p) t -> p m t", p=128)
    kt_r = g["kt_dram"].rearrange("(m p) t -> p m t", p=128)
    ot_r = g["outT_dram"].rearrange("(m p) t -> p m t", p=128)

    for q in range(NQ):
        tq0 = 512 * q
        dma(p, "qsp", s.qt[:], qt_r[:, :, tq0:tq0 + 512], 
            reads=tuple(("qtd", sl, m) for sl in range(NSLAB) for m in range(8)),
            writes=("qt",))
        s_tiles = list(range(max(0, tq0 - BAND), tq0 + 512, 128))
        for si, s0 in enumerate(s_tiles):
            kb = si % 2
            dma(p, "qsp", s.kt[kb][:], kt_r[:, :, s0:s0 + 128],
                reads=tuple(("ktd", sl, m) for sl in range(NSLAB) for m in range(8)),
                writes=(("kt", kb),))
            bk = si % 2
            mm_group(p, s.ps_mm[bk][:],
                     [(s.kt[kb][:, k], s.qt[:, k]) for k in range(8)],
                     reads=(("kt", kb), "qt"), write_key=("psmm", bk))
            delta = tq0 - s0
            if delta >= 128:
                dve(p, "scalar_tensor_tensor", s.abuf[:, si, :],
                    s.ps_mm[bk][:], float(GAMMA ** delta), s.e0[:],
                    op0=OP.mult, op1=OP.mult,
                    reads=(("psmm", bk), "e0"), writes=(("abuf", si),))
            else:
                mi = (-delta) // 128
                dve(p, "tensor_tensor", s.abuf[:, si, :],
                    s.ps_mm[bk][:], s.em[:, mi, :], op=OP.mult,
                    reads=(("psmm", bk), "em"), writes=(("abuf", si),))
        if q == 0 and g.get("abuf_out") is not None:
            dma(p, "qsp", g["abuf_out"], s.abuf[:],
                reads=tuple(("abuf", si) for si in range(len(s_tiles))),
                writes=("abufout",))
        for mi in range(2):
            bank = s.ps_mm[2 + mi]
            mm_group(p, bank[:],
                     [(s.vbuf[:, s0 // 128, mi * 128:(mi + 1) * 128],
                       s.abuf[:, si, :])
                      for si, s0 in enumerate(s_tiles)],
                     reads=tuple(("abuf", si) for si in range(len(s_tiles)))
                     + tuple(("vbuf", s0 // 128) for s0 in s_tiles),
                     write_key=("psmm", 2 + mi))
            dve(p, "tensor_copy", s.ostg[:], bank[:],
                reads=(("psmm", 2 + mi),), writes=("ostg",))
            dma(p, "qsp", ot_r[:, mi, tq0:tq0 + 512], s.ostg[:],
                reads=("ostg",), writes=(("otd", q, mi),))


def build_scramble(nc, p, s, g):
    rot = tuple(("otd", q, mi) for q in range(T // 512) for mi in range(2))
    for m in range(16):
        src = g["outT_dram"][0:256, 1024 * m:1024 * (m + 1)]
        dst = g["attnbr_dram"].rearrange("(u s) d -> s u d", s=16)[m]
        dma(p, "qpool", dst, src, reads=rot, writes=(("attnbr", m),))


def declare_p2_dram(nc, g, x1_external):
    if x1_external:
        g["x1_dram"] = nc.dram_tensor("x1_in", [T, D], F32, kind="ExternalInput").ap()
    g["W_Q"] = nc.dram_tensor("W_Q", [D, D], F32, kind="ExternalInput").ap()
    g["W_K"] = nc.dram_tensor("W_K", [D, D], F32, kind="ExternalInput").ap()
    g["W_Vs"] = nc.dram_tensor("W_Vs", [D, VSL], F32, kind="ExternalInput").ap()
    g["e0_t"] = nc.dram_tensor("e0_t", [128, 512], F32, kind="ExternalInput").ap()
    g["em_t"] = nc.dram_tensor("em_t", [128, 4, 512], F32, kind="ExternalInput").ap()
    g["xn1_dram"] = nc.dram_tensor("xn1_dram", [T + 128, D], F32).ap()
    g["qt_dram"] = nc.dram_tensor("qt_dram", [D, T], BF16).ap()
    g["kt_dram"] = nc.dram_tensor("kt_dram", [D, T], BF16).ap()
    g["outT_dram"] = nc.dram_tensor("outT_dram", [VSL, T], F32).ap()


def build_p2_program(debug=False):
    import contextlib
    nc = bass.Bass("TRN2", target_bir_lowering=False)
    g = {}
    g["ident_bf"] = nc.dram_tensor("ident_bf", [128, 128], BF16, kind="ExternalInput").ap()
    g["ident_f32"] = nc.dram_tensor("ident_f32", [128, 128], F32, kind="ExternalInput").ap()
    declare_p2_dram(nc, g, x1_external=True)
    g["attnbr_dram"] = nc.dram_tensor("attnbr_out", [CHUNK, D], F32,
                                      kind="ExternalOutput").ap()
    g["xn1_out"] = nc.dram_tensor("xn1_out", [T, D], F32,
                                  kind="ExternalOutput").ap()
    if debug:
        g["outT_out"] = nc.dram_tensor("outT_out", [VSL, T], F32,
                                       kind="ExternalOutput").ap()
        g["abuf_out"] = nc.dram_tensor("abuf_out", [128, 36, 512], BF16,
                                       kind="ExternalOutput").ap()
        g["qt_out"] = nc.dram_tensor("qt_out", [D, T], BF16, kind="ExternalOutput").ap()
        g["kt_out"] = nc.dram_tensor("kt_out", [D, T], BF16, kind="ExternalOutput").ap()

    with contextlib.ExitStack() as stack:
        s = S()
        alloc_persist(nc, stack, s)
        alloc_vbuf(nc, stack, s)
        sems = {}
        for name in ENGINES + QUEUES:
            sems[name] = stack.enter_context(nc.semaphore(name))
        p = Prog(nc)
        for gt in range(T // 128):
            p.track[("x1", gt)] = [None, []]
        with nc.Block() as block:
            dma(p, "qsp", s.ident_bf[:], g["ident_bf"], writes=("identbf",))
            dma(p, "qsp", s.ident_f32[:], g["ident_f32"], writes=("identf32",))
            with contextlib.ExitStack() as st1:
                alloc_phase2a(nc, st1, s)
                build_phase2a(nc, p, s, g)
                p.fence()
                p.materialize(block, sems)
            with contextlib.ExitStack() as st2:
                alloc_phase2b(nc, st2, s)
                build_phase2b(nc, p, s, g)
                p.fence()
                p.materialize(block, sems)
            build_scramble(nc, p, s, g)
            if g.get("qt_out") is not None:
                dma(p, "qsp", g["qt_out"], g["qt_dram"],
                    reads=tuple(("qtd", sl, m) for sl in range(T // 512) for m in range(8)),
                    writes=("qtout",))
                dma(p, "qsp", g["kt_out"], g["kt_dram"],
                    reads=tuple(("ktd", sl, m) for sl in range(T // 512) for m in range(8)),
                    writes=("ktout",))
            if g.get("outT_out") is not None:
                dma(p, "qsp", g["outT_out"], g["outT_dram"],
                    reads=tuple(("otd", q, mi) for q in range(T // 512)
                                for mi in range(2)),
                    writes=("outTout",))
            dma(p, "qsp", g["xn1_out"], g["xn1_dram"][0:T, :],
                reads=tuple(("xn1_dram", sl) for sl in range(T // 512)),
                writes=("xn1out",))
            p.barrier_all()
            p.materialize(block, sems)
    return nc


def prep_p2_inputs(x1, inp, j):
    d = {}
    d["x1_in"] = np.ascontiguousarray(x1)
    d["W_Q"] = np.ascontiguousarray(_np(inp["L1_attn_W_Q"]))
    d["W_K"] = np.ascontiguousarray(_np(inp["L1_attn_W_K"]))
    d["W_Vs"] = np.ascontiguousarray(_np(inp["L1_attn_W_V"])[:, VSL * j:VSL * (j + 1)])
    e0, em = _decay_tiles()
    d["e0_t"] = e0
    d["em_t"] = np.ascontiguousarray(em)
    d["ident_bf"] = np.eye(128, dtype=BF16NP)
    d["ident_f32"] = np.eye(128, dtype=np.float32)
    return d



DEPTH3 = 6
HEADS = 16
DH = 64
WIN = 256
NWIN = SLABT // WIN
NTI = SLABT // 128
DIN_P = 2816
QK_SCALE = 8.0
P3_PARANOID = False


def alloc_phase3(nc, stack, s):
    sb, _ = _mk(nc, stack, "s3_")
    s.wbig = sb("wbig", [128, 24576], BF16)
    s.wsm = sb("wsm", [128, 8192], BF16)
    s.wqkv = s.wbig[:].rearrange("p (k c) -> p k c", k=8)
    s.w2p = s.wbig[:, 0:22528].rearrange("p (k c) -> p k c", k=22)
    s.wo3 = s.wsm[:].rearrange("p (k c) -> p k c", k=8)
    s.w1ch = [s.wsm[:, 0:4096].rearrange("p (k c) -> p k c", k=8),
              s.wsm[:, 4096:8192].rearrange("p (k c) -> p k c", k=8)]
    s.xt3d = sb("xt3d", [128, 2, 1024], F32)
    s.xt3 = s.xt3d[:, 0, :]
    s.xt3b = sb("xt3b", [128, 1024], BF16)
    s.xd3d = sb("xd3d", [128, 2, 1024], F32)
    s.xd3 = s.xd3d[:, 0, :]
    s.lnxc3 = sb("lnxc3", [128, D], F32)
    s.stats3 = sb("stats3", [128, 8], F32)
    s.xnT3 = sb("xnT3", [128, 8, 128], BF16)
    s.qkv = sb("qkv", [128, 3072], F32)
    s.u8k = sb("u8k", [128, 4096], BF16)
    s.qkb = s.u8k[:, 0:2048]
    s.hs = sb("hs", [128, 24], F32)
    s.rot = sb("rot", [128, 4, 64], F32)
    s.qsk = sb("qsk", [128, 2, 64], F32)
    s.qT = s.u8k[:, 2048:4096].rearrange("p (k c) -> p k c", k=8)
    s.uAB = sb("uAB", [128, 12288], BF16)
    s.kT = [s.uAB[:, 0:2048].rearrange("p (k c) -> p k c", k=8),
            s.uAB[:, 2048:4096].rearrange("p (k c) -> p k c", k=8)]
    s.vb = [s.uAB[:, 4096:6144].rearrange("p (k c) -> p k c", k=2),
            s.uAB[:, 6144:8192].rearrange("p (k c) -> p k c", k=2)]
    s.mks = sb("mks", [128, 2, 2, 512], BF16)
    s.mkw = sb("mkw", [128, 2, 512], BF16)
    s.selv = sb("selv", [128, NWIN], F32)
    s.hkb = sb("hkb", [128, NTI], F32)
    s.attnbd = sb("attnbd", [128, 2, 512], BF16)
    s.attnTd = sb("attnTd", [128, 2, 4, 128], BF16)
    s.attno = sb("attno", [128, 1024], BF16)
    s.headsT = s.uAB[:, 8192:12288].rearrange("p (k c) -> p k c", k=8)
    s.wst = sb("wst", [128, 8, 512], F32)
    s.h2T = sb("h2T", [128, 22, 512], BF16)
    s.sga = sb("sga", [128, 512], F32)
    s.sgg = sb("sgg", [128, 512], F32)
    s.gbt = s.uAB[:, 0:4096].bitcast(F32).rearrange("p (k c) -> p k c", k=2)
    s.b2t = s.uAB[:, 4096:6144].bitcast(F32)
    s.b1t = sb("b1t", [128, 44], F32)
    s.rowst = s.uAB[0:1, 6144:12288].bitcast(F32).rearrange("p (k c) -> p k c", k=3)
    s.xnT512 = s.u8k[:].rearrange("p (k c) -> p k c", k=8)
    s.att1 = sb("att1", [128, 1024], F32)
    s.att2 = sb("att2", [128, 1024], F32)
    s.idx3 = sb("idx3", [128, 1], U32)
    s.x1t = sb("x1t", [128, 1024], F32)
    s.abt = s.att1


def _shifted_tile(p, s, g, src, gen, it, dstbuf, dstkey):
    r0 = 1 + it * 128
    k0 = ("xc", gen, it)
    km = ("xc", gen, it - 1) if it > 0 else ("xcz", gen % 2)
    dma(p, "qsp", dstbuf[:, 0:512], src[r0:r0 + 128, 0:512],
        reads=(k0,), writes=(dstkey,))
    dma(p, "qsp", dstbuf[:, 512:1024], src[r0 - 1:r0 + 127, 512:1024],
        reads=(k0, km), writes=(dstkey,))


def _residual_writeback(p, s, g, dst, gen_out, t4, branch_stage, kstage,
                        src, gen_in):
    for q in range(4):
        it = t4 + q
        r0 = 1 + it * 128
        xd = s.xd3d[:, q % 2, :]
        kxd = ("xd3", q % 2)
        dma(p, "qsp", xd, src[r0:r0 + 128, :],
            reads=(("xc", gen_in, it),), writes=(kxd,))
        for half in range(2):
            items = [(s.pt_f[:, u * 128:(u + 1) * 128],
                      branch_stage[:, half * 4 + u, q * 128:(q + 1) * 128],
                      s.ident_f32[:]) for u in range(4)]
            transpose_group(p, items, reads=(kstage, "identf32"),
                            write_key="ptf")
            dve(p, "tensor_tensor", xd[:, half * 512:(half + 1) * 512],
                xd[:, half * 512:(half + 1) * 512], s.pt_f[:],
                op=OP.add, reads=("ptf", kxd), writes=(kxd,))
        dve(p, "tensor_scalar", xd, xd, s.hkb[:, it:it + 1],
            None, op0=OP.mult, reads=(kxd, "hkb"), writes=(kxd,))
        dma(p, "qsp", dst[r0:r0 + 128, :], xd, reads=(kxd,),
            writes=(("xc", gen_out, it),))


def build_p3_attn(nc, p, s, g, li, src, dst):
    gen = 2 * li
    dma(p, "qpool", s.wqkv,
        g["wqkv3"][li].rearrange("(k p) c -> p k c", p=128), writes=("wbig",))
    dma(p, "qpool", s.wo3,
        g["wo3_w"][li].rearrange("(k p) c -> p k c", p=128),
        writes=(("wsm", 0), ("wsm", 1)))
    dma(p, "qsp", s.qsk[:], g["qsk3"][li], writes=("qsk",))
    pool(p, "memset", s.kT[1][:], 0.0, writes=(("kTr", 1),))
    pool(p, "memset", s.vb[1][:], 0.0, writes=(("vbr", 1),))

    for w in range(NWIN):
        cur = w % 2
        prv = 1 - cur
        for par in range(2):
            dve(p, "scalar_tensor_tensor", s.mkw[:, par, :],
                s.mks[:, 1, par, :], s.selv[:, w:w + 1], s.mks[:, 0, par, :],
                op0=OP.mult, op1=OP.add,
                reads=("mks", "selv"), writes=("mkw",))
        for sub in range(2):
            it = 2 * w + sub
            xt = s.xt3d[:, it % 2, :]
            kxt = ("xt3", it % 2)
            _shifted_tile(p, s, g, src, gen, it, xt, kxt)
            dve(p, "tensor_copy", s.xt3b[:], xt, reads=(kxt,),
                writes=("xt3b",))
            items = [(s.pt_b[:, k * 128:(k + 1) * 128],
                      s.xt3b[:, k * 128:(k + 1) * 128], s.ident_bf[:])
                     for k in range(8)]
            transpose_group(p, items, reads=("xt3b", "identbf"),
                            write_key="ptb")
            dve(p, "tensor_copy", s.xnT3[:],
                s.pt_b[:].rearrange("p (k x) -> p k x", k=8),
                reads=("ptb",), writes=("xnT3",))
            for nch in range(6):
                bk = nch % 4
                mm_group(p, s.ps_mm[bk][:],
                         [(s.xnT3[:, k], s.wqkv[:, k, nch * 512:(nch + 1) * 512])
                          for k in range(8)],
                         reads=("xnT3", "wbig"), write_key=("psmm", bk))
                dve(p, "tensor_copy", s.qkv[:, nch * 512:(nch + 1) * 512],
                    s.ps_mm[bk][:], reads=(("psmm", bk),), writes=("qkv",))
            dma(p, "qsp", s.rot[:], g["rot3"][it], writes=("rot",))
            for which, off, sci, ssi in ((0, 0, 0, 1), (1, 1024, 2, 3)):
                qv = s.qkv[:, off:off + 1024]
                hv = qv.rearrange("p (h d) -> p h d", d=DH)
                dve(p, "scalar_tensor_tensor", s.att1[:], qv, 1.0, qv,
                    op0=OP.mult, op1=OP.mult, reads=("qkv",), writes=("att1",))
                dve(p, "tensor_reduce", s.hs[:, 8:24],
                    s.att1[:].rearrange("p (h d) -> p h d", d=DH),
                    axis=AX.X, op=OP.add, reads=("att1",), writes=("hsn",))
                act(p, s.hs[:, 8:24], s.hs[:, 8:24], AF.Sqrt,
                    reads=("hsn",), writes=("hsn",))
                dve(p, "tensor_scalar", s.hs[:, 8:24], s.hs[:, 8:24], 1e-12,
                    None, op0=OP.max, reads=("hsn",), writes=("hsn",))
                dve(p, "reciprocal", s.hs[:, 8:24], s.hs[:, 8:24],
                    reads=("hsn",), writes=("hsn",))
                dve(p, "tensor_tensor", hv, hv,
                    s.hs[:, 8:24].unsqueeze(2).broadcast_to([128, 16, DH]),
                    op=OP.mult, reads=("qkv", "hsn"), writes=("qkv",))
                dve(p, "tensor_tensor", hv, hv,
                    s.qsk[:, which, :].unsqueeze(1).broadcast_to([128, 16, DH]),
                    op=OP.mult, reads=("qkv", "qsk"), writes=("qkv",))
                cs = s.rot[:, sci, :]
                ss = s.rot[:, ssi, :]
                a1 = s.att1[:].rearrange("p (h d) -> p h d", d=DH)
                h1 = s.att2[:].rearrange("p (h d) -> p h d", d=DH)
                dve(p, "tensor_tensor", a1, hv,
                    cs.unsqueeze(1).broadcast_to([128, 16, DH]),
                    op=OP.mult, reads=("qkv", "rot"), writes=("att1",))
                dve(p, "tensor_tensor", h1[:, :, 0:32], hv[:, :, 32:64],
                    ss[:, 0:32].unsqueeze(1).broadcast_to([128, 16, 32]),
                    op=OP.mult, reads=("qkv", "rot"), writes=("att2",))
                dve(p, "tensor_tensor", h1[:, :, 32:64], hv[:, :, 0:32],
                    ss[:, 32:64].unsqueeze(1).broadcast_to([128, 16, 32]),
                    op=OP.mult, reads=("qkv", "rot"), writes=("att2",))
                dve(p, "tensor_tensor", hv[:, :, 0:32], a1[:, :, 0:32],
                    h1[:, :, 0:32], op=OP.subtract,
                    reads=("att1", "att2"), writes=("qkv",))
                dve(p, "tensor_tensor", hv[:, :, 32:64], a1[:, :, 32:64],
                    h1[:, :, 32:64], op=OP.add,
                    reads=("att1", "att2"), writes=("qkv",))
            if li == 0 and it == 4 and g.get("dbgq") is not None:
                dma(p, "qsp", g["dbgq"], s.qkv[:], reads=("qkv",),
                    writes=("dbgq",))
            dve(p, "tensor_copy", s.vb[cur][:, sub, :],
                s.qkv[:, 2048:3072], reads=("qkv",), writes=(("vbr", cur),))
            dve(p, "tensor_copy", s.qkb, s.qkv[:, 0:2048],
                reads=("qkv",), writes=("qkb",))
            for which, dstT, kdst in ((0, s.qT, "qT"), (1, s.kT[cur], ("kTr", cur))):
                items = [(s.pt_b[:, b * 128:(b + 1) * 128],
                          s.qkb[:, which * 1024 + b * 128:which * 1024 + (b + 1) * 128],
                          s.ident_bf[:]) for b in range(8)]
                transpose_group(p, items, reads=("qkb", "identbf"),
                                write_key="ptb")
                dve(p, "tensor_copy", dstT[:, :, sub * 128:(sub + 1) * 128],
                    s.pt_b[:].rearrange("p (b x) -> p b x", b=8),
                    reads=("ptb",), writes=(kdst,))
        for tt in range(2):
            for h in range(HEADS):
                blk = h // 2
                pb = 64 * (h % 2)
                bk = tt
                def emit_sc(eng, blk=blk, pb=pb, tt=tt, bk=bk, cur=cur, prv=prv):
                    eng.matmul(s.ps_mm[bk][:, 0:256],
                               s.qT[:, blk, tt * 128:(tt + 1) * 128][pb:pb + 64, :],
                               s.kT[prv][:, blk, :][pb:pb + 64, :],
                               start=True, stop=False)
                    eng.matmul(s.ps_mm[bk][:, 256:512],
                               s.qT[:, blk, tt * 128:(tt + 1) * 128][pb:pb + 64, :],
                               s.kT[cur][:, blk, :][pb:pb + 64, :],
                               start=False, stop=False)
                    return eng.matmul(s.ps_mm[bk][:],
                                      s.ident_bf[:], s.mkw[:, tt, :],
                                      start=False, stop=True)
                p.op("pe", emit_sc,
                     reads=("qT", ("kTr", prv), ("kTr", cur), "identbf", "mkw"),
                     writes=(("psmm", bk),))
                if li == 0 and w == 2 and h == 0 and tt == 0 and g.get("dbgs") is not None:
                    dve(p, "tensor_copy", s.wst[:, 0, :], s.ps_mm[bk][:],
                        reads=(("psmm", bk),), writes=("wst",))
                    dma(p, "qsp", g["dbgs"], s.wst[:, 0, :], reads=("wst",),
                        writes=("dbgs",))
                pp = h % 2
                hsc = 4 * pp
                attnb = s.attnbd[:, pp, :]
                attnT = s.attnTd[:, pp]
                ptk = "ptb" if pp == 0 else "psrb"
                ptbank = s.pt_b if pp == 0 else s.ps_r[:].bitcast(BF16)
                dve(p, "tensor_reduce", s.hs[:, hsc:hsc + 1], s.ps_mm[bk][:],
                    axis=AX.X, op=OP.max, negate=True,
                    reads=(("psmm", bk),), writes=(("hs", pp),))
                dve(p, "tensor_scalar", s.hs[:, hsc + 1:hsc + 2], s.hs[:, hsc:hsc + 1],
                    QK_SCALE, None, op0=OP.mult, reads=(("hs", pp),), writes=(("hs", pp),))
                act(p, attnb, s.ps_mm[bk][:], AF.Exp,
                    bias=s.hs[:, hsc + 1:hsc + 2], scale=QK_SCALE,
                    accum_out=s.hs[:, hsc + 2:hsc + 3],
                    reads=(("psmm", bk), ("hs", pp)), writes=(("attnb", pp), ("hs2", pp)))
                dve(p, "reciprocal", s.hs[:, hsc + 3:hsc + 4], s.hs[:, hsc + 2:hsc + 3],
                    reads=(("hs2", pp),), writes=(("hs2", pp),))
                dve(p, "tensor_scalar", attnb, attnb, s.hs[:, hsc + 3:hsc + 4],
                    None, op0=OP.mult, reads=(("attnb", pp), ("hs2", pp)),
                    writes=(("attnb", pp),))
                items = [(ptbank[:, u * 128:(u + 1) * 128],
                          attnb[:, u * 128:(u + 1) * 128], s.ident_bf[:])
                         for u in range(4)]
                transpose_group(p, items, reads=(("attnb", pp), "identbf"),
                                write_key=ptk)
                dve(p, "tensor_copy", attnT,
                    ptbank[:, 0:512].rearrange("p (a x) -> p a x", a=4),
                    reads=(ptk,), writes=(("attnT", pp),))
                avbk = 2 + h // 8
                def emit_av(eng, h=h, avbk=avbk, cur=cur, prv=prv, attnT=attnT):
                    inst = None
                    c0 = 64 * (h % 8)
                    for a in range(4):
                        vtile = s.vb[prv if a < 2 else cur][:, a % 2,
                                                            h * 64:(h + 1) * 64]
                        inst = eng.matmul(
                            s.ps_mm[avbk][:, c0:c0 + 64],
                            attnT[:, a, :], vtile,
                            start=(h % 8 == 0 and a == 0),
                            stop=(h % 8 == 7 and a == 3))
                    return inst
                p.op("pe", emit_av,
                     reads=(("attnT", pp), ("vbr", prv), ("vbr", cur)),
                     writes=(("psmm", avbk),))
            for bb in range(2):
                dve(p, "tensor_copy", s.qkb[:, bb * 512:(bb + 1) * 512].bitcast(BF16)[:, 0:512] if False else s.attno[:, bb * 512:(bb + 1) * 512],
                    s.ps_mm[2 + bb][:], reads=(("psmm", 2 + bb),),
                    writes=("attno",))
            items = [(s.pt_b[:, u * 128:(u + 1) * 128],
                      s.attno[:, u * 128:(u + 1) * 128], s.ident_bf[:])
                     for u in range(8)]
            transpose_group(p, items, reads=("attno", "identbf"),
                            write_key="ptb")
            dve(p, "tensor_copy",
                s.headsT[:, :, cur * 256 + tt * 128:cur * 256 + tt * 128 + 128],
                s.pt_b[:].rearrange("p (k x) -> p k x", k=8),
                reads=("ptb",), writes=("headsT",))
        if w % 2 == 1:
            for m in range(8):
                bk = m % 4
                mm_group(p, s.ps_mm[bk][:],
                         [(s.wo3[:, k, m * 128:(m + 1) * 128], s.headsT[:, k, :])
                          for k in range(8)],
                         reads=("headsT", ("wsm", 0), ("wsm", 1)), write_key=("psmm", bk))
                dve(p, "tensor_copy", s.wst[:, m, :], s.ps_mm[bk][:],
                    reads=(("psmm", bk),), writes=("wst",))
            _residual_writeback(p, s, g, dst, gen + 1, 2 * (w - 1),
                                s.wst[:], "wst", src, gen)


def bcast_row(p, s, dst_ap, dst_key, row_ap, row_key, width=512):
    def emit1(eng):
        return eng.matmul(s.ps_r[:, 0:width], s.onescol32[:],
                          row_ap, start=True, stop=True)
    p.op("pe", emit1, reads=(row_key, "onescol32"), writes=("psr",))
    dve(p, "tensor_copy", dst_ap, s.ps_r[:, 0:width],
        reads=("psr",), writes=(dst_key,))


def build_p3_ffn(nc, p, s, g, li, src, dst):
    gen = 2 * li + 1
    dma(p, "qpool", s.w2p,
        g["w2p3"][li].rearrange("(k p) c -> p k c", p=128), writes=("wbig",))
    dma(p, "qsp", s.rowst[:], g["ffnrows"][li], writes=("rowst",))
    dma(p, "qsp", s.b1t[:], g["b1_3"][li], writes=("b1t",))
    for ci in range(2):
        for ri, (dstap, dk) in enumerate(((s.gbt[:, 0, :], "gbt"),
                                          (s.gbt[:, 1, :], "gbt"),
                                          (s.b2t[:, :], "b2t"))):
            bcast_row(p, s, dstap[:, ci * 512:(ci + 1) * 512], dk,
                      s.rowst[0:1, ri, ci * 512:(ci + 1) * 512], "rowst")

    w1_r = g["w1p3"][li].rearrange("(k p) c -> p k c", p=128)
    for t4 in range(0, NTI, 4):
        for q in range(4):
            it = t4 + q
            xt = s.xt3d[:, q % 2, :]
            kxt = ("xt3", q % 2)
            _shifted_tile(p, s, g, src, gen, it, xt, kxt)
            emit_ln(p, s.lnxc3, "lnxc3", s.att1, "att1", s.stats3, "lnst3",
                    xt, kxt, [(xt, kxt)])
            dve(p, "tensor_tensor", xt, xt, s.gbt[:, 0, :],
                op=OP.mult, reads=(kxt, "gbt"), writes=(kxt,))
            dve(p, "tensor_tensor", xt, xt, s.gbt[:, 1, :],
                op=OP.add, reads=(kxt, "gbt"), writes=(kxt,))
            dve(p, "tensor_copy", s.xt3b[:], xt, reads=(kxt,),
                writes=("xt3b",))
            items = [(s.pt_b[:, k * 128:(k + 1) * 128],
                      s.xt3b[:, k * 128:(k + 1) * 128], s.ident_bf[:])
                     for k in range(8)]
            transpose_group(p, items, reads=("xt3b", "identbf"),
                            write_key="ptb")
            dve(p, "tensor_copy", s.xnT512[:, :, q * 128:(q + 1) * 128],
                s.pt_b[:].rearrange("p (k x) -> p k x", k=8),
                reads=("ptb",), writes=("qkb", "qT"))
        for ci in range(11):
            wb = s.w1ch[ci % 2]
            kwb = ("wsm", ci % 2)
            dma(p, "qpool", wb, w1_r[:, :, ci * 512:(ci + 1) * 512],
                writes=(kwb,))
            for u2 in range(2):
                u = 2 * ci + u2
                ba, bg = 2 * u2, 2 * u2 + 1
                mm_group(p, s.ps_mm[ba][:],
                         [(wb[:, k, u2 * 256:u2 * 256 + 128], s.xnT512[:, k, :])
                          for k in range(8)],
                         reads=(kwb, "qkb", "qT"), write_key=("psmm", ba))
                mm_group(p, s.ps_mm[bg][:],
                         [(wb[:, k, u2 * 256 + 128:u2 * 256 + 256], s.xnT512[:, k, :])
                          for k in range(8)],
                         reads=(kwb, "qkb", "qT"), write_key=("psmm", bg))
                dve(p, "tensor_scalar", s.sga[:], s.ps_mm[ba][:],
                    s.b1t[:, u:u + 1], None, op0=OP.add,
                    reads=(("psmm", ba), "b1t"), writes=("sga",))
                dve(p, "tensor_scalar", s.sgg[:], s.ps_mm[bg][:],
                    s.b1t[:, 22 + u:23 + u], None, op0=OP.add,
                    reads=(("psmm", bg), "b1t"), writes=("sgg",))
                act(p, s.sgg[:], s.sgg[:], AF.Gelu, reads=("sgg",),
                    writes=("sgg",))
                dve(p, "tensor_tensor", s.h2T[:, u, :], s.sga[:], s.sgg[:],
                    op=OP.mult, reads=("sga", "sgg"), writes=(("h2T", u),))
        for q in range(4):
            it = t4 + q
            for n in range(2):
                bk = n
                mm_group(p, s.ps_mm[bk][:],
                         [(s.h2T[:, k, q * 128:(q + 1) * 128],
                           s.w2p[:, k, n * 512:(n + 1) * 512])
                          for k in range(22)],
                         reads=tuple(("h2T", k) for k in range(22)) + ("wbig",),
                         write_key=("psmm", bk))
            r0 = 1 + it * 128
            xd = s.xd3d[:, q % 2, :]
            kxd = ("xd3", q % 2)
            dma(p, "qsp", xd, src[r0:r0 + 128, :],
                reads=(("xc", gen, it),), writes=(kxd,))
            for n in range(2):
                dve(p, "tensor_tensor", xd[:, n * 512:(n + 1) * 512],
                    xd[:, n * 512:(n + 1) * 512], s.ps_mm[n][:],
                    op=OP.add, reads=(("psmm", n), kxd), writes=(kxd,))
            dve(p, "tensor_tensor", xd, xd, s.b2t[:],
                op=OP.add, reads=(kxd, "b2t"), writes=(kxd,))
            dve(p, "tensor_scalar", xd, xd, s.hkb[:, it:it + 1],
                None, op0=OP.mult, reads=(kxd, "hkb"), writes=(kxd,))
            dma(p, "qsp", dst[r0:r0 + 128, :], xd, reads=(kxd,),
                writes=(("xc", gen + 1, it),))


def build_phase3(nc, p, s, g):
    xcb = [g["xc_dram"], g["xd_dram"]]
    dma(p, "qsp", s.mks[:], g["mask3"], writes=("mks",))
    dma(p, "qsp", s.selv[:], g["selv"], writes=("selv",))
    dma(p, "qsp", s.hkb[:], g["hkb"], writes=("hkb",))
    pool(p, "memset", s.xt3[:], 0.0, writes=(("xt3", 0),))
    for buf in range(2):
        dma(p, "qpool", xcb[buf][0:1, :], s.xt3[0:1, 0:D], reads=(("xt3", 0),),
            writes=(("xcz", buf),))
    for it in range(NTI):
        dma(p, "qsp", s.idx3[:], g["idx_slab"][it].unsqueeze(1), writes=("idx3",))
        gather(p, s.xt3[:], g["xn1_dram"], s.idx3[:, 0:1],
               reads=("idx3", ("xn1_dram", "z"))
               + tuple(("xn1_dram", sl) for sl in range(T // 512)),
               writes=(("xt3", 0),))
        dve(p, "tensor_scalar", s.xt3[:], s.xt3[:], s.hkb[:, it:it + 1],
            None, op0=OP.mult, reads=(("xt3", 0), "hkb"), writes=(("xt3", 0),))
        dma(p, "qsp", xcb[0][1 + it * 128:1 + (it + 1) * 128, :], s.xt3[:],
            reads=(("xt3", 0),), writes=(("xc", 0, it),))
    if g.get("dbg3") is not None:
        dma(p, "qpool", g["dbg3"][0], xcb[0][:, :],
            reads=tuple(("xc", 0, it) for it in range(NTI)), writes=("d30",))
    for li in range(DEPTH3):
        build_p3_attn(nc, p, s, g, li, xcb[0], xcb[1])
        p.fence()
        if li == 0 and g.get("dbg3") is not None:
            dma(p, "qpool", g["dbg3"][1], xcb[1][:, :],
                reads=tuple(("xc", 1, it) for it in range(NTI)), writes=("d31",))
        build_p3_ffn(nc, p, s, g, li, xcb[1], xcb[0])
        p.fence()
        if li == 0 and g.get("dbg3") is not None:
            dma(p, "qpool", g["dbg3"][2], xcb[0][:, :],
                reads=tuple(("xc", 2, it) for it in range(NTI)), writes=("d32",))


def build_final(nc, p, s, g):
    for it in range(CHUNK // 128):
        dma(p, "qsp", s.idx3[:], g["idx_own"][it].unsqueeze(1), writes=("idx3",))
        gather(p, s.x1t[:], g["x1_dram"], s.idx3[:, 0:1],
               reads=("idx3",) + tuple(("x1", gt) for gt in range(T // 128)),
               writes=("x1t",))
        dma(p, "qsp", s.abt[:],
            g["attnbr_dram"][it * 128:(it + 1) * 128, :],
            reads=tuple(("attnbr", m) for m in range(16)), writes=("att1",))
        r0 = 1 + (HALO // 128 + it) * 128
        dma(p, "qsp", s.xd3[:], g["xc_dram"][r0:r0 + 128, :],
            reads=(("xc", 2 * DEPTH3, HALO // 128 + it),), writes=(("xd3", 0),))
        dve(p, "tensor_tensor", s.x1t[:], s.x1t[:], s.abt[:], op=OP.add,
            reads=("x1t", "att1"), writes=("x1t",))
        dve(p, "tensor_tensor", s.x1t[:], s.x1t[:], s.xd3[:], op=OP.add,
            reads=("x1t", ("xd3", 0)), writes=("x1t",))
        dma(p, "qsp", g["out_dram"][it * 128:(it + 1) * 128, :], s.x1t[:],
            reads=("x1t",), writes=(("out", it),))


def declare_p3_dram(nc, g, xn1_external):
    if xn1_external:
        g["xn1_dram"] = nc.dram_tensor("xn1_in", [T + 128, D], F32,
                                       kind="ExternalInput").ap()
    g["wqkv3"] = nc.dram_tensor("wqkv3", [DEPTH3, D, 3072], F32, kind="ExternalInput").ap()
    g["wo3_w"] = nc.dram_tensor("wo3_w", [DEPTH3, D, D], F32, kind="ExternalInput").ap()
    g["w1p3"] = nc.dram_tensor("w1p3", [DEPTH3, D, 2 * DIN_P], F32, kind="ExternalInput").ap()
    g["w2p3"] = nc.dram_tensor("w2p3", [DEPTH3, DIN_P, D], F32, kind="ExternalInput").ap()
    g["qsk3"] = nc.dram_tensor("qsk3", [DEPTH3, 128, 2, DH], F32, kind="ExternalInput").ap()
    g["rot3"] = nc.dram_tensor("rot3", [NTI, 128, 4, DH], F32, kind="ExternalInput").ap()
    g["mask3"] = nc.dram_tensor("mask3", [128, 2, 2, 512], BF16, kind="ExternalInput").ap()
    g["selv"] = nc.dram_tensor("selv", [128, NWIN], F32, kind="ExternalInput").ap()
    g["hkb"] = nc.dram_tensor("hkb", [128, NTI], F32, kind="ExternalInput").ap()
    g["idx_slab"] = nc.dram_tensor("idx_slab", [NTI, 128], U32, kind="ExternalInput").ap()
    g["idx_own"] = nc.dram_tensor("idx_own", [CHUNK // 128, 128], U32, kind="ExternalInput").ap()
    g["ffnrows"] = nc.dram_tensor("ffnrows", [DEPTH3, 1, 3, D], F32, kind="ExternalInput").ap()
    g["b1_3"] = nc.dram_tensor("b1_3", [DEPTH3, 128, 44], F32, kind="ExternalInput").ap()
    g["xc_dram"] = nc.dram_tensor("xc_dram", [1 + SLABT, D], F32).ap()
    g["xd_dram"] = nc.dram_tensor("xd_dram", [1 + SLABT, D], F32).ap()


def build_p3_program(debug=False):
    import contextlib
    nc = bass.Bass("TRN2", target_bir_lowering=False)
    g = {}
    g["ident_bf"] = nc.dram_tensor("ident_bf", [128, 128], BF16, kind="ExternalInput").ap()
    g["ident_f32"] = nc.dram_tensor("ident_f32", [128, 128], F32, kind="ExternalInput").ap()
    g["onescol"] = nc.dram_tensor("onescol", [1, 128], F32, kind="ExternalInput").ap()
    declare_p3_dram(nc, g, xn1_external=True)
    g["x1_dram"] = nc.dram_tensor("x1_in", [T, D], F32, kind="ExternalInput").ap()
    g["attnbr_dram"] = nc.dram_tensor("attnbr_in", [CHUNK, D], F32, kind="ExternalInput").ap()
    g["out_dram"] = nc.dram_tensor("out", [CHUNK, D], F32, kind="ExternalOutput").ap()
    if debug:
        g["dbg3"] = nc.dram_tensor("dbg3", [3, 1 + SLABT, D], F32, kind="ExternalOutput").ap()
        g["dbgq"] = nc.dram_tensor("dbgq", [128, 3072], F32, kind="ExternalOutput").ap()
        g["dbgh"] = nc.dram_tensor("dbgh", [128, 8, 256], BF16, kind="ExternalOutput").ap()
        g["dbgs"] = nc.dram_tensor("dbgs", [128, 512], F32, kind="ExternalOutput").ap()
        g["dbga"] = nc.dram_tensor("dbga", [128, 512], BF16, kind="ExternalOutput").ap()
        g["dbgo"] = nc.dram_tensor("dbgo", [128, 256], F32, kind="ExternalOutput").ap()
        g["dbgv"] = nc.dram_tensor("dbgv", [2, 128, 2, 1024], BF16, kind="ExternalOutput").ap()

    with contextlib.ExitStack() as stack:
        s = S()
        alloc_persist(nc, stack, s)
        sems = {}
        for name in ENGINES + QUEUES:
            sems[name] = stack.enter_context(nc.semaphore(name))
        p = Prog(nc)
        with nc.Block() as block:
            dma(p, "qsp", s.ident_bf[:], g["ident_bf"], writes=("identbf",))
            dma(p, "qsp", s.ident_f32[:], g["ident_f32"], writes=("identf32",))
            dma(p, "qsp", s.onescol32[:], g["onescol"], writes=("onescol32",))
            with contextlib.ExitStack() as st1:
                alloc_phase3(nc, st1, s)
                build_phase3(nc, p, s, g)
                build_final(nc, p, s, g)
                p.fence()
                p.materialize(block, sems)
            p.barrier_all()
            p.materialize(block, sems)
    return nc


def _rotary_tables_np():
    n = T
    half = np.arange(0, DH, 2, dtype=np.float64)
    inv_freq = 1.0 / 10000.0 ** (half / DH)
    t = np.arange(n, dtype=np.float64)
    freqs = t[:, None] * inv_freq[None, :]
    freqs = np.concatenate([freqs, freqs], -1)
    base_scale = (half + 0.4 * DH) / (1.4 * DH)
    power = (t - n // 2) / (WIN // 2)
    scale = base_scale[None, :] ** power[:, None]
    scale = np.concatenate([scale, scale], -1)
    cos, sin = np.cos(freqs), np.sin(freqs)
    out = np.stack([cos * scale, sin * scale, cos / scale, sin / scale], 1)
    return out.astype(np.float32)


def _mask_tiles_np():
    out = np.zeros((128, 2, 2, 512), np.float32)
    pq = np.arange(128)
    jj = np.arange(512)
    for par in range(2):
        tq = 128 * par + pq
        allowed = (jj[None, :] >= tq[:, None]) & (jj[None, :] <= tq[:, None] + 256)
        out[:, 0, par, :] = np.where(allowed, 0.0, NEG)
        out[:, 1, par, :] = np.where(jj[None, :] < 256, NEG, 0.0)
    return out.astype(BF16NP)


def prep_p3_inputs(inp, j):
    d = {}
    d["ident_bf"] = np.eye(128, dtype=BF16NP)
    d["ident_f32"] = np.eye(128, dtype=np.float32)
    d["onescol"] = np.ones((1, 128), np.float32)
    d["wqkv3"] = np.ascontiguousarray(_np(inp["L1_ffn_Wqkv"]))
    d["wo3_w"] = np.ascontiguousarray(_np(inp["L1_ffn_Wo"]))
    W1 = _np(inp["L1_ffn_W1"])
    b1 = _np(inp["L1_ffn_b1"])
    W2 = _np(inp["L1_ffn_W2"])
    W1q = np.zeros((DEPTH3, D, 2 * DIN_P), np.float32)
    b1q = np.zeros((DEPTH3, 128, 44), np.float32)
    for u in range(22):
        c0, c1 = u * 128, min(2730, (u + 1) * 128)
        w = c1 - c0
        W1q[:, :, 256 * u:256 * u + w] = W1[:, :, c0:c1]
        W1q[:, :, 256 * u + 128:256 * u + 128 + w] = W1[:, :, 2730 + c0:2730 + c1]
        b1q[:, 0:w, u] = b1[:, c0:c1].T.reshape(w, DEPTH3).T.T if False else 0
        b1q[:, :, u][:, 0:w] = b1[:, c0:c1]
        b1q[:, :, 22 + u][:, 0:w] = b1[:, 2730 + c0:2730 + c1]
    d["w1p3"] = W1q
    d["b1_3"] = np.ascontiguousarray(np.transpose(b1q, (0, 2, 1))) if False else b1q
    W2q = np.zeros((DEPTH3, DIN_P, D), np.float32)
    W2q[:, 0:2730, :] = 0
    for u in range(22):
        c0, c1 = u * 128, min(2730, (u + 1) * 128)
        W2q[:, u * 128:u * 128 + (c1 - c0), :] = W2[:, c0:c1, :]
    d["w2p3"] = W2q
    qs = _np(inp["L1_ffn_q_scale"]).astype(np.float32)
    ks = _np(inp["L1_ffn_k_scale"]).astype(np.float32)
    qsk = np.zeros((DEPTH3, 128, 2, DH), np.float32)
    qsk[:, :, 0, :] = qs[:, None, :]
    qsk[:, :, 1, :] = ks[:, None, :]
    d["qsk3"] = qsk
    rot = _rotary_tables_np()
    start = 4096 * j - HALO
    pos = np.clip(np.arange(start, start + SLABT), 0, T - 1)
    d["rot3"] = np.ascontiguousarray(
        rot[pos].reshape(NTI, 128, 4, DH))
    d["mask3"] = _mask_tiles_np()
    selv = np.zeros((128, NWIN), np.float32)
    gw0 = (0 - start) // WIN
    if 0 <= gw0 < NWIN and start <= 0:
        selv[:, gw0] = 1.0
    d["selv"] = selv
    hk = np.ones((128, NTI), np.float32)
    if j == 0:
        for it in range(NTI):
            tpos = start + it * 128 + np.arange(128)
            hk[:, it] = (tpos >= 0).astype(np.float32)
    d["hkb"] = hk
    idx = np.empty((NTI, 128), np.uint32)
    for it in range(NTI):
        tpos = start + it * 128 + np.arange(128)
        idx[it] = np.where((tpos >= 0) & (tpos < T), tpos, T).astype(np.uint32)
    d["idx_slab"] = idx
    d["idx_own"] = (4096 * j + np.arange(CHUNK).reshape(CHUNK // 128, 128)).astype(np.uint32)
    gb = np.zeros((DEPTH3, 1, 3, D), np.float32)
    gb[:, 0, 0, :] = _np(inp["L1_ffn_ln_g"])
    gb[:, 0, 1, :] = _np(inp["L1_ffn_ln_b"])
    gb[:, 0, 2, :] = _np(inp["L1_ffn_b2"])
    d["ffnrows"] = gb
    return d



def build_full_program():
    import contextlib
    nc = bass.Bass("TRN2", target_bir_lowering=False)
    g = {}
    declare_p1_dram(nc, g, external_x1=False)
    declare_p2_dram(nc, g, x1_external=False)
    declare_p3_dram(nc, g, xn1_external=False)
    g["attnbr_dram"] = nc.dram_tensor("attnbr_dram", [CHUNK, D], F32).ap()
    g["out_dram"] = nc.dram_tensor("out", [CHUNK, D], F32, kind="ExternalOutput").ap()

    with contextlib.ExitStack() as stack:
        s = S()
        alloc_persist(nc, stack, s)
        sems = {}
        for name in ENGINES + QUEUES:
            sems[name] = stack.enter_context(nc.semaphore(name))
        p = Prog(nc)
        with nc.Block() as block:
            dma(p, "qsp", s.ident_bf[:], g["ident_bf"], writes=("identbf",))
            dma(p, "qsp", s.ident_f32[:], g["ident_f32"], writes=("identf32",))
            dma(p, "qsp", s.onescol32[:], g["onescol"], writes=("onescol32",))
            with contextlib.ExitStack() as st1:
                alloc_phase1(nc, st1, s)
                build_phase1(nc, p, s, g)
                p.fence()
                p.materialize(block, sems)
            with contextlib.ExitStack() as st1b:
                alloc_moe(nc, st1b, s)
                build_moe(nc, p, s, g)
                p.fence()
                p.materialize(block, sems)
            with contextlib.ExitStack() as stv:
                alloc_vbuf(nc, stv, s)
                with contextlib.ExitStack() as st2a:
                    alloc_phase2a(nc, st2a, s)
                    build_phase2a(nc, p, s, g)
                    p.fence()
                    p.materialize(block, sems)
                with contextlib.ExitStack() as st2b:
                    alloc_phase2b(nc, st2b, s)
                    build_phase2b(nc, p, s, g)
                    p.fence()
                    p.materialize(block, sems)
            build_scramble(nc, p, s, g)
            with contextlib.ExitStack() as st3:
                alloc_phase3(nc, st3, s)
                build_phase3(nc, p, s, g)
                build_final(nc, p, s, g)
                p.fence()
                p.materialize(block, sems)
            p.barrier_all()
            p.materialize(block, sems)
    return nc


def prep_full_inputs(x_batch, inputs, j):
    d = prep_p1_inputs(x_batch, inputs)
    d["W_Q"] = np.ascontiguousarray(_np(inputs["L1_attn_W_Q"]))
    d["W_K"] = np.ascontiguousarray(_np(inputs["L1_attn_W_K"]))
    d["W_Vs"] = np.ascontiguousarray(_np(inputs["L1_attn_W_V"])[:, VSL * j:VSL * (j + 1)])
    e0, em = _decay_tiles()
    d["e0_t"] = e0
    d["em_t"] = np.ascontiguousarray(em)
    d3 = prep_p3_inputs(inputs, j)
    for k in ("ident_bf", "ident_f32", "onescol"):
        d3.pop(k, None)
    d.update(d3)
    d["onescol"] = np.ones((1, 128), np.float32)
    return d


FUSED = True


def kernel_fused(**inputs):
    x = np.ascontiguousarray(np.asarray(inputs["L0_attn_x"], dtype=np.float32))
    if "full" not in _PROG_CACHE:
        _PROG_CACHE["full"] = build_full_program()
    in_maps = [prep_full_inputs(x[c // 4], inputs, c % 4) for c in range(8)]
    res = run_bass_kernel_spmd(_PROG_CACHE["full"], in_maps,
                               core_ids=list(range(8)))
    out = np.empty((B, T, D), np.float32)
    for c in range(8):
        b, j = c // 4, c % 4
        out[b, CHUNK * j:CHUNK * (j + 1)] = res.results[c]["out"]
    return out


# version 04
# speedup vs baseline: 1.0417x; 1.0004x over previous
import numpy as np
import ml_dtypes
import concourse.bass as bass
import concourse.mybir as mybir
from concourse.bass_utils import run_bass_kernel_spmd
from concourse.alu_op_type import AluOpType as OP

F32 = mybir.dt.float32
BF16 = mybir.dt.bfloat16
U32 = mybir.dt.uint32
AF = mybir.ActivationFunctionType
AX = mybir.AxisListType

BF16NP = ml_dtypes.bfloat16

B = 2
T = 16384
D = 1024
CHUNK = 4096
HALO = 2048
SLABT = HALO + CHUNK
NEG = -1e30


ENGINES = ("pe", "dve", "act", "pool", "sp")
QUEUES = ("qsp", "qpool", "qact")
ENG_OF_Q = {"qsp": "sp", "qpool": "pool", "qact": "act"}


class Prog:
    def __init__(self, nc, debug_serial=False):
        self.nc = nc
        self.ops = {e: [] for e in ENGINES}
        self.count = dict.fromkeys(ENGINES, 0)
        self.qcount = dict.fromkeys(QUEUES, 0)
        self.track = {}
        self.waited = {e: {} for e in ENGINES}
        self.debug_serial = debug_serial
        self.act_last_func = None

    def _resolve(self, engine, reads, writes):
        waits = {}

        def add(tok):
            if tok is None:
                return
            sem, v = tok
            if waits.get(sem, 0) < v:
                waits[sem] = v

        for k in reads:
            ent = self.track.get(k)
            if ent:
                add(ent[0])
        for k in writes:
            ent = self.track.get(k)
            if ent:
                add(ent[0])
                for t in ent[1]:
                    add(t)
        out = []
        w = self.waited[engine]
        for sem, v in sorted(waits.items()):
            if w.get(sem, 0) < v:
                out.append((sem, v))
                w[sem] = v
        return out

    def _commit(self, tok, reads, writes):
        for k in writes:
            self.track[k] = [tok, []]
        for k in reads:
            ent = self.track.get(k)
            if ent is None:
                self.track[k] = [None, [tok]]
            else:
                ent[1].append(tok)

    def op(self, engine, emit, reads=(), writes=(), act_func=None):
        waits = self._resolve(engine, reads, writes)
        if engine == "act":
            if (self.act_last_func is not None and act_func != self.act_last_func
                    and self.count["act"] > 0):
                v = self.count["act"]
                if self.waited["act"].get("act", 0) < v:
                    waits.append(("act", v))
                    self.waited["act"]["act"] = v
            self.act_last_func = act_func
        if self.debug_serial and self.count[engine] > 0:
            v = self.count[engine]
            if self.waited[engine].get(engine, 0) < v:
                waits.append((engine, v))
                self.waited[engine][engine] = v
        self.count[engine] += 1
        tok = (engine, self.count[engine])
        self.ops[engine].append((waits, emit, tok, 1))
        self._commit(tok, reads, writes)
        return tok

    def dma(self, queue, emit, reads=(), writes=()):
        eng = ENG_OF_Q[queue]
        waits = self._resolve(eng, reads, writes)
        self.qcount[queue] += 16
        tok = (queue, self.qcount[queue])
        self.ops[eng].append((waits, emit, tok, 16))
        self._commit(tok, reads, writes)
        return tok

    def barrier_all(self, engine="sp"):
        waits = []
        for e in ENGINES:
            if e != engine and self.count[e] > 0:
                waits.append((e, self.count[e]))
        for q in QUEUES:
            if self.qcount[q] > 0:
                waits.append((q, self.qcount[q]))
        w = self.waited[engine]
        waits = [(s, v) for (s, v) in waits if w.get(s, 0) < v]
        for s, v in waits:
            w[s] = v
        self.ops[engine].append((waits, None, None, 0))

    def fence(self):
        snap_e = dict(self.count)
        snap_q = dict(self.qcount)
        for e in ENGINES:
            waits = [(o, snap_e[o]) for o in ENGINES if o != e and snap_e[o] > 0]
            waits += [(q, snap_q[q]) for q in QUEUES if snap_q[q] > 0]
            w = self.waited[e]
            waits = [(s_, v) for (s_, v) in waits if w.get(s_, 0) < v]
            for s_, v in waits:
                w[s_] = v
            if waits:
                self.ops[e].append((waits, None, None, 0))

    def materialize(self, block, sems):
        def run(ename, eng):
            for waits, emit, tok, inc in self.ops[ename]:
                for sem, v in waits:
                    eng.wait_ge(sems[sem], v)
                if emit is None:
                    continue
                inst = emit(eng)
                assert inst is not None, f"op on {ename} returned no instruction"
                inst.then_inc(sems[tok[0]], inc)
            self.ops[ename] = []

        for ename, starter in (("pe", block.tensor), ("dve", block.vector),
                               ("act", block.scalar), ("pool", block.gpsimd),
                               ("sp", block.sync)):
            if self.ops[ename]:
                starter(lambda eng, ename=ename: run(ename, eng))



def mm_group(p, ps_ap, pairs, reads, write_key, accumulate=False):
    def emit(eng):
        inst = None
        n = len(pairs)
        for i, (lhsT, rhs) in enumerate(pairs):
            inst = eng.matmul(ps_ap, lhsT, rhs,
                              start=(i == 0 and not accumulate),
                              stop=(i == n - 1))
        return inst
    return p.op("pe", emit, reads=reads, writes=(write_key,))


def transpose_group(p, items, reads, write_key):
    def emit(eng):
        inst = None
        for dst, src, ident in items:
            inst = eng.transpose(dst, src, ident)
        return inst
    return p.op("pe", emit, reads=reads, writes=(write_key,))


def dve(p, name, *a, reads=(), writes=(), **kw):
    def emit(eng):
        return getattr(eng, name)(*a, **kw)
    return p.op("dve", emit, reads=reads, writes=writes)


def pool(p, name, *a, reads=(), writes=(), **kw):
    def emit(eng):
        return getattr(eng, name)(*a, **kw)
    return p.op("pool", emit, reads=reads, writes=writes)


def act(p, out, in_, func, reads=(), writes=(), **kw):
    def emit(eng):
        return eng.activation(out, in_, func, **kw)
    return p.op("act", emit, reads=reads, writes=writes, act_func=func)


def dma(p, queue, out, in_, reads=(), writes=(), **kw):
    def emit(eng):
        return eng.dma_start(out=out, in_=in_, **kw)
    return p.dma(queue, emit, reads=reads, writes=writes)


def gather(p, out, src, idx_ap, reads=(), writes=()):
    def emit(eng):
        return eng.indirect_dma_start(
            out=out, out_offset=None, in_=src,
            in_offset=bass.IndirectOffsetOnAxis(ap=idx_ap, axis=0))
    return p.dma("qpool", emit, reads=reads, writes=writes)


def scatter(p, dst, dst_idx_ap, src, reads=(), writes=()):
    def emit(eng):
        return eng.indirect_dma_start(
            out=dst, out_offset=bass.IndirectOffsetOnAxis(ap=dst_idx_ap, axis=0),
            in_=src, in_offset=None)
    return p.dma("qpool", emit, reads=reads, writes=writes)



def emit_ln(p, xc_buf, kxc, sq_buf, ksq, stat, kst, x_ap, key_in, outs):
    nmean = stat[:, 0:1]
    ssq = stat[:, 1:2]
    var = stat[:, 2:3]
    rstd = stat[:, 3:4]
    sq = stat[:, 4:5]
    xc = xc_buf[:, 0:D]
    dve(p, "tensor_reduce", nmean, x_ap, axis=AX.X, op=OP.add, negate=True,
        reads=(key_in,), writes=((kst, "nmean"),))
    dve(p, "tensor_scalar", nmean, nmean, 1.0 / D, None, op0=OP.mult,
        reads=((kst, "nmean"),), writes=((kst, "nmean"),))
    dve(p, "tensor_scalar", xc, x_ap, nmean, None, op0=OP.add,
        reads=(key_in, (kst, "nmean")), writes=(kxc,))
    dve(p, "scalar_tensor_tensor", sq_buf[:, 0:D], xc, 1.0, xc,
        op0=OP.mult, op1=OP.mult, accum_out=ssq,
        reads=(kxc,), writes=(ksq, (kst, "ssq")))
    dve(p, "tensor_scalar", var, ssq, 1.0 / D, 1e-5, op0=OP.mult, op1=OP.add,
        reads=((kst, "ssq"),), writes=((kst, "var"),))
    act(p, sq, var, AF.Sqrt, reads=((kst, "var"),), writes=((kst, "sqv"),))
    dve(p, "reciprocal", rstd, sq, reads=((kst, "sqv"),), writes=((kst, "rstd"),))
    for out_ap, out_key in outs:
        dve(p, "tensor_scalar", out_ap, xc, rstd, None, op0=OP.mult,
            reads=(kxc, (kst, "rstd")), writes=(out_key,))


class S:
    pass


P1_NSLAB = None


def build_phase1(nc, p, s, g):
    SLAB = 512
    NT = SLAB // 128
    NSLAB = T // SLAB if P1_NSLAB is None else P1_NSLAB

    for wname, wap in (("wf", g["Wf_t"]), ("wi", g["Wi_t"]), ("wh", g["Wh_t"])):
        dma(p, "qpool", getattr(s, wname)[:],
            wap.rearrange("(k p) e -> p k e", p=128),
            writes=(wname,))
    dma(p, "qsp", s.wr32[:], g["w_router"].rearrange("(k p) e -> p k e", p=128),
        writes=("wr32",))
    for bname, bap in (("bfb", g["b_f"]), ("bib", g["b_i"]), ("bhb", g["b_h"])):
        dma(p, "qsp", getattr(s, bname)[:], bap, writes=(bname,))
    pool(p, "memset", s.zrow_bf[:], 0.0, writes=("zrowbf",))
    dma(p, "qpool", g["xn_dram"][T:T + 1, :], s.zrow_bf[0:1, 0:D],
        reads=("zrowbf",), writes=(("xn_dram", "z"),))
    pool(p, "memset", s.state[:], 0.0, writes=tuple(("state", e) for e in range(8)))

    pre = [s.preF, s.preI, s.preH]
    wsb = [s.wf, s.wi, s.wh]
    wk = ["wf", "wi", "wh"]
    bias = [s.bfb, s.bib, s.bhb]
    biask = ["bfb", "bib", "bhb"]
    kF, kI, kH = "preF", "preI", "preH"
    k1, k2 = "S1", "S2"
    kx, kxn, kxnT = "xsl", "xnsl", "xnT"

    xb_r = g["xb"].rearrange("(n tt p) d -> n p tt d", p=128, tt=NT)
    xn_r = g["xn_dram"][0:T, :].rearrange("(n tt p) d -> n p tt d", p=128, tt=NT)
    pa_r = g["partial_dram"].rearrange("(n tt p) d -> n p tt d", p=128, tt=NT)

    for sl in range(NSLAB):
        dma(p, "qsp", s.xsl[:], xb_r[sl], writes=(kx,))
        for tt in range(NT):
            emit_ln(p, s.lnxc, "lnxc", s.xn32sl[:, tt], "xn32sl",
                    s.stats, "lnst",
                    s.xsl[:, tt], kx, [(s.xn32sl[:, tt], "xn32sl")])
        dma(p, "qpool", xn_r[sl], s.xn32sl[:], reads=("xn32sl",),
            writes=(("xn_dram", sl),))
        for k in range(8):
            items = [(s.pt_f[:, tt * 128:(tt + 1) * 128],
                      s.xn32sl[:, tt, k * 128:(k + 1) * 128], s.ident_f32[:])
                     for tt in range(NT)]
            transpose_group(p, items, reads=("xn32sl", "identf32"),
                            write_key="ptf")
            dve(p, "tensor_copy", s.xnT[:, k], s.pt_f[:, 0:512],
                reads=("ptf",), writes=((kxnT, k),))
            dve(p, "tensor_copy", s.xnT32k[:], s.pt_f[:, 0:512],
                reads=("ptf",), writes=("xnT32k",))
            def emit_rmm(eng, k=k):
                inst = None
                for tt in range(NT):
                    inst = eng.matmul(s.ps_mm[tt][:, 0:8],
                                      s.xnT32k[:, tt * 128:(tt + 1) * 128],
                                      s.wr32[:, k],
                                      start=(k == 0), stop=(k == 7))
                return inst
            p.op("pe", emit_rmm, reads=("xnT32k", "wr32"),
                 writes=tuple(("psmm", tt) for tt in range(NT)))
        rxnT = tuple((kxnT, k) for k in range(8))
        for tt in range(NT):
            dve(p, "tensor_copy",
                s.logits[:, sl * NT + tt, :], s.ps_mm[tt][:, 0:8],
                reads=(("psmm", tt),),
                writes=(("logits", sl * NT + tt),))
        for w in range(3):
            for e in range(8):
                bk = (w * 8 + e) % 4
                bank = s.ps_mm[bk]
                mm_group(p, bank[:],
                         [(wsb[w][:, k, e * 128:(e + 1) * 128], s.xnT[:, k])
                          for k in range(8)],
                         reads=rxnT + (wk[w],), write_key=("psmm", bk))
                dve(p, "tensor_scalar", pre[w][:, e], bank[:],
                    bias[w][:, e:e + 1], None, op0=OP.add,
                    reads=(("psmm", bk), biask[w]),
                    writes=((wk[w] + "pre", e),))
        rF = tuple(("wfpre", e) for e in range(8))
        rI = tuple(("wipre", e) for e in range(8))
        rH = tuple(("whpre", e) for e in range(8))
        if sl == 0 and g.get("dbg_pre") is not None:
            dma(p, "qsp", g["dbg_pre"][0], s.preF[:], reads=rF, writes=(("dbg", 0),))
            dma(p, "qsp", g["dbg_pre"][1], s.preI[:], reads=rI, writes=(("dbg", 1),))
            dma(p, "qsp", g["dbg_pre"][2], s.preH[:], reads=rH, writes=(("dbg", 2),))
        F, I, H = s.preF[:], s.preI[:], s.preH[:]
        S1, S2 = s.sc1[:], s.sc2[:]
        act(p, S1, F, AF.Abs, reads=rF, writes=(k1,))
        act(p, S1, S1, AF.Exp, scale=-1.0, reads=(k1,), writes=(k1,))
        act(p, S1, S1, AF.Ln, bias=1.0, reads=(k1,), writes=(k1,))
        dve(p, "tensor_scalar", F, F, 0.0, None, op0=OP.min,
            reads=rF, writes=rF + (kF,))
        dve(p, "tensor_tensor", S1, S1, F, op=OP.subtract,
            reads=(k1, kF), writes=(k1,))
        act(p, S2, I, AF.Abs, reads=rI, writes=(k2,))
        act(p, S2, S2, AF.Exp, scale=-1.0, reads=(k2,), writes=(k2,))
        act(p, S2, S2, AF.Ln, bias=1.0, reads=(k2,), writes=(k2,))
        dve(p, "tensor_scalar", I, I, 0.0, None, op0=OP.min,
            reads=rI, writes=rI + (kI,))
        dve(p, "tensor_tensor", S2, S2, I, op=OP.subtract,
            reads=(k2, kI), writes=(k2,))
        dve(p, "tensor_tensor", S1, S1, S2, op=OP.subtract,
            reads=(k1, k2), writes=(k1,))
        act(p, F, S1, AF.Sigmoid, scale=-1.0, reads=(k1,), writes=rF + (kF,))
        act(p, S2, S1, AF.Sigmoid, reads=(k1,), writes=(k2,))
        dve(p, "tensor_scalar", S1, H, 0.5, None, op0=OP.add,
            reads=rH, writes=(k1,))
        act(p, H, H, AF.Sigmoid, reads=rH, writes=rH + (kH,))
        dve(p, "tensor_tensor", H, H, S1, op=OP.max,
            reads=(kH, k1), writes=(kH,))
        dve(p, "tensor_tensor", S2, S2, H, op=OP.mult,
            reads=(k2, kH), writes=(k2,))
        for e in range(8):
            dve(p, "tensor_tensor_scan", s.sc1[:, e], s.preF[:, e], s.sc2[:, e],
                s.state[:, e:e + 1], op0=OP.mult, op1=OP.add,
                reads=(kF, k2, ("state", e)), writes=(k1,))
            dve(p, "tensor_copy", s.state[:, e:e + 1], s.sc1[:, e, SLAB - 1:SLAB],
                reads=(k1,), writes=(("state", e),))
        for tt in range(NT):
            for half in range(2):
                items = [(s.pt_f[:, q * 128:(q + 1) * 128],
                          s.sc1[:, half * 4 + q, tt * 128:(tt + 1) * 128],
                          s.ident_f32[:]) for q in range(4)]
                transpose_group(p, items, reads=(k1, "identf32"),
                                write_key="ptf")
                dve(p, "tensor_tensor", s.xsl[:, tt, half * 512:(half + 1) * 512],
                    s.xsl[:, tt, half * 512:(half + 1) * 512], s.pt_f[:],
                    op=OP.add, reads=("ptf", kx), writes=(kx,))
        dma(p, "qsp", pa_r[sl], s.xsl[:], reads=(kx,),
            writes=(("partial", sl),))

    if g.get("xn_out") is not None:
        dma(p, "qsp", g["xn_out"], g["xn_dram"][0:T, :],
            reads=tuple(("xn_dram", sl) for sl in range(NSLAB)),
            writes=(("xnout",),))


def build_x1_copy(nc, p, s, g):
    NT = 4
    NSLAB = T // 512 if P1_NSLAB is None else P1_NSLAB
    pa_r = g["partial_dram"].rearrange("(n tt p) d -> n p tt d", p=128, tt=NT)
    x1_r = g["x1_dram"].rearrange("(n tt p) d -> n p tt d", p=128, tt=NT)
    for sl in range(NSLAB):
        dma(p, "qsp", s.xsl[:], pa_r[sl],
            reads=(("partial", sl),), writes=("xsl",))
        dma(p, "qsp", x1_r[sl], s.xsl[:], reads=("xsl",),
            writes=(("x1", sl),))


NEXP = 8
CAP = 512
NGRP = 4
GT = 32
INVN = 4352
XND_Z = T


def bcast_m(ap2d, n):
    return ap2d.unsqueeze(1).broadcast_to([ap2d.shape[0], n, ap2d.shape[1]])


def bcast_l(ap2d, k):
    return ap2d.unsqueeze(2).broadcast_to([ap2d.shape[0], ap2d.shape[1], k])


def build_moe_routing(nc, p, s, g, gi):
    lt = s.logits[:, GT * gi:GT * (gi + 1), :]
    prb = s.prb[:]
    oh1, oh2 = s.ohb1[:], s.ohb2[:]
    cntb = s.cntb[:]
    r = lambda i: s.rstat[:, :, i]
    rb = lambda i: bcast_l(s.rstat[:, :, i], NEXP)
    kl = tuple(("logits", t) for t in range(GT * gi, GT * (gi + 1)))
    K = lambda *names: tuple(names)

    dve(p, "tensor_reduce", r(0), lt, axis=AX.X, op=OP.max, negate=True,
        reads=kl, writes=("rs0",))
    dve(p, "tensor_tensor", prb, lt, rb(0), op=OP.add,
        reads=kl + ("rs0",), writes=("prb",))
    act(p, prb, prb, AF.Exp, reads=("prb",), writes=("prb",))
    dve(p, "tensor_reduce", r(1), prb, axis=AX.X, op=OP.add,
        reads=("prb",), writes=("rs1",))
    dve(p, "reciprocal", r(1), r(1), reads=("rs1",), writes=("rs1",))
    dve(p, "tensor_tensor", prb, prb, rb(1), op=OP.mult,
        reads=("prb", "rs1"), writes=("prb",))
    dve(p, "tensor_reduce", r(2), prb, axis=AX.X, op=OP.max,
        reads=("prb",), writes=("rs2",))
    dve(p, "tensor_tensor", oh1, prb, rb(2), op=OP.is_equal,
        reads=("prb", "rs2"), writes=("oh1",))
    dve(p, "tensor_tensor", oh1, oh1, bcast_m(s.iota8m8[:], GT), op=OP.mult,
        reads=("oh1", "iota8m8"), writes=("oh1",))
    dve(p, "tensor_scalar", oh1, oh1, 8.0, None, op0=OP.add,
        reads=("oh1",), writes=("oh1",))
    dve(p, "tensor_reduce", r(3), oh1, axis=AX.X, op=OP.min,
        reads=("oh1",), writes=("rs3",))
    dve(p, "tensor_tensor", oh1, bcast_m(s.iota8[:], GT), rb(3), op=OP.is_equal,
        reads=("iota8", "rs3"), writes=("oh1",))
    dve(p, "tensor_copy", s.pbuf[:, gi, :, 0], r(2),
        reads=("rs2",), writes=(("pbuf", gi),))
    dve(p, "scalar_tensor_tensor", prb, oh1, NEG, prb, op0=OP.mult, op1=OP.add,
        reads=("oh1", "prb"), writes=("prb",))
    dve(p, "tensor_reduce", r(4), prb, axis=AX.X, op=OP.max,
        reads=("prb",), writes=("rs4",))
    dve(p, "tensor_tensor", oh2, prb, rb(4), op=OP.is_equal,
        reads=("prb", "rs4"), writes=("oh2",))
    dve(p, "tensor_tensor", oh2, oh2, bcast_m(s.iota8m8[:], GT), op=OP.mult,
        reads=("oh2", "iota8m8"), writes=("oh2",))
    dve(p, "tensor_scalar", oh2, oh2, 8.0, None, op0=OP.add,
        reads=("oh2",), writes=("oh2",))
    dve(p, "tensor_reduce", r(5), oh2, axis=AX.X, op=OP.min,
        reads=("oh2",), writes=("rs5",))
    dve(p, "tensor_tensor", oh2, bcast_m(s.iota8[:], GT), rb(5), op=OP.is_equal,
        reads=("iota8", "rs5"), writes=("oh2",))
    dve(p, "tensor_copy", s.pbuf[:, gi, :, 1], r(4),
        reads=("rs4",), writes=(("pbuf2", gi),))

    for cpass, (ohap, kOH) in enumerate(((oh1, "oh1"), (oh2, "oh2"))):
        pool(p, "memset", s.totrow[:], 0.0, writes=("totrow",))
        for i in range(GT):
            mm_group(p, s.ps_r[:, 0:8],
                     [(s.tri[:], ohap[:, i, :]), (s.onescol[:], s.totrow[:])],
                     reads=(kOH, "tri", "onescol", "totrow"), write_key="psr")
            dve(p, "tensor_copy", cntb[:, i, :], s.ps_r[:, 0:8],
                reads=("psr",), writes=(("cntb", i),))
            mm_group(p, s.pt_f[0:1, 0:8], [(s.ones128[:], ohap[:, i, :])],
                     reads=(kOH, "ones128"), write_key="ptf",
                     accumulate=(i > 0))
            dve(p, "tensor_copy", s.totrow[:], s.pt_f[0:1, 0:8],
                reads=("ptf",), writes=("totrow",))
        if cpass == 0:
            mm_group(p, s.ps_r[:, 0:8], [(s.onescol[:], s.totrow[:])],
                     reads=("totrow", "onescol"), write_key="psr")
            dve(p, "tensor_copy", s.tot1bc[:], s.ps_r[:, 0:8],
                reads=("psr",), writes=("tot1bc",))
            dve(p, "tensor_tensor", s.prb[:], oh1, cntb, op=OP.mult,
                reads=("oh1",) + tuple(("cntb", i) for i in range(GT)),
                writes=("prb",))
            dve(p, "tensor_reduce", s.prib[:, :, 0], s.prb[:], axis=AX.X,
                op=OP.add, reads=("prb",), writes=("pri1",))
        else:
            dve(p, "tensor_tensor", s.prb[:], oh2, cntb, op=OP.mult,
                reads=("oh2",) + tuple(("cntb", i) for i in range(GT)),
                writes=("prb",))
            dve(p, "tensor_reduce", s.prib[:, :, 1], s.prb[:], axis=AX.X,
                op=OP.add, reads=("prb",), writes=("pri2",))
            dve(p, "tensor_tensor", s.prb[:], oh2, bcast_m(s.tot1bc[:], GT),
                op=OP.mult, reads=("oh2", "tot1bc"), writes=("prb",))
            dve(p, "tensor_reduce", s.rstat[:, :, 6], s.prb[:], axis=AX.X,
                op=OP.add, reads=("prb",), writes=("rs6",))
            dve(p, "tensor_tensor", s.prib[:, :, 1], s.prib[:, :, 1],
                s.rstat[:, :, 6], op=OP.add,
                reads=("pri2", "rs6"), writes=("pri2",))

    pool(p, "memset", s.invinit[:], XND_Z, writes=("invinit",))
    dma(p, "qsp",
        g["inv_dram"][INVN * gi:INVN * (gi + 1), :].rearrange("(q a) c -> q a c", a=34),
        s.invinit[:].unsqueeze(2), reads=("invinit",), writes=(("inv", gi),))

    for c, kpri, kidx in ((0, "pri1", "rs3x"), (1, "pri2", "rs5x")):
        idxs = s.rstat[:, :, 3 + 2 * c]
        kidxr = "rs3" if c == 0 else "rs5"
        pri = s.prib[:, :, c]
        t1, t2, t3 = s.rt1[:], s.rt2[:], s.rt3[:]
        dve(p, "tensor_scalar", t1, pri, 512.0, None, op0=OP.is_ge,
            reads=(kpri,), writes=("rt1",))
        dve(p, "tensor_scalar", t2, idxs, 512.0, None, op0=OP.mult,
            reads=(kidxr,), writes=("rt2",))
        dve(p, "tensor_tensor", t2, t2, pri, op=OP.add,
            reads=("rt2", kpri), writes=("rt2",))
        dve(p, "tensor_scalar", t3, t2, -1.0, 4096.0, op0=OP.mult, op1=OP.add,
            reads=("rt2",), writes=("rt3",))
        dve(p, "tensor_tensor", t3, t3, t1, op=OP.mult,
            reads=("rt3", "rt1"), writes=("rt3",))
        dve(p, "tensor_tensor", t3, t3, t2, op=OP.add,
            reads=("rt3", "rt2"), writes=("rt3",))
        dve(p, "tensor_copy", s.slotc[:, gi, :, c], t3,
            reads=("rt3",), writes=(("slotc", gi, c),))
        dve(p, "tensor_scalar", t3,
            s.pidx[:, 0:1].broadcast_to([128, GT]), 1.0, 4200.0,
            op0=OP.mult, op1=OP.add, reads=("pidx",), writes=("rt3",))
        dve(p, "tensor_tensor", t3, t3, t2, op=OP.subtract,
            reads=("rt3", "rt2"), writes=("rt3",))
        dve(p, "tensor_tensor", t3, t3, t1, op=OP.mult,
            reads=("rt3", "rt1"), writes=("rt3",))
        dve(p, "tensor_tensor", t3, t3, t2, op=OP.add,
            reads=("rt3", "rt2"), writes=("rt3",))
        dve(p, "tensor_scalar", t3, t3, float(INVN * gi), None, op0=OP.add,
            reads=("rt3",), writes=("rt3",))
        dve(p, "tensor_copy", s.sslotu[:, :, c], t3,
            reads=("rt3",), writes=(("sslotu", c),))
    for i in range(GT):
        for c in range(2):
            scatter(p, g["inv_dram"], s.sslotu[:, i, c:c + 1],
                    s.tokidx[:, GT * gi + i:GT * gi + i + 1],
                    reads=(("sslotu", c), "tokidx", ("inv", gi)),
                    writes=(("inv", gi),))


def build_moe_experts(nc, p, s, g):
    wi_r = [g["moe_wi"][e].rearrange("(k p) f -> p k f", p=128) for e in range(NEXP)]
    wo_r = [g["moe_wo"][e].rearrange("(k p) h -> p k h", p=128) for e in range(NEXP)]
    for e in range(NEXP):
        dma(p, "qpool", s.wo_sb[:], wo_r[e], writes=("wo_sb",))
        for gi in range(NGRP):
            for q in range(4):
                a = INVN * gi + e * CAP + q * 128
                dma(p, "qsp", s.idxu[:], g["inv_dram"][a:a + 128, :],
                    reads=(("inv", gi),), writes=("idxu",))
                gather(p, s.ein[:], g["xn_dram"], s.idxu[:, 0:1],
                       reads=("idxu",) + tuple(("xn_dram", sl) for sl in range(32))
                       + (("xn_dram", "z"),),
                       writes=("ein",))
                items = [(s.pt_b[:, k * 128:(k + 1) * 128],
                          s.ein[:, k * 128:(k + 1) * 128], s.ident_bf[:])
                         for k in range(8)]
                transpose_group(p, items, reads=("ein", "identbf"),
                                write_key="ptb")
                dve(p, "tensor_copy", s.einT[:, :, q * 128:(q + 1) * 128],
                    s.pt_b[:].rearrange("p (k x) -> p k x", k=8),
                    reads=("ptb",), writes=(("einT", q),))
            reinT = tuple(("einT", q) for q in range(4))
            for fk in range(8):
                wb = s.wi_ch[fk % 2]
                kwb = ("wi_ch", fk % 2)
                dma(p, "qpool", wb[:], wi_r[e][:, :, fk * 512:(fk + 1) * 512],
                    writes=(kwb,))
                for m in range(4):
                    bk = m % 4
                    mm_group(p, s.ps_mm[bk][:],
                             [(wb[:, k, m * 128:(m + 1) * 128], s.einT[:, k])
                              for k in range(8)],
                             reads=reinT + (kwb,), write_key=("psmm", bk))
                    act(p, s.h1T[:, fk * 4 + m], s.ps_mm[bk][:],
                        AF.Gelu_apprx_tanh,
                        reads=(("psmm", bk),), writes=(("h1T", fk * 4 + m),))
            rh1 = tuple(("h1T", i) for i in range(32))
            for n in range(2):
                for m in range(4):
                    bk = m
                    mm_group(p, s.ps_mm[bk][:],
                             [(s.h1T[:, k, m * 128:(m + 1) * 128],
                               s.wo_sb[:, k, n * 512:(n + 1) * 512])
                              for k in range(32)],
                             reads=rh1 + ("wo_sb",), write_key=("psmm", bk))
                    dve(p, "tensor_copy", s.stg[:], s.ps_mm[bk][:],
                        reads=(("psmm", bk),), writes=("stg",))
                    dma(p, "qsp",
                        g["out2_dram"][gi][e * CAP + m * 128:e * CAP + (m + 1) * 128,
                                           n * 512:(n + 1) * 512],
                        s.stg[:], reads=("stg",), writes=(("out2", gi, e, n, m),))


def build_moe_combine(nc, p, s, g):
    pa_t = g["partial_dram"].rearrange("(t p) d -> t p d", p=128)
    x1_t = g["x1_dram"].rearrange("(t p) d -> t p d", p=128)
    for gi in range(NGRP):
        pool(p, "memset", s.zrow_f[:], 0.0, writes=("zrowf",))
        dma(p, "qpool", g["out2_dram"][gi][NEXP * CAP:NEXP * CAP + 1, :],
            s.zrow_f[0:1, :], reads=("zrowf",), writes=(("out2z", gi),))
        rout2 = tuple(("out2", gi, e, n, m)
                      for e in range(NEXP) for n in range(2) for m in range(4))
        for i in range(GT):
            gt = GT * gi + i
            dma(p, "qsp", s.part[:], pa_t[gt], reads=(("partial", gt // 4),),
                writes=("part",))
            gather(p, s.g1t[:], g["out2_dram"][gi], s.slotc[:, gi, i, 0:1],
                   reads=(("slotc", gi, 0), ("out2z", gi)) + rout2,
                   writes=("g1t",))
            gather(p, s.g2t[:], g["out2_dram"][gi], s.slotc[:, gi, i, 1:2],
                   reads=(("slotc", gi, 1), ("out2z", gi)) + rout2,
                   writes=("g2t",))
            dve(p, "tensor_scalar", s.g1t[:], s.g1t[:], s.pbuf[:, gi, i, 0:1],
                None, op0=OP.mult,
                reads=("g1t", ("pbuf", gi)), writes=("g1t",))
            dve(p, "scalar_tensor_tensor", s.g1t[:], s.g2t[:],
                s.pbuf[:, gi, i, 1:2], s.g1t[:], op0=OP.mult, op1=OP.add,
                reads=("g2t", ("pbuf2", gi), "g1t"), writes=("g1t",))
            dve(p, "tensor_tensor", s.part[:], s.part[:], s.g1t[:], op=OP.add,
                reads=("part", "g1t"), writes=("part",))
            dma(p, "qsp", x1_t[gt], s.part[:], reads=("part",),
                writes=(("x1", gt),))


def build_moe(nc, p, s, g):
    dma(p, "qsp", s.tri[:], g["tri_f32"], writes=("tri",))
    dma(p, "qsp", s.iota8[:], g["iota8"], writes=("iota8",))
    dma(p, "qsp", s.iota8m8[:], g["iota8m8"], writes=("iota8m8",))
    dma(p, "qsp", s.onescol[:], g["onescol"], writes=("onescol",))
    dma(p, "qsp", s.ones128[:], g["ones128"], writes=("ones128",))
    dma(p, "qsp", s.pidx[:], g["pidx"], writes=("pidx",))
    dma(p, "qsp", s.tokidx[:], g["tokidx"], writes=("tokidx",))
    for gi in range(NGRP):
        build_moe_routing(nc, p, s, g, gi)
    build_moe_experts(nc, p, s, g)
    build_moe_combine(nc, p, s, g)



def _mk(nc, stack, prefix):
    def sb(name, shape, dt):
        return stack.enter_context(nc.sbuf_tensor(prefix + name, shape, dt))
    def psb(name, shape, dt):
        return stack.enter_context(nc.psum_tensor(prefix + name, shape, dt))
    return sb, psb


def alloc_persist(nc, stack, s):
    sb, psb = _mk(nc, stack, "sp_")
    s.ident_bf = sb("ident_bf", [128, 128], BF16)
    s.ident_f32 = sb("ident_f32", [128, 128], F32)
    s.logits = sb("logits", [128, 128, 8], F32)
    s.onescol32 = sb("onescol32", [1, 128], F32)
    s.ps_mm = [psb(f"ps_mm{i}", [128, 512], F32) for i in range(4)]
    s.ps_r = psb("ps_r", [128, 512], F32)
    s.ps_x = psb("ps_x", [128, 512], F32)
    s.pt_b = psb("pt_b", [128, 1024], BF16)
    s.pt_f = psb("pt_f", [128, 512], F32)
    return s


def alloc_phase1(nc, stack, s):
    sb, _ = _mk(nc, stack, "s1_")
    s.zrow_bf = sb("zrow_bf", [128, D], BF16)
    s.wf = sb("wf", [128, 8, D], BF16)
    s.wi = sb("wi", [128, 8, D], BF16)
    s.wh = sb("wh", [128, 8, D], BF16)
    s.wr32 = sb("wr32", [128, 8, 8], F32)
    s.bfb = sb("bfb", [128, 8], F32)
    s.bib = sb("bib", [128, 8], F32)
    s.bhb = sb("bhb", [128, 8], F32)
    s.xsl = sb("xsl", [128, 4, D], F32)
    s.xn32sl = sb("xn32sl", [128, 4, D], F32)
    s.lnxc = sb("lnxc", [128, D], F32)
    s.xnT32k = sb("xnT32k", [128, 512], F32)
    s.stats = sb("stats", [128, 8], F32)
    s.xnT = sb("xnT", [128, 8, 512], BF16)
    s.preF = sb("preF", [128, 8, 512], F32)
    s.preI = sb("preI", [128, 8, 512], F32)
    s.preH = sb("preH", [128, 8, 512], F32)
    s.sc1 = sb("sc1", [128, 8, 512], F32)
    s.sc2 = sb("sc2", [128, 8, 512], F32)
    s.state = sb("state", [128, 8], F32)
    return s


def alloc_moe(nc, stack, s):
    sb, _ = _mk(nc, stack, "sm_")
    s.tri = sb("tri", [128, 128], F32)
    s.iota8 = sb("iota8", [128, 8], F32)
    s.iota8m8 = sb("iota8m8", [128, 8], F32)
    s.onescol = sb("onescol", [1, 128], F32)
    s.ones128 = sb("ones128", [128, 1], F32)
    s.pidx = sb("pidx", [128, 1], F32)
    s.tokidx = sb("tokidx", [128, 128], U32)
    s.prb = sb("prb", [128, GT, 8], F32)
    s.ohb1 = sb("ohb1", [128, GT, 8], F32)
    s.ohb2 = sb("ohb2", [128, GT, 8], F32)
    s.cntb = sb("cntb", [128, GT, 8], F32)
    s.rstat = sb("rstat", [128, GT, 8], F32)
    s.prib = sb("prib", [128, GT, 2], F32)
    s.rt1 = sb("rt1", [128, GT], F32)
    s.rt2 = sb("rt2", [128, GT], F32)
    s.rt3 = sb("rt3", [128, GT], F32)
    s.totrow = sb("totrow", [1, 8], F32)
    s.tot1bc = sb("tot1bc", [128, 8], F32)
    s.invinit = sb("invinit", [128, 34], U32)
    s.sslotu = sb("sslotu", [128, GT, 2], U32)
    s.slotc = sb("slotc", [128, NGRP, GT, 2], U32)
    s.pbuf = sb("pbuf", [128, NGRP, GT, 2], F32)
    s.wo_sb = sb("wo_sb", [128, 32, 1024], BF16)
    s.wi_ch = [sb(f"wi_ch{i}", [128, 8, 512], BF16) for i in range(2)]
    s.ein = sb("ein", [128, 1024], BF16)
    s.einT = sb("einT", [128, 8, 512], BF16)
    s.h1T = sb("h1T", [128, 32, 512], BF16)
    s.idxu = sb("idxu", [128, 1], U32)
    s.stg = sb("stg", [128, 512], F32)
    s.g1t = sb("g1t", [128, 1024], F32)
    s.g2t = sb("g2t", [128, 1024], F32)
    s.part = sb("part", [128, 1024], F32)
    s.zrow_f = sb("zrow_f", [128, 1024], F32)
    return s



def _np(x):
    return np.asarray(x)


def declare_p1_dram(nc, g, external_x1=True):
    g["xb"] = nc.dram_tensor("xb", [T, D], F32, kind="ExternalInput").ap()
    g["Wf_t"] = nc.dram_tensor("Wf_t", [D, D], F32, kind="ExternalInput").ap()
    g["Wi_t"] = nc.dram_tensor("Wi_t", [D, D], F32, kind="ExternalInput").ap()
    g["Wh_t"] = nc.dram_tensor("Wh_t", [D, D], F32, kind="ExternalInput").ap()
    g["b_f"] = nc.dram_tensor("b_f", [128, 8], F32, kind="ExternalInput").ap()
    g["b_i"] = nc.dram_tensor("b_i", [128, 8], F32, kind="ExternalInput").ap()
    g["b_h"] = nc.dram_tensor("b_h", [128, 8], F32, kind="ExternalInput").ap()
    g["w_router"] = nc.dram_tensor("w_router", [D, 8], F32, kind="ExternalInput").ap()
    g["ident_bf"] = nc.dram_tensor("ident_bf", [128, 128], BF16, kind="ExternalInput").ap()
    g["ident_f32"] = nc.dram_tensor("ident_f32", [128, 128], F32, kind="ExternalInput").ap()
    g["moe_wi"] = nc.dram_tensor("moe_wi", [NEXP, D, 4096], F32, kind="ExternalInput").ap()
    g["moe_wo"] = nc.dram_tensor("moe_wo", [NEXP, 4096, D], F32, kind="ExternalInput").ap()
    g["tri_f32"] = nc.dram_tensor("tri_f32", [128, 128], F32, kind="ExternalInput").ap()
    g["iota8"] = nc.dram_tensor("iota8", [128, 8], F32, kind="ExternalInput").ap()
    g["iota8m8"] = nc.dram_tensor("iota8m8", [128, 8], F32, kind="ExternalInput").ap()
    g["onescol"] = nc.dram_tensor("onescol", [1, 128], F32, kind="ExternalInput").ap()
    g["ones128"] = nc.dram_tensor("ones128", [128, 1], F32, kind="ExternalInput").ap()
    g["pidx"] = nc.dram_tensor("pidx", [128, 1], F32, kind="ExternalInput").ap()
    g["tokidx"] = nc.dram_tensor("tokidx", [128, 128], U32, kind="ExternalInput").ap()
    g["xn_dram"] = nc.dram_tensor("xn_dram", [T + 128, D], BF16).ap()
    g["partial_dram"] = nc.dram_tensor("partial_dram", [T, D], F32).ap()
    g["inv_dram"] = nc.dram_tensor("inv_dram", [NGRP * INVN, 1], U32).ap()
    g["out2_dram"] = [nc.dram_tensor(f"out2_g{gi}", [NEXP * CAP + 1, D], F32).ap()
                      for gi in range(NGRP)]
    if external_x1:
        g["x1_dram"] = nc.dram_tensor("x1_out", [T, D], F32, kind="ExternalOutput").ap()
    else:
        g["x1_dram"] = nc.dram_tensor("x1_dram", [T, D], F32).ap()


def build_p1_program(with_moe=False, debug=False):
    import contextlib
    nc = bass.Bass("TRN2", target_bir_lowering=False)
    g = {}
    declare_p1_dram(nc, g, external_x1=True)
    if debug:
        g["dbg_pre"] = nc.dram_tensor("dbg_pre", [3, 128, 8, 512], F32, kind="ExternalOutput").ap()
        g["xn_out"] = nc.dram_tensor("xn_out", [T, D], BF16, kind="ExternalOutput").ap()

    with contextlib.ExitStack() as stack:
        s = S()
        alloc_persist(nc, stack, s)
        sems = {}
        for name in ENGINES + QUEUES:
            sems[name] = stack.enter_context(nc.semaphore(name))
        p = Prog(nc)
        with nc.Block() as block:
            dma(p, "qsp", s.ident_bf[:], g["ident_bf"], writes=("identbf",))
            dma(p, "qsp", s.ident_f32[:], g["ident_f32"], writes=("identf32",))
            with contextlib.ExitStack() as st1:
                alloc_phase1(nc, st1, s)
                build_phase1(nc, p, s, g)
                p.fence()
                p.materialize(block, sems)
            if with_moe:
                with contextlib.ExitStack() as st2:
                    alloc_moe(nc, st2, s)
                    build_moe(nc, p, s, g)
                    p.fence()
                    p.materialize(block, sems)
            else:
                with contextlib.ExitStack() as st2:
                    alloc_phase1(nc, st2, s)
                    build_x1_copy(nc, p, s, g)
                    p.materialize(block, sems)
            p.barrier_all()
            p.materialize(block, sems)
    return nc


def prep_p1_inputs(x_batch, inp):
    d = {}
    d["xb"] = np.ascontiguousarray(x_batch)
    d["Wf_t"] = np.ascontiguousarray(_np(inp["L0_attn_W_f"]).T)
    d["Wi_t"] = np.ascontiguousarray(_np(inp["L0_attn_W_i"]).T)
    d["Wh_t"] = np.ascontiguousarray(_np(inp["L0_attn_W_h"]).T)
    d["b_f"] = np.ascontiguousarray(_np(inp["L0_attn_b_f"]).reshape(8, 128).T)
    d["b_i"] = np.ascontiguousarray(_np(inp["L0_attn_b_i"]).reshape(8, 128).T)
    d["b_h"] = np.ascontiguousarray(_np(inp["L0_attn_b_h"]).reshape(8, 128).T)
    d["w_router"] = np.ascontiguousarray(_np(inp["L0_ffn_w_router"]))
    d["ident_bf"] = np.eye(128, dtype=BF16NP)
    d["ident_f32"] = np.eye(128, dtype=np.float32)
    d["moe_wi"] = np.ascontiguousarray(_np(inp["L0_ffn_wi"]))
    d["moe_wo"] = np.ascontiguousarray(_np(inp["L0_ffn_wo"]))
    d["tri_f32"] = np.triu(np.ones((128, 128), np.float32), 1)
    io8 = np.broadcast_to(np.arange(8, dtype=np.float32), (128, 8)).copy()
    d["iota8"] = io8
    d["iota8m8"] = io8 - 8.0
    d["onescol"] = np.ones((1, 128), np.float32)
    d["ones128"] = np.ones((128, 1), np.float32)
    d["pidx"] = np.arange(128, dtype=np.float32)[:, None].copy()
    tk = (np.arange(128, dtype=np.uint32)[None, :] * 128
          + np.arange(128, dtype=np.uint32)[:, None])
    d["tokidx"] = np.ascontiguousarray(tk)
    return d


_PROG_CACHE = {}


def kernel(**inputs):
    if FUSED:
        return kernel_fused(**inputs)
    x = np.ascontiguousarray(np.asarray(inputs["L0_attn_x"], dtype=np.float32))
    if "p1" not in _PROG_CACHE:
        _PROG_CACHE["p1"] = build_p1_program(with_moe=True)
        _PROG_CACHE["p2"] = build_p2_program()
        _PROG_CACHE["p3"] = build_p3_program()
    d1 = [prep_p1_inputs(x[b], inputs) for b in range(B)]
    res1 = run_bass_kernel_spmd(_PROG_CACHE["p1"], [d1[c // 4] for c in range(8)],
                                core_ids=list(range(8)))
    x1 = [np.asarray(res1.results[4 * b]["x1_out"]) for b in range(B)]
    d2 = [prep_p2_inputs(x1[c // 4], inputs, c % 4) for c in range(8)]
    res2 = run_bass_kernel_spmd(_PROG_CACHE["p2"], d2, core_ids=list(range(8)))
    xn1 = [np.asarray(res2.results[4 * b]["xn1_out"]) for b in range(B)]
    attnbr = [np.asarray(res2.results[c]["attnbr_out"]) for c in range(8)]
    d3 = []
    for c in range(8):
        b, j = c // 4, c % 4
        d = prep_p3_inputs(inputs, j)
        d["xn1_in"] = np.concatenate([xn1[b], np.zeros((128, D), np.float32)], 0)
        d["x1_in"] = x1[b]
        d["attnbr_in"] = attnbr[c]
        d3.append(d)
    res3 = run_bass_kernel_spmd(_PROG_CACHE["p3"], d3, core_ids=list(range(8)))
    out = np.empty((B, T, D), np.float32)
    for c in range(8):
        b, j = c // 4, c % 4
        out[b, CHUNK * j:CHUNK * (j + 1)] = res3.results[c]["out"]
    return out



GAMMA = 0.99
BAND = 4096
VSL = 256


def alloc_phase2a(nc, stack, s):
    sb, _ = _mk(nc, stack, "s2a_")
    s.wq = sb("wq", [128, 8, D], BF16)
    s.wk = sb("wk", [128, 8, D], BF16)
    s.wv = sb("wv", [128, 8, VSL], BF16)
    s.x1sl = sb("x1sl", [128, 4, D], F32)
    s.xn1sl = sb("xn1sl", [128, 4, D], F32)
    s.lnxc2 = sb("lnxc2", [128, D], F32)
    s.stats2 = sb("stats2", [128, 8], F32)
    s.xnT2 = sb("xnT2", [128, 8, 512], BF16)
    s.qstg = sb("qstg", [128, 512], BF16)
    s.kstg = sb("kstg", [128, 512], BF16)
    s.zrow2 = sb("zrow2", [128, D], F32)


def alloc_phase2b(nc, stack, s):
    sb, _ = _mk(nc, stack, "s2b_")
    s.qt = sb("qt", [128, 8, 512], BF16)
    s.kt = [sb(f"kt{i}", [128, 8, 128], BF16) for i in range(2)]
    s.abuf = sb("abuf", [128, 36, 512], BF16)
    s.e0 = sb("e0", [128, 512], F32)
    s.em = sb("em", [128, 4, 512], F32)
    s.ostg = sb("ostg", [128, 512], F32)


def alloc_vbuf(nc, stack, s):
    sb, _ = _mk(nc, stack, "s2v_")
    s.vbuf = sb("vbuf", [128, 128, VSL], BF16)


def build_phase2a(nc, p, s, g):
    NT = 4
    NSLAB = T // 512 if P1_NSLAB is None else P1_NSLAB
    dma(p, "qpool", s.wq[:], g["W_Q"].rearrange("(k p) c -> p k c", p=128),
        writes=("wq",))
    dma(p, "qpool", s.wk[:], g["W_K"].rearrange("(k p) c -> p k c", p=128),
        writes=("wk",))
    dma(p, "qpool", s.wv[:], g["W_Vs"].rearrange("(k p) c -> p k c", p=128),
        writes=("wv",))
    pool(p, "memset", s.zrow2[:], 0.0, writes=("zrow2",))
    dma(p, "qpool", g["xn1_dram"][T:T + 1, :], s.zrow2[0:1, 0:D],
        reads=("zrow2",), writes=(("xn1_dram", "z"),))

    x1_r = g["x1_dram"].rearrange("(n tt p) d -> n p tt d", p=128, tt=NT)
    xn1_r = g["xn1_dram"][0:T, :].rearrange("(n tt p) d -> n p tt d", p=128, tt=NT)
    qt_r = g["qt_dram"].rearrange("(m p) t -> p m t", p=128)
    kt_r = g["kt_dram"].rearrange("(m p) t -> p m t", p=128)

    for sl in range(NSLAB):
        t0 = sl * 512
        dma(p, "qsp", s.x1sl[:], x1_r[sl],
            reads=(("x1", 4 * sl), ("x1", 4 * sl + 1), ("x1", 4 * sl + 2),
                   ("x1", 4 * sl + 3)),
            writes=("x1sl",))
        for tt in range(NT):
            emit_ln(p, s.lnxc2, "lnxc2", s.xn1sl[:, tt], "xn1sl",
                    s.stats2, "lnst2",
                    s.x1sl[:, tt], "x1sl", [(s.xn1sl[:, tt], "xn1sl")])
        dma(p, "qsp", xn1_r[sl], s.xn1sl[:], reads=("xn1sl",),
            writes=(("xn1_dram", sl),))
        for k in range(8):
            items = [(s.pt_f[:, tt * 128:(tt + 1) * 128],
                      s.xn1sl[:, tt, k * 128:(k + 1) * 128], s.ident_f32[:])
                     for tt in range(NT)]
            transpose_group(p, items, reads=("xn1sl", "identf32"),
                            write_key="ptf")
            dve(p, "tensor_copy", s.xnT2[:, k], s.pt_f[:, 0:512],
                reads=("ptf",), writes=(("xnT2", k),))
        rxnT = tuple(("xnT2", k) for k in range(8))
        for which, wsb_, kw, stg, kstg, dr in (
                (0, s.wq, "wq", s.qstg, "qstg", qt_r),
                (1, s.wk, "wk", s.kstg, "kstg", kt_r)):
            for m in range(8):
                bk = m % 2 if which == 0 else 2 + m % 2
                mm_group(p, s.ps_mm[bk][:],
                         [(wsb_[:, k, m * 128:(m + 1) * 128], s.xnT2[:, k])
                          for k in range(8)],
                         reads=rxnT + (kw,), write_key=("psmm", bk))
                dve(p, "tensor_copy", stg[:], s.ps_mm[bk][:],
                    reads=(("psmm", bk),), writes=(kstg,))
                dma(p, "qsp", dr[:, m, t0:t0 + 512], stg[:],
                    reads=(kstg,), writes=((("qtd" if which == 0 else "ktd"), sl, m),))
        for tt in range(NT):
            gt = sl * NT + tt
            mm_group(p, s.ps_r[:, 0:VSL],
                     [(s.xnT2[:, k, tt * 128:(tt + 1) * 128], s.wv[:, k])
                      for k in range(8)],
                     reads=rxnT + ("wv",), write_key="psr")
            dve(p, "tensor_copy", s.vbuf[:, gt, :], s.ps_r[:, 0:VSL],
                reads=("psr",), writes=(("vbuf", gt),))


def _decay_tiles():
    i = np.arange(128, dtype=np.float64)[:, None]
    jj = np.arange(512, dtype=np.float64)[None, :]
    e0 = GAMMA ** (jj - i)
    ems = []
    for delta in (0, -128, -256, -384):
        ex = delta + jj - i
        em = np.where(ex >= 0, GAMMA ** ex, 0.0)
        ems.append(em)
    return (e0.astype(np.float32),
            np.stack(ems, 1).astype(np.float32))


def build_phase2b(nc, p, s, g):
    NSLAB = T // 512 if P1_NSLAB is None else P1_NSLAB
    NQ = NSLAB
    dma(p, "qsp", s.e0[:], g["e0_t"], writes=("e0",))
    dma(p, "qsp", s.em[:], g["em_t"], writes=("em",))
    qt_r = g["qt_dram"].rearrange("(m p) t -> p m t", p=128)
    kt_r = g["kt_dram"].rearrange("(m p) t -> p m t", p=128)
    ot_r = g["outT_dram"].rearrange("(m p) t -> p m t", p=128)

    for q in range(NQ):
        tq0 = 512 * q
        dma(p, "qsp", s.qt[:], qt_r[:, :, tq0:tq0 + 512], 
            reads=tuple(("qtd", sl, m) for sl in range(NSLAB) for m in range(8)),
            writes=("qt",))
        s_tiles = list(range(max(0, tq0 - BAND), tq0 + 512, 128))
        for si, s0 in enumerate(s_tiles):
            kb = si % 2
            dma(p, "qsp", s.kt[kb][:], kt_r[:, :, s0:s0 + 128],
                reads=tuple(("ktd", sl, m) for sl in range(NSLAB) for m in range(8)),
                writes=(("kt", kb),))
            bk = si % 2
            mm_group(p, s.ps_mm[bk][:],
                     [(s.kt[kb][:, k], s.qt[:, k]) for k in range(8)],
                     reads=(("kt", kb), "qt"), write_key=("psmm", bk))
            delta = tq0 - s0
            if delta >= 128:
                dve(p, "scalar_tensor_tensor", s.abuf[:, si, :],
                    s.ps_mm[bk][:], float(GAMMA ** delta), s.e0[:],
                    op0=OP.mult, op1=OP.mult,
                    reads=(("psmm", bk), "e0"), writes=(("abuf", si),))
            else:
                mi = (-delta) // 128
                dve(p, "tensor_tensor", s.abuf[:, si, :],
                    s.ps_mm[bk][:], s.em[:, mi, :], op=OP.mult,
                    reads=(("psmm", bk), "em"), writes=(("abuf", si),))
        if q == 0 and g.get("abuf_out") is not None:
            dma(p, "qsp", g["abuf_out"], s.abuf[:],
                reads=tuple(("abuf", si) for si in range(len(s_tiles))),
                writes=("abufout",))
        for mi in range(2):
            bank = s.ps_mm[2 + mi]
            mm_group(p, bank[:],
                     [(s.vbuf[:, s0 // 128, mi * 128:(mi + 1) * 128],
                       s.abuf[:, si, :])
                      for si, s0 in enumerate(s_tiles)],
                     reads=tuple(("abuf", si) for si in range(len(s_tiles)))
                     + tuple(("vbuf", s0 // 128) for s0 in s_tiles),
                     write_key=("psmm", 2 + mi))
            dve(p, "tensor_copy", s.ostg[:], bank[:],
                reads=(("psmm", 2 + mi),), writes=("ostg",))
            dma(p, "qsp", ot_r[:, mi, tq0:tq0 + 512], s.ostg[:],
                reads=("ostg",), writes=(("otd", q, mi),))


def build_scramble(nc, p, s, g):
    rot = tuple(("otd", q, mi) for q in range(T // 512) for mi in range(2))
    for m in range(16):
        src = g["outT_dram"][0:256, 1024 * m:1024 * (m + 1)]
        dst = g["attnbr_dram"].rearrange("(u s) d -> s u d", s=16)[m]
        dma(p, "qpool", dst, src, reads=rot, writes=(("attnbr", m),))


def declare_p2_dram(nc, g, x1_external):
    if x1_external:
        g["x1_dram"] = nc.dram_tensor("x1_in", [T, D], F32, kind="ExternalInput").ap()
    g["W_Q"] = nc.dram_tensor("W_Q", [D, D], F32, kind="ExternalInput").ap()
    g["W_K"] = nc.dram_tensor("W_K", [D, D], F32, kind="ExternalInput").ap()
    g["W_Vs"] = nc.dram_tensor("W_Vs", [D, VSL], F32, kind="ExternalInput").ap()
    g["e0_t"] = nc.dram_tensor("e0_t", [128, 512], F32, kind="ExternalInput").ap()
    g["em_t"] = nc.dram_tensor("em_t", [128, 4, 512], F32, kind="ExternalInput").ap()
    g["xn1_dram"] = nc.dram_tensor("xn1_dram", [T + 128, D], F32).ap()
    g["qt_dram"] = nc.dram_tensor("qt_dram", [D, T], BF16).ap()
    g["kt_dram"] = nc.dram_tensor("kt_dram", [D, T], BF16).ap()
    g["outT_dram"] = nc.dram_tensor("outT_dram", [VSL, T], F32).ap()


def build_p2_program(debug=False):
    import contextlib
    nc = bass.Bass("TRN2", target_bir_lowering=False)
    g = {}
    g["ident_bf"] = nc.dram_tensor("ident_bf", [128, 128], BF16, kind="ExternalInput").ap()
    g["ident_f32"] = nc.dram_tensor("ident_f32", [128, 128], F32, kind="ExternalInput").ap()
    declare_p2_dram(nc, g, x1_external=True)
    g["attnbr_dram"] = nc.dram_tensor("attnbr_out", [CHUNK, D], F32,
                                      kind="ExternalOutput").ap()
    g["xn1_out"] = nc.dram_tensor("xn1_out", [T, D], F32,
                                  kind="ExternalOutput").ap()
    if debug:
        g["outT_out"] = nc.dram_tensor("outT_out", [VSL, T], F32,
                                       kind="ExternalOutput").ap()
        g["abuf_out"] = nc.dram_tensor("abuf_out", [128, 36, 512], BF16,
                                       kind="ExternalOutput").ap()
        g["qt_out"] = nc.dram_tensor("qt_out", [D, T], BF16, kind="ExternalOutput").ap()
        g["kt_out"] = nc.dram_tensor("kt_out", [D, T], BF16, kind="ExternalOutput").ap()

    with contextlib.ExitStack() as stack:
        s = S()
        alloc_persist(nc, stack, s)
        alloc_vbuf(nc, stack, s)
        sems = {}
        for name in ENGINES + QUEUES:
            sems[name] = stack.enter_context(nc.semaphore(name))
        p = Prog(nc)
        for gt in range(T // 128):
            p.track[("x1", gt)] = [None, []]
        with nc.Block() as block:
            dma(p, "qsp", s.ident_bf[:], g["ident_bf"], writes=("identbf",))
            dma(p, "qsp", s.ident_f32[:], g["ident_f32"], writes=("identf32",))
            with contextlib.ExitStack() as st1:
                alloc_phase2a(nc, st1, s)
                build_phase2a(nc, p, s, g)
                p.fence()
                p.materialize(block, sems)
            with contextlib.ExitStack() as st2:
                alloc_phase2b(nc, st2, s)
                build_phase2b(nc, p, s, g)
                p.fence()
                p.materialize(block, sems)
            build_scramble(nc, p, s, g)
            if g.get("qt_out") is not None:
                dma(p, "qsp", g["qt_out"], g["qt_dram"],
                    reads=tuple(("qtd", sl, m) for sl in range(T // 512) for m in range(8)),
                    writes=("qtout",))
                dma(p, "qsp", g["kt_out"], g["kt_dram"],
                    reads=tuple(("ktd", sl, m) for sl in range(T // 512) for m in range(8)),
                    writes=("ktout",))
            if g.get("outT_out") is not None:
                dma(p, "qsp", g["outT_out"], g["outT_dram"],
                    reads=tuple(("otd", q, mi) for q in range(T // 512)
                                for mi in range(2)),
                    writes=("outTout",))
            dma(p, "qsp", g["xn1_out"], g["xn1_dram"][0:T, :],
                reads=tuple(("xn1_dram", sl) for sl in range(T // 512)),
                writes=("xn1out",))
            p.barrier_all()
            p.materialize(block, sems)
    return nc


def prep_p2_inputs(x1, inp, j):
    d = {}
    d["x1_in"] = np.ascontiguousarray(x1)
    d["W_Q"] = np.ascontiguousarray(_np(inp["L1_attn_W_Q"]))
    d["W_K"] = np.ascontiguousarray(_np(inp["L1_attn_W_K"]))
    d["W_Vs"] = np.ascontiguousarray(_np(inp["L1_attn_W_V"])[:, VSL * j:VSL * (j + 1)])
    e0, em = _decay_tiles()
    d["e0_t"] = e0
    d["em_t"] = np.ascontiguousarray(em)
    d["ident_bf"] = np.eye(128, dtype=BF16NP)
    d["ident_f32"] = np.eye(128, dtype=np.float32)
    return d



DEPTH3 = 6
HEADS = 16
DH = 64
WIN = 256
NWIN = SLABT // WIN
NTI = SLABT // 128
DIN_P = 2816
QK_SCALE = 8.0
P3_PARANOID = False


def alloc_phase3(nc, stack, s):
    sb, _ = _mk(nc, stack, "s3_")
    s.wbig = sb("wbig", [128, 24576], BF16)
    s.wsm = sb("wsm", [128, 8192], BF16)
    s.wqkv = s.wbig[:].rearrange("p (k c) -> p k c", k=8)
    s.w2p = s.wbig[:, 0:22528].rearrange("p (k c) -> p k c", k=22)
    s.wo3 = s.wsm[:].rearrange("p (k c) -> p k c", k=8)
    s.w1ch = [s.wsm[:, 0:4096].rearrange("p (k c) -> p k c", k=8),
              s.wsm[:, 4096:8192].rearrange("p (k c) -> p k c", k=8)]
    s.xt3d = sb("xt3d", [128, 2, 1024], F32)
    s.xt3 = s.xt3d[:, 0, :]
    s.xt3b = sb("xt3b", [128, 1024], BF16)
    s.xd3d = sb("xd3d", [128, 2, 1024], F32)
    s.xd3 = s.xd3d[:, 0, :]
    s.lnxc3 = sb("lnxc3", [128, D], F32)
    s.stats3 = sb("stats3", [128, 8], F32)
    s.xnT3 = sb("xnT3", [128, 8, 128], BF16)
    s.qkv = sb("qkv", [128, 3072], F32)
    s.u8k = sb("u8k", [128, 4096], BF16)
    s.qkb = s.u8k[:, 0:2048]
    s.hs = sb("hs", [128, 40], F32)
    s.rot = sb("rot", [128, 4, 64], F32)
    s.qsk = sb("qsk", [128, 2, 64], F32)
    s.qT = s.u8k[:, 2048:4096].rearrange("p (k c) -> p k c", k=8)
    s.uAB = sb("uAB", [128, 12288], BF16)
    s.kT = [s.uAB[:, 0:2048].rearrange("p (k c) -> p k c", k=8),
            s.uAB[:, 2048:4096].rearrange("p (k c) -> p k c", k=8)]
    s.vb = [s.uAB[:, 4096:6144].rearrange("p (k c) -> p k c", k=2),
            s.uAB[:, 6144:8192].rearrange("p (k c) -> p k c", k=2)]
    s.mks = sb("mks", [128, 2, 2, 512], BF16)
    s.mkw = sb("mkw", [128, 2, 512], BF16)
    s.selv = sb("selv", [128, NWIN], F32)
    s.hkb = sb("hkb", [128, NTI], F32)
    s.attnbd = sb("attnbd", [128, 4, 512], BF16)
    s.attnTd = sb("attnTd", [128, 4, 4, 128], BF16)
    s.attno = sb("attno", [128, 1024], BF16)
    s.headsT = s.uAB[:, 8192:12288].rearrange("p (k c) -> p k c", k=8)
    s.wst = sb("wst", [128, 8, 512], F32)
    s.h2T = sb("h2T", [128, 22, 512], BF16)
    s.sga = sb("sga", [128, 512], F32)
    s.sgg = sb("sgg", [128, 512], F32)
    s.gbt = s.uAB[:, 0:4096].bitcast(F32).rearrange("p (k c) -> p k c", k=2)
    s.b2t = s.uAB[:, 4096:6144].bitcast(F32)
    s.b1t = sb("b1t", [128, 44], F32)
    s.rowst = s.uAB[0:1, 6144:12288].bitcast(F32).rearrange("p (k c) -> p k c", k=3)
    s.xnT512 = s.u8k[:].rearrange("p (k c) -> p k c", k=8)
    s.att1 = sb("att1", [128, 1024], F32)
    s.att2 = sb("att2", [128, 1024], F32)
    s.idx3 = sb("idx3", [128, 1], U32)
    s.x1t = s.att2
    s.abt = s.att1


def _shifted_tile(p, s, g, src, gen, it, dstbuf, dstkey):
    r0 = 1 + it * 128
    k0 = ("xc", gen, it)
    km = ("xc", gen, it - 1) if it > 0 else ("xcz", gen % 2)
    dma(p, "qsp", dstbuf[:, 0:512], src[r0:r0 + 128, 0:512],
        reads=(k0,), writes=(dstkey,))
    dma(p, "qsp", dstbuf[:, 512:1024], src[r0 - 1:r0 + 127, 512:1024],
        reads=(k0, km), writes=(dstkey,))


def _residual_writeback(p, s, g, dst, gen_out, t4, branch_stage, kstage,
                        src, gen_in):
    for q in range(4):
        it = t4 + q
        r0 = 1 + it * 128
        xd = s.xd3d[:, q % 2, :]
        kxd = ("xd3", q % 2)
        dma(p, "qsp", xd, src[r0:r0 + 128, :],
            reads=(("xc", gen_in, it),), writes=(kxd,))
        for half in range(2):
            items = [(s.pt_f[:, u * 128:(u + 1) * 128],
                      branch_stage[:, half * 4 + u, q * 128:(q + 1) * 128],
                      s.ident_f32[:]) for u in range(4)]
            transpose_group(p, items, reads=(kstage, "identf32"),
                            write_key="ptf")
            dve(p, "tensor_tensor", xd[:, half * 512:(half + 1) * 512],
                xd[:, half * 512:(half + 1) * 512], s.pt_f[:],
                op=OP.add, reads=("ptf", kxd), writes=(kxd,))
        dve(p, "tensor_scalar", xd, xd, s.hkb[:, it:it + 1],
            None, op0=OP.mult, reads=(kxd, "hkb"), writes=(kxd,))
        dma(p, "qsp", dst[r0:r0 + 128, :], xd, reads=(kxd,),
            writes=(("xc", gen_out, it),))


def build_p3_attn(nc, p, s, g, li, src, dst):
    gen = 2 * li
    dma(p, "qpool", s.wqkv,
        g["wqkv3"][li].rearrange("(k p) c -> p k c", p=128), writes=("wbig",))
    dma(p, "qpool", s.wo3,
        g["wo3_w"][li].rearrange("(k p) c -> p k c", p=128),
        writes=(("wsm", 0), ("wsm", 1)))
    dma(p, "qsp", s.qsk[:], g["qsk3"][li], writes=("qsk",))
    pool(p, "memset", s.kT[1][:], 0.0, writes=(("kTr", 1),))
    pool(p, "memset", s.vb[1][:], 0.0, writes=(("vbr", 1),))

    for w in range(NWIN):
        cur = w % 2
        prv = 1 - cur
        for par in range(2):
            dve(p, "scalar_tensor_tensor", s.mkw[:, par, :],
                s.mks[:, 1, par, :], s.selv[:, w:w + 1], s.mks[:, 0, par, :],
                op0=OP.mult, op1=OP.add,
                reads=("mks", "selv"), writes=("mkw",))
        for sub in range(2):
            it = 2 * w + sub
            xt = s.xt3d[:, it % 2, :]
            kxt = ("xt3", it % 2)
            _shifted_tile(p, s, g, src, gen, it, xt, kxt)
            dve(p, "tensor_copy", s.xt3b[:], xt, reads=(kxt,),
                writes=("xt3b",))
            items = [(s.pt_b[:, k * 128:(k + 1) * 128],
                      s.xt3b[:, k * 128:(k + 1) * 128], s.ident_bf[:])
                     for k in range(8)]
            transpose_group(p, items, reads=("xt3b", "identbf"),
                            write_key="ptb")
            dve(p, "tensor_copy", s.xnT3[:],
                s.pt_b[:].rearrange("p (k x) -> p k x", k=8),
                reads=("ptb",), writes=("xnT3",))
            for nch in range(6):
                bk = nch % 4
                mm_group(p, s.ps_mm[bk][:],
                         [(s.xnT3[:, k], s.wqkv[:, k, nch * 512:(nch + 1) * 512])
                          for k in range(8)],
                         reads=("xnT3", "wbig"), write_key=("psmm", bk))
                dve(p, "tensor_copy", s.qkv[:, nch * 512:(nch + 1) * 512],
                    s.ps_mm[bk][:], reads=(("psmm", bk),), writes=("qkv",))
            dma(p, "qsp", s.rot[:], g["rot3"][it], writes=("rot",))
            for which, off, sci, ssi in ((0, 0, 0, 1), (1, 1024, 2, 3)):
                qv = s.qkv[:, off:off + 1024]
                hv = qv.rearrange("p (h d) -> p h d", d=DH)
                dve(p, "scalar_tensor_tensor", s.att1[:], qv, 1.0, qv,
                    op0=OP.mult, op1=OP.mult, reads=("qkv",), writes=("att1",))
                dve(p, "tensor_reduce", s.hs[:, 24:40],
                    s.att1[:].rearrange("p (h d) -> p h d", d=DH),
                    axis=AX.X, op=OP.add, reads=("att1",), writes=("hsn",))
                act(p, s.hs[:, 24:40], s.hs[:, 24:40], AF.Sqrt,
                    reads=("hsn",), writes=("hsn",))
                dve(p, "tensor_scalar", s.hs[:, 24:40], s.hs[:, 24:40], 1e-12,
                    None, op0=OP.max, reads=("hsn",), writes=("hsn",))
                dve(p, "reciprocal", s.hs[:, 24:40], s.hs[:, 24:40],
                    reads=("hsn",), writes=("hsn",))
                dve(p, "tensor_tensor", hv, hv,
                    s.hs[:, 24:40].unsqueeze(2).broadcast_to([128, 16, DH]),
                    op=OP.mult, reads=("qkv", "hsn"), writes=("qkv",))
                dve(p, "tensor_tensor", hv, hv,
                    s.qsk[:, which, :].unsqueeze(1).broadcast_to([128, 16, DH]),
                    op=OP.mult, reads=("qkv", "qsk"), writes=("qkv",))
                cs = s.rot[:, sci, :]
                ss = s.rot[:, ssi, :]
                a1 = s.att1[:].rearrange("p (h d) -> p h d", d=DH)
                h1 = s.att2[:].rearrange("p (h d) -> p h d", d=DH)
                dve(p, "tensor_tensor", a1, hv,
                    cs.unsqueeze(1).broadcast_to([128, 16, DH]),
                    op=OP.mult, reads=("qkv", "rot"), writes=("att1",))
                dve(p, "tensor_tensor", h1[:, :, 0:32], hv[:, :, 32:64],
                    ss[:, 0:32].unsqueeze(1).broadcast_to([128, 16, 32]),
                    op=OP.mult, reads=("qkv", "rot"), writes=("att2",))
                dve(p, "tensor_tensor", h1[:, :, 32:64], hv[:, :, 0:32],
                    ss[:, 32:64].unsqueeze(1).broadcast_to([128, 16, 32]),
                    op=OP.mult, reads=("qkv", "rot"), writes=("att2",))
                dve(p, "tensor_tensor", hv[:, :, 0:32], a1[:, :, 0:32],
                    h1[:, :, 0:32], op=OP.subtract,
                    reads=("att1", "att2"), writes=("qkv",))
                dve(p, "tensor_tensor", hv[:, :, 32:64], a1[:, :, 32:64],
                    h1[:, :, 32:64], op=OP.add,
                    reads=("att1", "att2"), writes=("qkv",))
            if li == 0 and it == 4 and g.get("dbgq") is not None:
                dma(p, "qsp", g["dbgq"], s.qkv[:], reads=("qkv",),
                    writes=("dbgq",))
            dve(p, "tensor_copy", s.vb[cur][:, sub, :],
                s.qkv[:, 2048:3072], reads=("qkv",), writes=(("vbr", cur),))
            dve(p, "tensor_copy", s.qkb, s.qkv[:, 0:2048],
                reads=("qkv",), writes=("qkb",))
            for which, dstT, kdst in ((0, s.qT, "qT"), (1, s.kT[cur], ("kTr", cur))):
                items = [(s.pt_b[:, b * 128:(b + 1) * 128],
                          s.qkb[:, which * 1024 + b * 128:which * 1024 + (b + 1) * 128],
                          s.ident_bf[:]) for b in range(8)]
                transpose_group(p, items, reads=("qkb", "identbf"),
                                write_key="ptb")
                dve(p, "tensor_copy", dstT[:, :, sub * 128:(sub + 1) * 128],
                    s.pt_b[:].rearrange("p (b x) -> p b x", b=8),
                    reads=("ptb",), writes=(kdst,))
        for tt in range(2):
            for h in range(HEADS):
                blk = h // 2
                pb = 64 * (h % 2)
                bk = h % 4
                def emit_sc(eng, blk=blk, pb=pb, tt=tt, bk=bk, cur=cur, prv=prv):
                    eng.matmul(s.ps_mm[bk][:, 0:256],
                               s.qT[:, blk, tt * 128:(tt + 1) * 128][pb:pb + 64, :],
                               s.kT[prv][:, blk, :][pb:pb + 64, :],
                               start=True, stop=False)
                    eng.matmul(s.ps_mm[bk][:, 256:512],
                               s.qT[:, blk, tt * 128:(tt + 1) * 128][pb:pb + 64, :],
                               s.kT[cur][:, blk, :][pb:pb + 64, :],
                               start=False, stop=False)
                    return eng.matmul(s.ps_mm[bk][:],
                                      s.ident_bf[:], s.mkw[:, tt, :],
                                      start=False, stop=True)
                p.op("pe", emit_sc,
                     reads=("qT", ("kTr", prv), ("kTr", cur), "identbf", "mkw"),
                     writes=(("psmm", bk),))
                if li == 0 and w == 2 and h == 0 and tt == 0 and g.get("dbgs") is not None:
                    dve(p, "tensor_copy", s.wst[:, 0, :], s.ps_mm[bk][:],
                        reads=(("psmm", bk),), writes=("wst",))
                    dma(p, "qsp", g["dbgs"], s.wst[:, 0, :], reads=("wst",),
                        writes=("dbgs",))
                pp = h % 4
                hsc = 4 * pp
                attnb = s.attnbd[:, pp, :]
                attnT = s.attnTd[:, pp]
                dve(p, "tensor_reduce", s.hs[:, hsc:hsc + 1], s.ps_mm[bk][:],
                    axis=AX.X, op=OP.max, negate=True,
                    reads=(("psmm", bk),), writes=(("hs", pp),))
                act(p, attnb, s.ps_mm[bk][:], AF.Exp,
                    bias=s.hs[:, hsc:hsc + 1],
                    accum_out=s.hs[:, hsc + 2:hsc + 3],
                    reads=(("psmm", bk), ("hs", pp)), writes=(("attnb", pp), ("hs2", pp)))
                dve(p, "reciprocal", s.hs[:, hsc + 3:hsc + 4], s.hs[:, hsc + 2:hsc + 3],
                    reads=(("hs2", pp),), writes=(("hs2", pp),))
                dve(p, "tensor_scalar", attnb, attnb, s.hs[:, hsc + 3:hsc + 4],
                    None, op0=OP.mult, reads=(("attnb", pp), ("hs2", pp)),
                    writes=(("attnb", pp),))
                items = [(s.pt_b[:, u * 128:(u + 1) * 128],
                          attnb[:, u * 128:(u + 1) * 128], s.ident_bf[:])
                         for u in range(4)]
                transpose_group(p, items, reads=(("attnb", pp), "identbf"),
                                write_key="ptb")
                dve(p, "tensor_copy", attnT,
                    s.pt_b[:, 0:512].rearrange("p (a x) -> p a x", a=4),
                    reads=("ptb",), writes=(("attnT", pp),))
                avb = s.ps_r if h < 8 else s.ps_x
                avk = "psr" if h < 8 else "psx"
                def emit_av(eng, h=h, avb=avb, cur=cur, prv=prv, attnT=attnT):
                    inst = None
                    c0 = 64 * (h % 8)
                    for a in range(4):
                        vtile = s.vb[prv if a < 2 else cur][:, a % 2,
                                                            h * 64:(h + 1) * 64]
                        inst = eng.matmul(
                            avb[:, c0:c0 + 64],
                            attnT[:, a, :], vtile,
                            start=(h % 8 == 0 and a == 0),
                            stop=(h % 8 == 7 and a == 3))
                    return inst
                p.op("pe", emit_av,
                     reads=(("attnT", pp), ("vbr", prv), ("vbr", cur)),
                     writes=(avk,))
            for bb in range(2):
                dve(p, "tensor_copy", s.attno[:, bb * 512:(bb + 1) * 512],
                    (s.ps_r if bb == 0 else s.ps_x)[:],
                    reads=("psr" if bb == 0 else "psx",),
                    writes=("attno",))
            items = [(s.pt_b[:, u * 128:(u + 1) * 128],
                      s.attno[:, u * 128:(u + 1) * 128], s.ident_bf[:])
                     for u in range(8)]
            transpose_group(p, items, reads=("attno", "identbf"),
                            write_key="ptb")
            dve(p, "tensor_copy",
                s.headsT[:, :, cur * 256 + tt * 128:cur * 256 + tt * 128 + 128],
                s.pt_b[:].rearrange("p (k x) -> p k x", k=8),
                reads=("ptb",), writes=("headsT",))
        if w % 2 == 1:
            for m in range(8):
                bk = m % 4
                mm_group(p, s.ps_mm[bk][:],
                         [(s.wo3[:, k, m * 128:(m + 1) * 128], s.headsT[:, k, :])
                          for k in range(8)],
                         reads=("headsT", ("wsm", 0), ("wsm", 1)), write_key=("psmm", bk))
                dve(p, "tensor_copy", s.wst[:, m, :], s.ps_mm[bk][:],
                    reads=(("psmm", bk),), writes=("wst",))
            _residual_writeback(p, s, g, dst, gen + 1, 2 * (w - 1),
                                s.wst[:], "wst", src, gen)


def bcast_row(p, s, dst_ap, dst_key, row_ap, row_key, width=512):
    def emit1(eng):
        return eng.matmul(s.ps_r[:, 0:width], s.onescol32[:],
                          row_ap, start=True, stop=True)
    p.op("pe", emit1, reads=(row_key, "onescol32"), writes=("psr",))
    dve(p, "tensor_copy", dst_ap, s.ps_r[:, 0:width],
        reads=("psr",), writes=(dst_key,))


def build_p3_ffn(nc, p, s, g, li, src, dst):
    gen = 2 * li + 1
    dma(p, "qpool", s.w2p,
        g["w2p3"][li].rearrange("(k p) c -> p k c", p=128), writes=("wbig",))
    dma(p, "qsp", s.rowst[:], g["ffnrows"][li], writes=("rowst",))
    dma(p, "qsp", s.b1t[:], g["b1_3"][li], writes=("b1t",))
    for ci in range(2):
        for ri, (dstap, dk) in enumerate(((s.gbt[:, 0, :], "gbt"),
                                          (s.gbt[:, 1, :], "gbt"),
                                          (s.b2t[:, :], "b2t"))):
            bcast_row(p, s, dstap[:, ci * 512:(ci + 1) * 512], dk,
                      s.rowst[0:1, ri, ci * 512:(ci + 1) * 512], "rowst")

    w1_r = g["w1p3"][li].rearrange("(k p) c -> p k c", p=128)
    for t4 in range(0, NTI, 4):
        for q in range(4):
            it = t4 + q
            xt = s.xt3d[:, q % 2, :]
            kxt = ("xt3", q % 2)
            _shifted_tile(p, s, g, src, gen, it, xt, kxt)
            emit_ln(p, s.lnxc3, "lnxc3", s.att1, "att1", s.stats3, "lnst3",
                    xt, kxt, [(xt, kxt)])
            dve(p, "tensor_tensor", xt, xt, s.gbt[:, 0, :],
                op=OP.mult, reads=(kxt, "gbt"), writes=(kxt,))
            dve(p, "tensor_tensor", xt, xt, s.gbt[:, 1, :],
                op=OP.add, reads=(kxt, "gbt"), writes=(kxt,))
            dve(p, "tensor_copy", s.xt3b[:], xt, reads=(kxt,),
                writes=("xt3b",))
            items = [(s.pt_b[:, k * 128:(k + 1) * 128],
                      s.xt3b[:, k * 128:(k + 1) * 128], s.ident_bf[:])
                     for k in range(8)]
            transpose_group(p, items, reads=("xt3b", "identbf"),
                            write_key="ptb")
            dve(p, "tensor_copy", s.xnT512[:, :, q * 128:(q + 1) * 128],
                s.pt_b[:].rearrange("p (k x) -> p k x", k=8),
                reads=("ptb",), writes=("qkb", "qT"))
        for ci in range(11):
            wb = s.w1ch[ci % 2]
            kwb = ("wsm", ci % 2)
            dma(p, "qpool", wb, w1_r[:, :, ci * 512:(ci + 1) * 512],
                writes=(kwb,))
            for u2 in range(2):
                u = 2 * ci + u2
                ba, bg = 2 * u2, 2 * u2 + 1
                mm_group(p, s.ps_mm[ba][:],
                         [(wb[:, k, u2 * 256:u2 * 256 + 128], s.xnT512[:, k, :])
                          for k in range(8)],
                         reads=(kwb, "qkb", "qT"), write_key=("psmm", ba))
                mm_group(p, s.ps_mm[bg][:],
                         [(wb[:, k, u2 * 256 + 128:u2 * 256 + 256], s.xnT512[:, k, :])
                          for k in range(8)],
                         reads=(kwb, "qkb", "qT"), write_key=("psmm", bg))
                dve(p, "tensor_scalar", s.sga[:], s.ps_mm[ba][:],
                    s.b1t[:, u:u + 1], None, op0=OP.add,
                    reads=(("psmm", ba), "b1t"), writes=("sga",))
                dve(p, "tensor_scalar", s.sgg[:], s.ps_mm[bg][:],
                    s.b1t[:, 22 + u:23 + u], None, op0=OP.add,
                    reads=(("psmm", bg), "b1t"), writes=("sgg",))
                act(p, s.sgg[:], s.sgg[:], AF.Gelu, reads=("sgg",),
                    writes=("sgg",))
                dve(p, "tensor_tensor", s.h2T[:, u, :], s.sga[:], s.sgg[:],
                    op=OP.mult, reads=("sga", "sgg"), writes=(("h2T", u),))
        for q in range(4):
            it = t4 + q
            for n in range(2):
                bk = n
                mm_group(p, s.ps_mm[bk][:],
                         [(s.h2T[:, k, q * 128:(q + 1) * 128],
                           s.w2p[:, k, n * 512:(n + 1) * 512])
                          for k in range(22)],
                         reads=tuple(("h2T", k) for k in range(22)) + ("wbig",),
                         write_key=("psmm", bk))
            r0 = 1 + it * 128
            xd = s.xd3d[:, q % 2, :]
            kxd = ("xd3", q % 2)
            dma(p, "qsp", xd, src[r0:r0 + 128, :],
                reads=(("xc", gen, it),), writes=(kxd,))
            for n in range(2):
                dve(p, "tensor_tensor", xd[:, n * 512:(n + 1) * 512],
                    xd[:, n * 512:(n + 1) * 512], s.ps_mm[n][:],
                    op=OP.add, reads=(("psmm", n), kxd), writes=(kxd,))
            dve(p, "tensor_tensor", xd, xd, s.b2t[:],
                op=OP.add, reads=(kxd, "b2t"), writes=(kxd,))
            dve(p, "tensor_scalar", xd, xd, s.hkb[:, it:it + 1],
                None, op0=OP.mult, reads=(kxd, "hkb"), writes=(kxd,))
            dma(p, "qsp", dst[r0:r0 + 128, :], xd, reads=(kxd,),
                writes=(("xc", gen + 1, it),))


def build_phase3(nc, p, s, g):
    xcb = [g["xc_dram"], g["xd_dram"]]
    dma(p, "qsp", s.mks[:], g["mask3"], writes=("mks",))
    dma(p, "qsp", s.selv[:], g["selv"], writes=("selv",))
    dma(p, "qsp", s.hkb[:], g["hkb"], writes=("hkb",))
    pool(p, "memset", s.xt3[:], 0.0, writes=(("xt3", 0),))
    for buf in range(2):
        dma(p, "qpool", xcb[buf][0:1, :], s.xt3[0:1, 0:D], reads=(("xt3", 0),),
            writes=(("xcz", buf),))
    for it in range(NTI):
        dma(p, "qsp", s.idx3[:], g["idx_slab"][it].unsqueeze(1), writes=("idx3",))
        gather(p, s.xt3[:], g["xn1_dram"], s.idx3[:, 0:1],
               reads=("idx3", ("xn1_dram", "z"))
               + tuple(("xn1_dram", sl) for sl in range(T // 512)),
               writes=(("xt3", 0),))
        dve(p, "tensor_scalar", s.xt3[:], s.xt3[:], s.hkb[:, it:it + 1],
            None, op0=OP.mult, reads=(("xt3", 0), "hkb"), writes=(("xt3", 0),))
        dma(p, "qsp", xcb[0][1 + it * 128:1 + (it + 1) * 128, :], s.xt3[:],
            reads=(("xt3", 0),), writes=(("xc", 0, it),))
    if g.get("dbg3") is not None:
        dma(p, "qpool", g["dbg3"][0], xcb[0][:, :],
            reads=tuple(("xc", 0, it) for it in range(NTI)), writes=("d30",))
    for li in range(DEPTH3):
        build_p3_attn(nc, p, s, g, li, xcb[0], xcb[1])
        p.fence()
        if li == 0 and g.get("dbg3") is not None:
            dma(p, "qpool", g["dbg3"][1], xcb[1][:, :],
                reads=tuple(("xc", 1, it) for it in range(NTI)), writes=("d31",))
        build_p3_ffn(nc, p, s, g, li, xcb[1], xcb[0])
        p.fence()
        if li == 0 and g.get("dbg3") is not None:
            dma(p, "qpool", g["dbg3"][2], xcb[0][:, :],
                reads=tuple(("xc", 2, it) for it in range(NTI)), writes=("d32",))


def build_final(nc, p, s, g):
    for it in range(CHUNK // 128):
        dma(p, "qsp", s.idx3[:], g["idx_own"][it].unsqueeze(1), writes=("idx3",))
        gather(p, s.x1t[:], g["x1_dram"], s.idx3[:, 0:1],
               reads=("idx3",) + tuple(("x1", gt) for gt in range(T // 128)),
               writes=("att2",))
        dma(p, "qsp", s.abt[:],
            g["attnbr_dram"][it * 128:(it + 1) * 128, :],
            reads=tuple(("attnbr", m) for m in range(16)), writes=("att1",))
        r0 = 1 + (HALO // 128 + it) * 128
        dma(p, "qsp", s.xd3[:], g["xc_dram"][r0:r0 + 128, :],
            reads=(("xc", 2 * DEPTH3, HALO // 128 + it),), writes=(("xd3", 0),))
        dve(p, "tensor_tensor", s.x1t[:], s.x1t[:], s.abt[:], op=OP.add,
            reads=("att2", "att1"), writes=("att2",))
        dve(p, "tensor_tensor", s.x1t[:], s.x1t[:], s.xd3[:], op=OP.add,
            reads=("att2", ("xd3", 0)), writes=("att2",))
        dma(p, "qsp", g["out_dram"][it * 128:(it + 1) * 128, :], s.x1t[:],
            reads=("att2",), writes=(("out", it),))


def declare_p3_dram(nc, g, xn1_external):
    if xn1_external:
        g["xn1_dram"] = nc.dram_tensor("xn1_in", [T + 128, D], F32,
                                       kind="ExternalInput").ap()
    g["wqkv3"] = nc.dram_tensor("wqkv3", [DEPTH3, D, 3072], F32, kind="ExternalInput").ap()
    g["wo3_w"] = nc.dram_tensor("wo3_w", [DEPTH3, D, D], F32, kind="ExternalInput").ap()
    g["w1p3"] = nc.dram_tensor("w1p3", [DEPTH3, D, 2 * DIN_P], F32, kind="ExternalInput").ap()
    g["w2p3"] = nc.dram_tensor("w2p3", [DEPTH3, DIN_P, D], F32, kind="ExternalInput").ap()
    g["qsk3"] = nc.dram_tensor("qsk3", [DEPTH3, 128, 2, DH], F32, kind="ExternalInput").ap()
    g["rot3"] = nc.dram_tensor("rot3", [NTI, 128, 4, DH], F32, kind="ExternalInput").ap()
    g["mask3"] = nc.dram_tensor("mask3", [128, 2, 2, 512], BF16, kind="ExternalInput").ap()
    g["selv"] = nc.dram_tensor("selv", [128, NWIN], F32, kind="ExternalInput").ap()
    g["hkb"] = nc.dram_tensor("hkb", [128, NTI], F32, kind="ExternalInput").ap()
    g["idx_slab"] = nc.dram_tensor("idx_slab", [NTI, 128], U32, kind="ExternalInput").ap()
    g["idx_own"] = nc.dram_tensor("idx_own", [CHUNK // 128, 128], U32, kind="ExternalInput").ap()
    g["ffnrows"] = nc.dram_tensor("ffnrows", [DEPTH3, 1, 3, D], F32, kind="ExternalInput").ap()
    g["b1_3"] = nc.dram_tensor("b1_3", [DEPTH3, 128, 44], F32, kind="ExternalInput").ap()
    g["xc_dram"] = nc.dram_tensor("xc_dram", [1 + SLABT, D], F32).ap()
    g["xd_dram"] = nc.dram_tensor("xd_dram", [1 + SLABT, D], F32).ap()


def build_p3_program(debug=False):
    import contextlib
    nc = bass.Bass("TRN2", target_bir_lowering=False)
    g = {}
    g["ident_bf"] = nc.dram_tensor("ident_bf", [128, 128], BF16, kind="ExternalInput").ap()
    g["ident_f32"] = nc.dram_tensor("ident_f32", [128, 128], F32, kind="ExternalInput").ap()
    g["onescol"] = nc.dram_tensor("onescol", [1, 128], F32, kind="ExternalInput").ap()
    declare_p3_dram(nc, g, xn1_external=True)
    g["x1_dram"] = nc.dram_tensor("x1_in", [T, D], F32, kind="ExternalInput").ap()
    g["attnbr_dram"] = nc.dram_tensor("attnbr_in", [CHUNK, D], F32, kind="ExternalInput").ap()
    g["out_dram"] = nc.dram_tensor("out", [CHUNK, D], F32, kind="ExternalOutput").ap()
    if debug:
        g["dbg3"] = nc.dram_tensor("dbg3", [3, 1 + SLABT, D], F32, kind="ExternalOutput").ap()
        g["dbgq"] = nc.dram_tensor("dbgq", [128, 3072], F32, kind="ExternalOutput").ap()
        g["dbgh"] = nc.dram_tensor("dbgh", [128, 8, 256], BF16, kind="ExternalOutput").ap()
        g["dbgs"] = nc.dram_tensor("dbgs", [128, 512], F32, kind="ExternalOutput").ap()
        g["dbga"] = nc.dram_tensor("dbga", [128, 512], BF16, kind="ExternalOutput").ap()
        g["dbgo"] = nc.dram_tensor("dbgo", [128, 256], F32, kind="ExternalOutput").ap()
        g["dbgv"] = nc.dram_tensor("dbgv", [2, 128, 2, 1024], BF16, kind="ExternalOutput").ap()

    with contextlib.ExitStack() as stack:
        s = S()
        alloc_persist(nc, stack, s)
        sems = {}
        for name in ENGINES + QUEUES:
            sems[name] = stack.enter_context(nc.semaphore(name))
        p = Prog(nc)
        with nc.Block() as block:
            dma(p, "qsp", s.ident_bf[:], g["ident_bf"], writes=("identbf",))
            dma(p, "qsp", s.ident_f32[:], g["ident_f32"], writes=("identf32",))
            dma(p, "qsp", s.onescol32[:], g["onescol"], writes=("onescol32",))
            with contextlib.ExitStack() as st1:
                alloc_phase3(nc, st1, s)
                build_phase3(nc, p, s, g)
                build_final(nc, p, s, g)
                p.fence()
                p.materialize(block, sems)
            p.barrier_all()
            p.materialize(block, sems)
    return nc


def _rotary_tables_np():
    n = T
    half = np.arange(0, DH, 2, dtype=np.float64)
    inv_freq = 1.0 / 10000.0 ** (half / DH)
    t = np.arange(n, dtype=np.float64)
    freqs = t[:, None] * inv_freq[None, :]
    freqs = np.concatenate([freqs, freqs], -1)
    base_scale = (half + 0.4 * DH) / (1.4 * DH)
    power = (t - n // 2) / (WIN // 2)
    scale = base_scale[None, :] ** power[:, None]
    scale = np.concatenate([scale, scale], -1)
    cos, sin = np.cos(freqs), np.sin(freqs)
    out = np.stack([cos * scale, sin * scale, cos / scale, sin / scale], 1)
    return out.astype(np.float32)


def _mask_tiles_np():
    out = np.zeros((128, 2, 2, 512), np.float32)
    pq = np.arange(128)
    jj = np.arange(512)
    for par in range(2):
        tq = 128 * par + pq
        allowed = (jj[None, :] >= tq[:, None]) & (jj[None, :] <= tq[:, None] + 256)
        out[:, 0, par, :] = np.where(allowed, 0.0, NEG)
        out[:, 1, par, :] = np.where(jj[None, :] < 256, NEG, 0.0)
    return out.astype(BF16NP)


def prep_p3_inputs(inp, j):
    d = {}
    d["ident_bf"] = np.eye(128, dtype=BF16NP)
    d["ident_f32"] = np.eye(128, dtype=np.float32)
    d["onescol"] = np.ones((1, 128), np.float32)
    d["wqkv3"] = np.ascontiguousarray(_np(inp["L1_ffn_Wqkv"]))
    d["wo3_w"] = np.ascontiguousarray(_np(inp["L1_ffn_Wo"]))
    W1 = _np(inp["L1_ffn_W1"])
    b1 = _np(inp["L1_ffn_b1"])
    W2 = _np(inp["L1_ffn_W2"])
    W1q = np.zeros((DEPTH3, D, 2 * DIN_P), np.float32)
    b1q = np.zeros((DEPTH3, 128, 44), np.float32)
    for u in range(22):
        c0, c1 = u * 128, min(2730, (u + 1) * 128)
        w = c1 - c0
        W1q[:, :, 256 * u:256 * u + w] = W1[:, :, c0:c1]
        W1q[:, :, 256 * u + 128:256 * u + 128 + w] = W1[:, :, 2730 + c0:2730 + c1]
        b1q[:, 0:w, u] = b1[:, c0:c1].T.reshape(w, DEPTH3).T.T if False else 0
        b1q[:, :, u][:, 0:w] = b1[:, c0:c1]
        b1q[:, :, 22 + u][:, 0:w] = b1[:, 2730 + c0:2730 + c1]
    d["w1p3"] = W1q
    d["b1_3"] = np.ascontiguousarray(np.transpose(b1q, (0, 2, 1))) if False else b1q
    W2q = np.zeros((DEPTH3, DIN_P, D), np.float32)
    W2q[:, 0:2730, :] = 0
    for u in range(22):
        c0, c1 = u * 128, min(2730, (u + 1) * 128)
        W2q[:, u * 128:u * 128 + (c1 - c0), :] = W2[:, c0:c1, :]
    d["w2p3"] = W2q
    qs = _np(inp["L1_ffn_q_scale"]).astype(np.float32)
    ks = _np(inp["L1_ffn_k_scale"]).astype(np.float32)
    qsk = np.zeros((DEPTH3, 128, 2, DH), np.float32)
    qsk[:, :, 0, :] = qs[:, None, :] * QK_SCALE
    qsk[:, :, 1, :] = ks[:, None, :]
    d["qsk3"] = qsk
    rot = _rotary_tables_np()
    start = 4096 * j - HALO
    pos = np.clip(np.arange(start, start + SLABT), 0, T - 1)
    d["rot3"] = np.ascontiguousarray(
        rot[pos].reshape(NTI, 128, 4, DH))
    d["mask3"] = _mask_tiles_np()
    selv = np.zeros((128, NWIN), np.float32)
    gw0 = (0 - start) // WIN
    if 0 <= gw0 < NWIN and start <= 0:
        selv[:, gw0] = 1.0
    d["selv"] = selv
    hk = np.ones((128, NTI), np.float32)
    if j == 0:
        for it in range(NTI):
            tpos = start + it * 128 + np.arange(128)
            hk[:, it] = (tpos >= 0).astype(np.float32)
    d["hkb"] = hk
    idx = np.empty((NTI, 128), np.uint32)
    for it in range(NTI):
        tpos = start + it * 128 + np.arange(128)
        idx[it] = np.where((tpos >= 0) & (tpos < T), tpos, T).astype(np.uint32)
    d["idx_slab"] = idx
    d["idx_own"] = (4096 * j + np.arange(CHUNK).reshape(CHUNK // 128, 128)).astype(np.uint32)
    gb = np.zeros((DEPTH3, 1, 3, D), np.float32)
    gb[:, 0, 0, :] = _np(inp["L1_ffn_ln_g"])
    gb[:, 0, 1, :] = _np(inp["L1_ffn_ln_b"])
    gb[:, 0, 2, :] = _np(inp["L1_ffn_b2"])
    d["ffnrows"] = gb
    return d



def build_full_program():
    import contextlib
    nc = bass.Bass("TRN2", target_bir_lowering=False)
    g = {}
    declare_p1_dram(nc, g, external_x1=False)
    declare_p2_dram(nc, g, x1_external=False)
    declare_p3_dram(nc, g, xn1_external=False)
    g["attnbr_dram"] = nc.dram_tensor("attnbr_dram", [CHUNK, D], F32).ap()
    g["out_dram"] = nc.dram_tensor("out", [CHUNK, D], F32, kind="ExternalOutput").ap()

    with contextlib.ExitStack() as stack:
        s = S()
        alloc_persist(nc, stack, s)
        sems = {}
        for name in ENGINES + QUEUES:
            sems[name] = stack.enter_context(nc.semaphore(name))
        p = Prog(nc)
        with nc.Block() as block:
            dma(p, "qsp", s.ident_bf[:], g["ident_bf"], writes=("identbf",))
            dma(p, "qsp", s.ident_f32[:], g["ident_f32"], writes=("identf32",))
            dma(p, "qsp", s.onescol32[:], g["onescol"], writes=("onescol32",))
            with contextlib.ExitStack() as st1:
                alloc_phase1(nc, st1, s)
                build_phase1(nc, p, s, g)
                p.fence()
                p.materialize(block, sems)
            with contextlib.ExitStack() as st1b:
                alloc_moe(nc, st1b, s)
                build_moe(nc, p, s, g)
                p.fence()
                p.materialize(block, sems)
            with contextlib.ExitStack() as stv:
                alloc_vbuf(nc, stv, s)
                with contextlib.ExitStack() as st2a:
                    alloc_phase2a(nc, st2a, s)
                    build_phase2a(nc, p, s, g)
                    p.fence()
                    p.materialize(block, sems)
                with contextlib.ExitStack() as st2b:
                    alloc_phase2b(nc, st2b, s)
                    build_phase2b(nc, p, s, g)
                    p.fence()
                    p.materialize(block, sems)
            build_scramble(nc, p, s, g)
            with contextlib.ExitStack() as st3:
                alloc_phase3(nc, st3, s)
                build_phase3(nc, p, s, g)
                build_final(nc, p, s, g)
                p.fence()
                p.materialize(block, sems)
            p.barrier_all()
            p.materialize(block, sems)
    return nc


def prep_full_inputs(x_batch, inputs, j):
    d = prep_p1_inputs(x_batch, inputs)
    d["W_Q"] = np.ascontiguousarray(_np(inputs["L1_attn_W_Q"]))
    d["W_K"] = np.ascontiguousarray(_np(inputs["L1_attn_W_K"]))
    d["W_Vs"] = np.ascontiguousarray(_np(inputs["L1_attn_W_V"])[:, VSL * j:VSL * (j + 1)])
    e0, em = _decay_tiles()
    d["e0_t"] = e0
    d["em_t"] = np.ascontiguousarray(em)
    d3 = prep_p3_inputs(inputs, j)
    for k in ("ident_bf", "ident_f32", "onescol"):
        d3.pop(k, None)
    d.update(d3)
    d["onescol"] = np.ones((1, 128), np.float32)
    return d


FUSED = True


def kernel_fused(**inputs):
    x = np.ascontiguousarray(np.asarray(inputs["L0_attn_x"], dtype=np.float32))
    if "full" not in _PROG_CACHE:
        _PROG_CACHE["full"] = build_full_program()
    in_maps = [prep_full_inputs(x[c // 4], inputs, c % 4) for c in range(8)]
    res = run_bass_kernel_spmd(_PROG_CACHE["full"], in_maps,
                               core_ids=list(range(8)))
    out = np.empty((B, T, D), np.float32)
    for c in range(8):
        b, j = c // 4, c % 4
        out[b, CHUNK * j:CHUNK * (j + 1)] = res.results[c]["out"]
    return out
